# version 04
# baseline (speedup 1.0000x reference)
import traceback
import numpy as np
import concourse.bass as bass
import concourse.mybir as mybir
from concourse.bass_utils import run_bass_kernel_spmd

F32 = mybir.dt.float32
BF16 = mybir.dt.bfloat16
U8 = mybir.dt.uint8
I32 = mybir.dt.int32
AF = mybir.ActivationFunctionType
ALU = mybir.AluOpType
AX = mybir.AxisListType.X

D = 2048
NCORES = 8
TPC = 1024
NT = TPC // 128
DCH = D // 128
DFF = 1408
FCH = DFF // 128
NEXP = 6
G = 4096
GCH = G // 128
GD = 256
NHEAD = 16
LN_EPS = 1e-5
RMS_EPS = float(np.finfo(np.float32).eps)
F32_TINY = float(np.finfo(np.float32).tiny)
LN2 = float(np.log(2.0))

ENGINES = ("sync", "scalar", "vector", "tensor", "gpsimd")


class Prog:
    def __init__(self, nc):
        self.nc = nc
        self.q = {e: [] for e in ENGINES}
        self.sems = {}
        self.cnt = {}
        self.last_wait = {}

    def sem(self, ctx, name):
        h = ctx.enter_context(self.nc.semaphore(name))
        self.sems[name] = h
        self.cnt[name] = 0

    def emit(self, eng, fn, deps=(), inc=None, amt=1):
        deps = [d for d in deps if d is not None]
        real = []
        for (s, v) in deps:
            if v <= 0:
                continue
            key = (eng, s)
            if self.last_wait.get(key, 0) >= v:
                continue
            self.last_wait[key] = v
            real.append((self.sems[s], v))
        tok = None
        semh = None
        if inc is not None:
            self.cnt[inc] += amt
            tok = (inc, self.cnt[inc])
            semh = self.sems[inc]
        a = amt

        site = traceback.extract_stack()[-3:-1]

        def run(e, real=real, fn=fn, semh=semh, a=a, site=site):
            for (h, v) in real:
                e.wait_ge(h, v)
            try:
                inst = fn(e)
            except Exception:
                print("EMIT SITE:", "".join(traceback.format_list(site)))
                traceback.print_exc()
                raise
            if semh is not None:
                inst.then_inc(semh, a)
        self.q[eng].append(run)
        return tok

    def dma(self, eng, out, in_, deps=(), sem=None, **kw):
        if sem is None:
            sem = {"sync": "sd", "gpsimd": "sg", "scalar": "so"}[eng]

        def f(e, out=out, in_=in_, kw=kw):
            src = in_() if callable(in_) else in_
            try:
                return e.dma_start(out=out, in_=src, **kw)
            except Exception:
                print("DMA FAIL out=", out, "src=", src, "kw=", kw)
                raise
        return self.emit(eng, f, deps=deps, inc=sem, amt=16)

    def dmat(self, eng, out, in_, deps=(), sem=None):
        if sem is None:
            sem = {"sync": "sd", "scalar": "so"}[eng]
        return self.emit(eng, lambda e: e.dma_start_transpose(out, in_),
                         deps=deps, inc=sem, amt=16)

    def fence(self, eng, dma_sem):
        fsem = {"sync": "sfd", "gpsimd": "sfg", "scalar": "sfs"}[eng]
        ceil = self.cnt[dma_sem]
        self.cnt[fsem] += 1
        tok = (fsem, self.cnt[fsem])
        fh = self.sems[fsem]
        real = []
        key = (eng, dma_sem)
        if self.last_wait.get(key, 0) < ceil:
            self.last_wait[key] = ceil
            real.append((self.sems[dma_sem], ceil))

        def run(e, real=real, fh=fh):
            for (h, v) in real:
                e.wait_ge(h, v)
            e.sem_inc(fh, 1)
        self.q[eng].append(run)
        return tok

    def all_toks(self):
        return [(s, c) for s, c in self.cnt.items()]


def build_program(upto=6, taps=None):
    taps = taps or []
    nc = bass.Bass("TRN2", target_bir_lowering=False)

    I = {}

    def di(name, shape):
        I[name] = nc.dram_tensor(name, list(shape), F32, kind="ExternalInput").ap()

    di("x", (TPC, D))
    di("coef", (8, 1))
    di("L0_attn_Wq", (D, D)); di("L0_attn_Wv", (D, D))
    di("L0_attn_Wo", (D, D)); di("L0_attn_Wg", (D, D))
    di("L0_attn_ln_gamma", (1, D)); di("L0_attn_ln_beta", (1, D))
    di("L0_attn_beta", (1, NHEAD))
    di("L0_ffn_gate_W", (NEXP, D))
    di("L0_ffn_W1", (NEXP, D, DFF)); di("L0_ffn_b1", (NEXP, DFF))
    di("L0_ffn_W2", (NEXP, DFF, D)); di("L0_ffn_b2", (NEXP, D))
    di("L1_attn_Wqk", (D, D)); di("L1_attn_bqk", (1, D))
    di("L1_attn_Wv", (D, D)); di("L1_attn_bv", (1, D))
    di("L1_attn_Wu", (D, D)); di("L1_attn_bu", (1, D))
    di("L1_attn_Wo", (D, D)); di("L1_attn_bo", (1, D))
    di("L1_attn_ln_g", (1, D)); di("L1_attn_ln_b", (1, D))
    di("L1_ffn_Wc", (G, D)); di("L1_ffn_temperature", (1, 1))
    di("L1_ffn_genes", (G, GD)); di("L1_ffn_Wd", (D, GD)); di("L1_ffn_Wu", (D, GD))
    di("L1_ffn_norm_w", (1, D)); di("L1_ffn_scale", (1, 1))

    y = nc.dram_tensor("y", [TPC, D], F32, kind="ExternalOutput").ap()

    wq_bf = nc.dram_tensor("wq_bf", [D, D], BF16).ap()
    wv_bf = nc.dram_tensor("wv_bf", [D, D], BF16).ap()
    wg_bf = nc.dram_tensor("wg_bf", [D, D], BF16).ap()
    wo0_bf = nc.dram_tensor("wo0_bf", [D, D], BF16).ap()
    wc_bf = nc.dram_tensor("wc_bf", [G, D], BF16).ap()
    wd_bf = nc.dram_tensor("wd_bf", [D, GD], BF16).ap()
    wu_bf = nc.dram_tensor("wu_bf", [D, GD], BF16).ap()
    wqk1_bf = nc.dram_tensor("wqk1_bf", [D, D], BF16)
    wv1_bf = nc.dram_tensor("wv1_bf", [D, D], BF16)
    wu1_bf = nc.dram_tensor("wu1_bf", [D, D], BF16)
    wo1_bf = nc.dram_tensor("wo1_bf", [D, D], BF16)
    w1sel = [nc.dram_tensor("w1sel%d" % c, [2, D, DFF], BF16).ap()
             for c in range(2)]
    w2sel = [nc.dram_tensor("w2sel%d" % c, [2, DFF, D], BF16).ap()
             for c in range(2)]
    lnscr = nc.dram_tensor("lnscr", [TPC, D], BF16).ap()
    scr2 = nc.dram_tensor("scr2", [TPC, D], BF16).ap()
    lgscr = nc.dram_tensor("lgscr", [TPC, G], F32).ap()
    rowscr = nc.dram_tensor("rowscr", [8, 4096], F32).ap()
    cc_g_in = nc.dram_tensor("cc_g_in", [1, D], F32).ap()
    cc_g_out = nc.dram_tensor("cc_g_out", [1, D], F32).ap()
    cc_s_in = nc.dram_tensor("cc_s_in", [128, NHEAD], F32).ap()
    cc_s_out = nc.dram_tensor("cc_s_out", [128 * NCORES, NHEAD], F32).ap()
    cc_kv_in = nc.dram_tensor("cc_kv_in", [128, D], F32).ap()
    cc_kv_out = nc.dram_tensor("cc_kv_out", [128, D], F32).ap()
    tap_dram = [nc.dram_tensor("tap_" + tn, list(tsh), F32,
                               kind="ExternalOutput").ap()
                for (tn, tsh, _f) in taps]

    from contextlib import ExitStack
    ctx = ExitStack()

    def sbt(name, shape, dt):
        return ctx.enter_context(nc.sbuf_tensor(name, list(shape), dt))

    X = sbt("X", (128, NT * D), F32)
    LT = sbt("LT", (128, DCH * TPC), BF16)
    QT = sbt("QT", (128, DCH * TPC), BF16)
    AR = sbt("AR", (128, 12288), BF16)
    BR = sbt("BR", (128, 10752), F32)
    STT = sbt("STT", (128, 176), F32)
    RW = sbt("RW", (16, 1792), F32)
    MSK = sbt("MSK", (128, G), U8)
    ONEC = sbt("ONEC", (128, 1), F32)
    ONEB = sbt("ONEB", (1, 128), BF16)
    IDB = sbt("IDB", (128, 128), BF16)
    IDF = sbt("IDF", (128, 128), F32)
    NEGB = sbt("NEGB", (128, 1), F32)
    ICOL = sbt("ICOL", (1, 4), I32)
    CB = sbt("CB", (128, 64), F32)
    T8 = sbt("T8", (128, 40), F32)
    GWT = sbt("GWT", (128, 96), F32)
    MEC = sbt("MEC", (128, 16), F32)
    CONSTS = sbt("CONSTS", (128, 2), F32)
    nc.const_aps.register(F32, LN_EPS, CONSTS[:, 0:1])
    nc.const_aps.register(F32, RMS_EPS, CONSTS[:, 1:2])

    PB = [ctx.enter_context(nc.psum_tensor("pb%d" % i, [128, 512], F32))
          for i in range(8)]

    p = Prog(nc)
    for s in ("sd", "sg", "so", "sv", "ss", "st", "sq", "scc",
              "sfd", "sfg", "sfs", "wp0", "wp1", "wp2", "wp3", "sgw", "sgc",
              "sgw2", "sgw3"):
        p.sem(ctx, s)

    S_SUM, S_SS, S_MU, S_VAR, S_R, S_NMR, S_M1, S_M2 = 0, 8, 16, 24, 32, 40, 48, 56
    S_ACC = 64
    S_GAM = 96
    S_BET = 112
    S_DEN = 128
    S_MISC = 144

    ns = dict(X=X, LT=LT, QT=QT, AR=AR, BR=BR, STT=STT, RW=RW, CB=CB, T8=T8,
              MSK=MSK, IDB=IDB, MEC=MEC, GWT=GWT, cc_g_in=cc_g_in,
              cc_g_out=cc_g_out, lnscr=lnscr, scr2=scr2, lgscr=lgscr)

    x_toks = [p.dma("sync", X[:, t * D:(t + 1) * D],
                    I["x"][t * 128:(t + 1) * 128, :]) for t in range(NT)]
    t_ones = p.emit("vector", lambda e: e.memset(ONEC[:, :], 1.0), inc="sv")
    t_onesb = p.emit("vector", lambda e: e.memset(ONEB[:, :], 1.0), inc="sv")
    t_negb = p.emit("vector", lambda e: e.memset(NEGB[:, :], -1e30), inc="sv")
    t_id0 = p.emit("vector", lambda e: e.memset(IDB[:, :], 1.0), inc="sv")
    t_idb = p.emit("gpsimd", lambda e: e.affine_select(
        IDB[:, :], IDB[:, :], pattern=[[1, 128]], compare_op=ALU.is_equal,
        fill=0.0, base=0, channel_multiplier=-1), deps=[t_id0], inc="sq")
    t_if0 = p.emit("vector", lambda e: e.memset(IDF[:, :], 1.0), inc="sv")
    t_idf = p.emit("gpsimd", lambda e: e.affine_select(
        IDF[:, :], IDF[:, :], pattern=[[1, 128]], compare_op=ALU.is_equal,
        fill=0.0, base=0, channel_multiplier=-1), deps=[t_if0], inc="sq")
    t_coef = p.dma("sync", CB[0:8, 0:1], I["coef"][:, :])
    t_c0 = p.emit("vector", lambda e: e.memset(CONSTS[:, 0:1], LN_EPS), inc="sv")
    t_c1 = p.emit("vector", lambda e: e.memset(CONSTS[:, 1:2], RMS_EPS),
                  inc="sv")

    t_wq = p.dma("gpsimd", wq_bf[:, :], I["L0_attn_Wq"][:, :], sem="sgw")
    t_wv = p.dma("gpsimd", wv_bf[:, :], I["L0_attn_Wv"][:, :], sem="sgw")
    t_wg = p.dma("gpsimd", wg_bf[:, :], I["L0_attn_Wg"][:, :], sem="sgw")
    t_wo0 = p.dma("gpsimd", wo0_bf[:, :], I["L0_attn_Wo"][:, :], sem="sgw")
    t_wimg = ("sgw", p.cnt["sgw"])
    t_wq = t_wv = t_wg = t_wo0 = t_wimg
    l1_imgs = {"L1_attn_Wqk": wqk1_bf, "L1_attn_Wv": wv1_bf,
               "L1_attn_Wu": wu1_bf, "L1_attn_Wo": wo1_bf}
    imgs_tok = {}

    t_tmp = p.dma("sync", RW[0:1, 128:129], I["L1_ffn_temperature"][:, :])
    t_scl = p.dma("sync", RW[0:1, 129:130], I["L1_ffn_scale"][:, :])
    f_setup_d = p.fence("sync", "sd")
    t_tm = p.emit("vector", lambda e: e.tensor_scalar_max(
        RW[0:1, 128:129], RW[0:1, 128:129], 0.1), deps=[f_setup_d], inc="sv")
    t_tr = p.emit("vector", lambda e: e.reciprocal(
        RW[0:1, 130:131], RW[0:1, 128:129]), deps=[t_tm], inc="sv")
    t_sh = p.emit("vector", lambda e: e.tensor_scalar_mul(
        RW[0:1, 131:132], RW[0:1, 129:130], 0.5), deps=[f_setup_d, t_tr],
        inc="sv")
    t_rs = p.dma("gpsimd", rowscr[5:6, 0:4], RW[0:1, 128:132], deps=[t_tr, t_sh])
    f_rs = p.fence("gpsimd", "sg")
    t_hs = p.dma("gpsimd", CB[:, 42:43], rowscr[5:6, 3:4].to_broadcast([128, 1]),
                 deps=[f_rs], allow_slow_non_contiguous=True)
    t_it = p.dma("gpsimd", CB[:, 43:44], rowscr[5:6, 2:3].to_broadcast([128, 1]),
                 deps=[f_rs], allow_slow_non_contiguous=True)
    f_setup_g = p.fence("gpsimd", "sg")
    t_hs = t_it = f_setup_g
    t_coef = f_setup_d

    state = {"sum_tok": [None] * NT, "x_tok": [f_setup_d] * NT,
             "lt_free": [], "qt_free": [], "lnscr_free": [], "scr2_free": [],
             "stream_guard": None}

    def ln_prologue(rms=False, f32_consumer=None, post_mul=None,
                    post_mul_tok=None):
        sum_toks, ss_toks = [], []
        for t in range(NT):
            xs = X[:, t * D:(t + 1) * D]
            if state["sum_tok"][t] is None:
                sum_toks.append(p.emit(
                    "vector", lambda e, xs=xs, t=t: e.reduce_sum(
                        STT[:, S_SUM + t:S_SUM + t + 1], xs, axis=AX),
                    deps=[state["x_tok"][t]], inc="sv"))
            else:
                sum_toks.append(state["sum_tok"][t])
            sqsl = 4096 if (t % 2 == 0) else 6144
            sq_t = p.emit(
                "vector", lambda e, xs=xs, sqsl=sqsl: e.tensor_mul(
                    BR[:, sqsl:sqsl + D], xs, xs),
                deps=[state["x_tok"][t], ss_toks[-2] if len(ss_toks) >= 2
                      else None], inc="sv")
            ss_toks.append(p.emit(
                "vector", lambda e, t=t, sqsl=sqsl: e.reduce_sum(
                    STT[:, S_SS + t:S_SS + t + 1], BR[:, sqsl:sqsl + D],
                    axis=AX),
                deps=[sq_t], inc="sv"))
        mu_t = p.emit("vector", lambda e: e.tensor_scalar_mul(
            STT[:, S_MU:S_MU + 8], STT[:, S_SUM:S_SUM + 8], 1.0 / D),
            deps=sum_toks, inc="sv")
        ex2_t = p.emit("vector", lambda e: e.tensor_scalar_mul(
            STT[:, S_VAR:S_VAR + 8], STT[:, S_SS:S_SS + 8], 1.0 / D),
            deps=ss_toks, inc="sv")
        mu2_t = p.emit("vector", lambda e: e.tensor_tensor(
            STT[:, S_M1:S_M1 + 8], STT[:, S_MU:S_MU + 8],
            STT[:, S_MU:S_MU + 8], ALU.mult), deps=[mu_t], inc="sv")
        var_t = p.emit("vector", lambda e: e.tensor_sub(
            STT[:, S_VAR:S_VAR + 8], STT[:, S_VAR:S_VAR + 8],
            STT[:, S_M1:S_M1 + 8]), deps=[ex2_t, mu2_t], inc="sv")
        lnv_t = p.emit("scalar", lambda e: e.activation(
            out=STT[:, S_M1:S_M1 + 8], in_=STT[:, S_VAR:S_VAR + 8],
            func=AF.Ln, bias=LN_EPS), deps=[var_t, t_c0], inc="ss")
        r_t = p.emit("scalar", lambda e: e.activation(
            out=STT[:, S_R:S_R + 8], in_=STT[:, S_M1:S_M1 + 8],
            func=AF.Exp, scale=-0.5), deps=[lnv_t], inc="ss")
        if rms:
            r2 = p.emit("vector", lambda e: e.tensor_tensor(
                STT[:, S_M1:S_M1 + 8], STT[:, S_R:S_R + 8],
                STT[:, S_R:S_R + 8], ALU.mult), deps=[r_t], inc="sv")
            msq = p.emit("vector", lambda e: e.tensor_tensor(
                STT[:, S_M1:S_M1 + 8], STT[:, S_M1:S_M1 + 8],
                STT[:, S_VAR:S_VAR + 8], ALU.mult), deps=[r2], inc="sv")
            lq = p.emit("scalar", lambda e: e.activation(
                out=STT[:, S_M1:S_M1 + 8], in_=STT[:, S_M1:S_M1 + 8],
                func=AF.Ln, bias=RMS_EPS), deps=[msq, t_c1], inc="ss")
            q_t = p.emit("scalar", lambda e: e.activation(
                out=STT[:, S_M1:S_M1 + 8], in_=STT[:, S_M1:S_M1 + 8],
                func=AF.Exp, scale=-0.5), deps=[lq], inc="ss")
            r_t = p.emit("vector", lambda e: e.tensor_tensor(
                STT[:, S_R:S_R + 8], STT[:, S_R:S_R + 8],
                STT[:, S_M1:S_M1 + 8], ALU.mult), deps=[q_t, r_t], inc="sv")
        nmu_t = p.emit("vector", lambda e: e.tensor_scalar_mul(
            STT[:, S_NMR:S_NMR + 8], STT[:, S_MU:S_MU + 8], -1.0),
            deps=[mu_t], inc="sv")
        nmr_t = p.emit("vector", lambda e: e.tensor_tensor(
            STT[:, S_NMR:S_NMR + 8], STT[:, S_NMR:S_NMR + 8],
            STT[:, S_R:S_R + 8], ALU.mult), deps=[nmu_t, r_t], inc="sv")

        st_toks = []
        guard_br = [[], []]
        guard_ar = [[], []]
        lt_deps = state["lt_free"]
        for t in range(NT):
            xs = X[:, t * D:(t + 1) * D]
            sl = t % 2
            need_f32 = (f32_consumer is not None) or (post_mul is not None)
            if need_f32:
                dst = BR[:, sl * D:(sl + 1) * D]
                aff = p.emit("scalar", lambda e, xs=xs, t=t, dst=dst:
                             e.activation(out=dst, in_=xs, func=AF.Identity,
                                          bias=STT[:, S_NMR + t:S_NMR + t + 1],
                                          scale=STT[:, S_R + t:S_R + t + 1]),
                             deps=[nmr_t] + guard_br[sl], inc="ss")
                cons = None
                if f32_consumer is not None:
                    cons = f32_consumer(t, dst, aff)
                if post_mul is not None:
                    cast = p.emit("vector", lambda e, dst=dst, sl=sl:
                                  e.tensor_tensor(AR[:, sl * D:(sl + 1) * D],
                                                  dst, post_mul, ALU.mult),
                                  deps=[aff, post_mul_tok] + guard_ar[sl],
                                  inc="sv")
                else:
                    cast = p.emit("vector", lambda e, dst=dst, sl=sl:
                                  e.tensor_copy(AR[:, sl * D:(sl + 1) * D],
                                                dst),
                                  deps=[aff] + guard_ar[sl], inc="sv")
                guard_br[sl] = [x for x in (cons, cast) if x is not None]
            else:
                cast = p.emit("scalar", lambda e, xs=xs, t=t, sl=sl:
                              e.activation(out=AR[:, sl * D:(sl + 1) * D],
                                           in_=xs, func=AF.Identity,
                                           bias=STT[:, S_NMR + t:S_NMR + t + 1],
                                           scale=STT[:, S_R + t:S_R + t + 1]),
                              deps=[nmr_t] + guard_ar[sl], inc="ss")
            stk = p.dma("gpsimd", lnscr[t * 128:(t + 1) * 128, :],
                        AR[:, sl * D:(sl + 1) * D],
                        deps=[cast] + state["lnscr_free"])
            guard_ar[sl] = [stk]
            st_toks.append(stk)
        f_store = p.fence("gpsimd", "sg")
        for dc in range(DCH):
            p.dmat("sync", LT[:, dc * TPC:(dc + 1) * TPC],
                   lnscr[:, dc * 128:(dc + 1) * 128],
                   deps=[f_store] + lt_deps)
        f_lt = p.fence("sync", "sd")
        lt_toks = [f_lt] * DCH
        state["lnscr_free"] = [f_lt]
        return lt_toks

    def finalize_sums(tile, toks, nparts):
        tk = p.emit("vector", lambda e, tile=tile, nparts=nparts: e.reduce_sum(
            STT[:, S_SUM + tile:S_SUM + tile + 1],
            STT[:, S_ACC + tile * 4:S_ACC + tile * 4 + nparts], axis=AX),
            deps=toks, inc="sv")
        state["sum_tok"][tile] = tk
        state["x_tok"][tile] = tk

    def emit_l0ffn(call):
        import os as _os
        _part = int(_os.environ.get("L0FFN_PART", "9"))
        gate_mm = {}

        def gate_consumer(t, dst, aff_tok):
            last = None
            for seg in range(4):
                last = p.emit("tensor", lambda e, dst=dst, seg=seg, t=t:
                              e.matmul(PB[4 + seg][0:1, :], ONEC[:, :],
                                       dst[:, seg * 512:(seg + 1) * 512],
                                       start=(t == 0), stop=(t == NT - 1)),
                              deps=[aff_tok, t_ones], inc="st")
            gate_mm[t] = last
            return last

        lt_toks = ln_prologue(rms=False, f32_consumer=gate_consumer
                              if _part >= 2 else None)
        if _part < 3:
            return
        ev = [p.emit("vector", lambda e, seg=seg: e.tensor_copy(
            BR[0:1, 6144 + seg * 512:6144 + (seg + 1) * 512],
            PB[4 + seg][0:1, :]),
            deps=[gate_mm[NT - 1]], inc="sv") for seg in range(4)]
        t_in = p.dma("gpsimd", cc_g_in[:, :], BR[0:1, 6144:6144 + D], deps=ev)
        t_ar = p.emit("gpsimd", lambda e: e.collective_compute(
            "AllReduce", ALU.add, replica_groups=[list(range(NCORES))],
            ins=[cc_g_in[:, :]], outs=[cc_g_out[:, :]]),
            deps=[t_in], inc="scc")
        t_mc = p.dma("gpsimd", MEC[:, :],
                     cc_g_out[0, :].rearrange("(c p) -> p c", p=128),
                     deps=[t_ar], allow_slow_non_contiguous=True)
        t_mc = p.fence("gpsimd", "sg")
        for ex in range(6):
            p.dma("sync", GWT[:, ex * 16:(ex + 1) * 16],
                  I["L0_ffn_gate_W"][ex, :].rearrange("(c p) -> p c", p=128),
                  allow_slow_non_contiguous=True)
        t_gw = [p.fence("sync", "sd")]
        glast = None
        for dc in range(DCH):
            glast = p.emit("tensor", lambda e, dc=dc: e.matmul(
                PB[0][0:1, 0:6], MEC[:, dc:dc + 1],
                GWT[:, dc:dc + 81:16],
                start=(dc == 0), stop=(dc == DCH - 1)),
                deps=[t_mc] + t_gw, inc="st")
        t_l = p.emit("vector", lambda e: e.tensor_scalar_mul(
            RW[0:1, 0:6], PB[0][0:1, 0:6], 1.0 / 8192.0),
            deps=[glast], inc="sv")
        t_mx1 = p.emit("vector", lambda e: e.reduce_max(
            RW[0:1, 8:9], RW[0:1, 0:6], axis=AX), deps=[t_l], inc="sv")
        t_o1 = p.emit("vector", lambda e: e.tensor_single_scalar(
            RW[0:1, 10:16], RW[0:1, 0:6], RW[0:1, 8:9], ALU.is_ge),
            deps=[t_mx1], inc="sv")
        t_msk = p.emit("vector", lambda e: e.scalar_tensor_tensor(
            RW[0:1, 16:22], RW[0:1, 10:16], -1e30, RW[0:1, 0:6],
            ALU.mult, ALU.add), deps=[t_o1], inc="sv")
        t_mx2 = p.emit("vector", lambda e: e.reduce_max(
            RW[0:1, 9:10], RW[0:1, 16:22], axis=AX), deps=[t_msk], inc="sv")
        t_o2 = p.emit("vector", lambda e: e.tensor_single_scalar(
            RW[0:1, 24:30], RW[0:1, 16:22], RW[0:1, 9:10], ALU.is_ge),
            deps=[t_mx2], inc="sv")
        t_io = p.emit("gpsimd", lambda e: e.iota(
            RW[0:1, 56:62], pattern=[[1, 6]], base=0, channel_multiplier=0,
            allow_small_or_imprecise_dtypes=True), inc="sq")
        t_i1 = p.emit("vector", lambda e: e.tensor_mul(
            RW[0:1, 34:40], RW[0:1, 10:16], RW[0:1, 56:62]),
            deps=[t_o1, t_io], inc="sv")
        t_i1r = p.emit("vector", lambda e: e.reduce_sum(
            RW[0:1, 44:45], RW[0:1, 34:40], axis=AX), deps=[t_i1], inc="sv")
        t_i2 = p.emit("vector", lambda e: e.tensor_mul(
            RW[0:1, 34:40], RW[0:1, 24:30], RW[0:1, 56:62]),
            deps=[t_o2, t_i1r], inc="sv")
        t_i2r = p.emit("vector", lambda e: e.reduce_sum(
            RW[0:1, 45:46], RW[0:1, 34:40], axis=AX), deps=[t_i2], inc="sv")
        t_ic = p.emit("vector", lambda e: e.tensor_copy(
            ICOL[0:1, 0:2], RW[0:1, 44:46]), deps=[t_i2r], inc="sv")
        t_d = p.emit("vector", lambda e: e.tensor_sub(
            RW[0:1, 30:31], RW[0:1, 9:10], RW[0:1, 8:9]),
            deps=[t_mx2], inc="sv")
        t_e = p.emit("scalar", lambda e: e.activation(
            out=RW[0:1, 31:32], in_=RW[0:1, 30:31], func=AF.Exp),
            deps=[t_d], inc="ss")
        t_e1 = p.emit("vector", lambda e: e.tensor_scalar_add(
            RW[0:1, 31:32], RW[0:1, 31:32], 1.0), deps=[t_e], inc="sv")
        t_wa = p.emit("vector", lambda e: e.reciprocal(
            RW[0:1, 32:33], RW[0:1, 31:32]), deps=[t_e1], inc="sv")
        t_wb = p.emit("vector", lambda e: e.tensor_scalar(
            RW[0:1, 33:34], RW[0:1, 32:33], -1.0, 1.0, ALU.mult, ALU.add),
            deps=[t_wa], inc="sv")
        cell = {}
        t_rv = p.emit("gpsimd", lambda e: (
            cell.__setitem__("rA", e.value_load(ICOL[0:1, 0:1])),
            cell.__setitem__("rB", e.value_load(ICOL[0:1, 1:2])),
            e.engine_nop())[-1], deps=[t_ic], inc="sq")
        t_b2a = p.dma("gpsimd", BR[0:1, 4096:4096 + D],
                      lambda: I["L0_ffn_b2"][cell["rA"], :], deps=[t_rv])
        t_b2b = p.dma("gpsimd", BR[0:1, 6144:6144 + D],
                      lambda: I["L0_ffn_b2"][cell["rB"], :], deps=[t_rv])
        fb2 = p.fence("gpsimd", "sg")
        t_b2a = t_b2b = fb2
        t_c1 = p.emit("vector", lambda e: e.tensor_single_scalar(
            BR[0:1, 4096:4096 + D], BR[0:1, 4096:4096 + D],
            RW[0:1, 32:33], ALU.mult), deps=[t_b2a, t_wa], inc="sv")
        t_c2 = p.emit("vector", lambda e: e.tensor_single_scalar(
            BR[0:1, 6144:6144 + D], BR[0:1, 6144:6144 + D],
            RW[0:1, 33:34], ALU.mult), deps=[t_b2b, t_wb], inc="sv")
        t_c3 = p.emit("vector", lambda e: e.tensor_add(
            BR[0:1, 4096:4096 + D], BR[0:1, 4096:4096 + D],
            BR[0:1, 6144:6144 + D]), deps=[t_c1, t_c2], inc="sv")
        t_b2s = p.dma("gpsimd", rowscr[1:2, 0:D], BR[0:1, 4096:4096 + D],
                      deps=[t_c3])
        t_ws = p.dma("gpsimd", rowscr[2:3, 0:2], RW[0:1, 32:34],
                     deps=[t_wa, t_wb])
        fst = p.fence("gpsimd", "sg")
        t_b2r = p.dma("gpsimd", AR[0:1, 6656:6656 + D], rowscr[1:2, 0:D],
                      deps=[fst] + state["qt_free"])
        t_b1a = p.dma("gpsimd", CB[:, 16:16 + FCH],
                      lambda: I["L0_ffn_b1"][cell["rA"], :]
                      .rearrange("(c p) -> p c", p=128), deps=[t_rv],
                      allow_slow_non_contiguous=True)
        t_b1b = p.dma("gpsimd", CB[:, 27:27 + FCH],
                      lambda: I["L0_ffn_b1"][cell["rB"], :]
                      .rearrange("(c p) -> p c", p=128), deps=[t_rv],
                      allow_slow_non_contiguous=True)
        t_wac = p.dma("gpsimd", CB[:, 40:41],
                      rowscr[2:3, 0:1].to_broadcast([128, 1]), deps=[fst],
                      allow_slow_non_contiguous=True)
        t_wbc = p.dma("gpsimd", CB[:, 41:42],
                      rowscr[2:3, 1:2].to_broadcast([128, 1]), deps=[fst],
                      allow_slow_non_contiguous=True)
        fsm = p.fence("gpsimd", "sg")
        t_b2r = t_b1a = t_b1b = t_wac = t_wbc = fsm
        w1c_toks = []
        for c in range(2):
            for si, rn in ((0, "rA"), (1, "rB")):
                p.dma("gpsimd", w1sel[call][si, c * 1024:(c + 1) * 1024, :],
                      (lambda rn=rn, c=c:
                       I["L0_ffn_W1"][cell[rn], c * 1024:(c + 1) * 1024, :]),
                      deps=[t_rv, state.get("stream_guard")], sem="sgc")
            w1c_toks.append(p.fence("gpsimd", "sgc"))
        for si, rn in ((0, "rA"), (1, "rB")):
            p.dma("gpsimd", w2sel[call][si, :, :],
                  (lambda rn=rn: I["L0_ffn_W2"][cell[rn], :, :]),
                  deps=[t_rv], sem="sgc")
        f_w2c = p.fence("gpsimd", "sgc")
        if _part < 4:
            return
        grps = [(0, 6), (6, 11)]
        if _os.environ.get("L0FFN_LIM"):
            _lim = _os.environ["L0FFN_LIM"]
            grps = grps[:1] if "g" in _lim else grps
        res_toks = {t: [] for t in range(NT)}
        ht_free = {}
        qt_guard = state["qt_free"]
        last_s1 = None
        pb_guard = {0: t_l, 4: ev[0], 5: ev[1], 6: ev[2], 7: ev[3]}
        w1_guard = {}
        w2_prev = {}
        _tts = range(1) if _os.environ.get("L0FFN_LIM", "").count("t") else range(2)
        _lv = _os.environ.get("L0FFN_LIM", "")
        _ss = (0,) if "s" in _lv else ((1,) if "B" in _lv else (0, 1))
        for tt in _tts:
            h_done = {}
            for s in _ss:
                rn = "rA" if s == 0 else "rB"
                for (g0, g1) in grps:
                    nfc = g1 - g0
                    mm_last = {}
                    w1_t = {}
                    for dc in range(DCH):
                        bslot = dc % 2
                        buf = 4096 + bslot * 768
                        w1_t[dc] = p.dma(
                            "sync", AR[:, buf:buf + nfc * 128],
                            w1sel[call][s, dc * 128:(dc + 1) * 128,
                                        g0 * 128:g1 * 128],
                            deps=[w1c_toks[dc // 8], w1_guard.get(bslot),
                                  state.get("stream_guard")],
                            sem="wp%d" % bslot)
                        last = None
                        for fc in range(g0, g1):
                            last = p.emit(
                                "tensor",
                                lambda e, dc=dc, fc=fc, g0=g0, buf=buf, tt=tt:
                                e.matmul(
                                    PB[fc - g0][:, :],
                                    AR[:, buf + (fc - g0) * 128:
                                       buf + (fc - g0) * 128 + 128],
                                    LT[:, dc * TPC + tt * 512:
                                       dc * TPC + tt * 512 + 512],
                                    start=(dc == 0), stop=(dc == DCH - 1)),
                                deps=[w1_t[dc], lt_toks[dc],
                                      ht_free.get((s, fc)),
                                      pb_guard.get(fc - g0) if dc == 0
                                      else None], inc="st")
                        mm_last[dc] = last
                        w1_guard[bslot] = last
                        last_s1 = last
                    for fc in range(g0, g1):
                        bcol = CB[:, (16 if s == 0 else 27) + fc:
                                  (17 if s == 0 else 28) + fc]
                        ge = p.emit(
                            "scalar",
                            lambda e, fc=fc, g0=g0, s=s, bcol=bcol:
                            e.activation(
                                out=QT[:, s * 5632 + fc * 512:
                                       s * 5632 + fc * 512 + 512],
                                in_=PB[fc - g0][:, :], func=AF.Gelu,
                                bias=bcol),
                            deps=[mm_last[DCH - 1], t_b1a, t_b1b] + qt_guard,
                            inc="ss")
                        pb_guard[fc - g0] = ge
                        wm = p.emit(
                            "vector",
                            lambda e, fc=fc, s=s: e.tensor_single_scalar(
                                QT[:, s * 5632 + fc * 512:
                                   s * 5632 + fc * 512 + 512],
                                QT[:, s * 5632 + fc * 512:
                                   s * 5632 + fc * 512 + 512],
                                CB[:, 40 + s:41 + s], ALU.mult),
                            deps=[ge, t_wac, t_wbc], inc="sv")
                        h_done[(s, fc)] = wm
            for d2s in range(4 if _part >= 5 else 0):
                last_by_tg = {}
                for s in (0, 1):
                    rn = "rA" if s == 0 else "rB"
                    for fc in range(FCH):
                        bslot = (s * FCH + fc) % 2
                        buf2 = 5632 + bslot * 512
                        w2_t = p.dma(
                            "sync", AR[:, buf2:buf2 + 512],
                            w2sel[call][s, fc * 128:(fc + 1) * 128,
                                        d2s * 512:(d2s + 1) * 512],
                            deps=[f_w2c, w2_prev.get(bslot)] + ev,
                            sem="wp%d" % (2 + bslot))
                        for tg in range(4):
                            tk = p.emit(
                                "tensor",
                                lambda e, s=s, fc=fc, tg=tg, buf2=buf2:
                                e.matmul(
                                    PB[4 + tg][:, :],
                                    QT[:, s * 5632 + fc * 512 + tg * 128:
                                       s * 5632 + fc * 512 + tg * 128 + 128],
                                    AR[:, buf2:buf2 + 512],
                                    start=(s == 0 and fc == 0), stop=False),
                                deps=[w2_t, h_done[(s, fc)],
                                      pb_guard.get(4 + tg)
                                      if (s == 0 and fc == 0) else None],
                                inc="st")
                            last_by_tg[tg] = tk
                        w2_prev[bslot] = last_by_tg[3]
                        if d2s == 3 and tt == 1:
                            pass
                        if d2s == 3:
                            ht_free[(s, fc)] = last_by_tg[3]
                for tg in range(4):
                    bt = p.emit(
                        "tensor", lambda e, tg=tg, d2s=d2s: e.matmul(
                            PB[4 + tg][:, :], ONEB[:, :],
                            AR[0:1, 6656 + d2s * 512:6656 + (d2s + 1) * 512],
                            start=False, stop=True),
                        deps=[last_by_tg[tg], t_b2r, t_onesb], inc="st")
                    tile = tt * 4 + tg
                    xs = X[:, tile * D + d2s * 512: tile * D + (d2s + 1) * 512]
                    rv = p.emit(
                        "vector", lambda e, tg=tg, xs=xs, tile=tile, d2s=d2s:
                        e.scalar_tensor_tensor(
                            xs, PB[4 + tg][:, :], 0.5, xs, ALU.mult, ALU.add,
                            accum_out=STT[:, S_ACC + tile * 4 + d2s:
                                          S_ACC + tile * 4 + d2s + 1]),
                        deps=[bt], inc="sv")
                    pb_guard[4 + tg] = rv
                    res_toks[tile].append(rv)
        if _part >= 5:
            for t in range(NT):
                finalize_sums(t, res_toks[t], 4)
        state["lt_free"] = [last_s1]
        state["qt_free"] = [v for v in ht_free.values()]
        state["stream_guard"] = ("st", p.cnt["st"])

    def emit_l0attn():
        p.dma("gpsimd", wc_bf[:, :], I["L1_ffn_Wc"][:, :], sem="sgw2")
        p.dma("gpsimd", wd_bf[:, :], I["L1_ffn_Wd"][:, :], sem="sgw2")
        p.dma("gpsimd", wu_bf[:, :], I["L1_ffn_Wu"][:, :], sem="sgw2")
        imgs_tok["wcdu"] = ("sgw2", p.cnt["sgw2"])
        lt_toks = ln_prologue(rms=False)
        t_gm = p.dma("sync", STT[:, S_GAM:S_GAM + 16],
                     I["L0_attn_ln_gamma"][0, :].rearrange("(c p) -> p c", p=128),
                     allow_slow_non_contiguous=True)
        t_bt = p.dma("sync", STT[:, S_BET:S_BET + 16],
                     I["L0_attn_ln_beta"][0, :].rearrange("(c p) -> p c", p=128),
                     allow_slow_non_contiguous=True)
        t_be = p.dma("sync", RW[0:1, 112:112 + NHEAD], I["L0_attn_beta"][:, :])
        f_sm = p.fence("sync", "sd")
        t_gm = t_bt = t_be = f_sm
        t_p2 = p.emit("scalar", lambda e: e.activation(
            out=RW[0:1, 96:96 + NHEAD], in_=RW[0:1, 112:112 + NHEAD],
            func=AF.Exp, scale=-LN2), deps=[t_be], inc="ss")
        t_lam = p.emit("vector", lambda e: e.tensor_scalar(
            RW[0:1, 64:64 + NHEAD], RW[0:1, 96:96 + NHEAD], -1.0, 1.0,
            ALU.mult, ALU.add), deps=[t_p2], inc="sv")
        t_cl1 = p.emit("vector", lambda e: e.tensor_scalar_min(
            RW[0:1, 64:64 + NHEAD], RW[0:1, 64:64 + NHEAD], 1.0 - 1e-9),
            deps=[t_lam], inc="sv")
        t_cl2 = p.emit("vector", lambda e: e.tensor_scalar_max(
            RW[0:1, 64:64 + NHEAD], RW[0:1, 64:64 + NHEAD], F32_TINY),
            deps=[t_cl1], inc="sv")
        t_ll = p.emit("scalar", lambda e: e.activation(
            out=RW[0:1, 80:80 + NHEAD], in_=RW[0:1, 64:64 + NHEAD],
            func=AF.Ln), deps=[t_cl2], inc="ss")
        t_ls = p.dma("gpsimd", rowscr[3:4, 0:NHEAD], RW[0:1, 64:64 + NHEAD],
                     deps=[t_cl2])
        f_ls = p.fence("gpsimd", "sg")
        t_lb = p.dma("gpsimd", CB[:, 8:8 + NHEAD],
                     rowscr[3:4, 0:NHEAD].to_broadcast([128, NHEAD]),
                     deps=[f_ls], allow_slow_non_contiguous=True)
        t_lb = p.fence("gpsimd", "sg")

        vt_toks = {}
        qt_guard = state["qt_free"]
        sbufg = {0: None, 1: None, 2: None}
        sq_state = {"j": 0}

        def run_strip_phase(img, imgtok, compute_set, nsets=4, entry=()):
            loaded = {}

            def load(j, extra=()):
                buf = sq_state["j"] % 3
                sq_state["j"] += 1
                base = buf * 4096
                os_, h = divmod(j, 2)
                for k in range(8):
                    dc = h * 8 + k
                    p.dmat("sync",
                           AR[:, base + k * 512: base + (k + 1) * 512],
                           img[os_ * 512:(os_ + 1) * 512,
                               dc * 128:(dc + 1) * 128],
                           deps=[imgtok, sbufg[buf]] + list(extra))
                loaded[j] = (buf, base, p.fence("sync", "sd"))
            for j in range(min(3, nsets * 2)):
                load(j, extra=entry)
            nxt = 3
            for os_ in range(nsets):
                b0, base0, f0 = loaded[2 * os_]
                b1, base1, f1 = loaded[2 * os_ + 1]

                def chunk(dc, base0=base0, base1=base1, f0=f0, f1=f1):
                    if dc < 8:
                        return base0 + dc * 512, f0
                    return base1 + (dc - 8) * 512, f1
                lastmm = compute_set(os_, chunk)
                sbufg[b0] = lastmm
                sbufg[b1] = lastmm
                while nxt < nsets * 2 and nxt <= 2 * os_ + 4:
                    load(nxt)
                    nxt += 1

        def vt_set(os_, chunk):
            last = None
            for oc in range(os_ * 4, os_ * 4 + 4):
                for T in range(2):
                    pb = PB[(oc * 2 + T) % 4]
                    for dc in range(DCH):
                        col, ftok = chunk(dc)
                        last = p.emit(
                            "tensor",
                            lambda e, col=col, oc=oc, T=T, dc=dc, os_=os_,
                            pb=pb:
                            e.matmul(pb[:, :],
                                     AR[:, col + (oc - os_ * 4) * 128:
                                        col + (oc - os_ * 4) * 128 + 128],
                                     LT[:, dc * TPC + T * 512:
                                        dc * TPC + T * 512 + 512],
                                     start=(dc == 0), stop=(dc == DCH - 1)),
                            deps=[ftok, lt_toks[dc],
                                  vt_toks.get((oc - 2, T))], inc="st")
                    ev = p.emit("vector", lambda e, oc=oc, T=T, pb=pb:
                                e.tensor_copy(QT[:, oc * TPC + T * 512:
                                                 oc * TPC + T * 512 + 512],
                                              pb[:, :]),
                                deps=[last] + qt_guard, inc="sv")
                    vt_toks[(oc, T)] = ev
            return last

        run_strip_phase(wv_bf, t_wv, vt_set, entry=qt_guard)
        p1_toks = []
        for oc in range(NHEAD):
            prev = None
            for T in range(2):
                sl = T % 2
                dst = BR[:, 8192 + sl * 512:8192 + (sl + 1) * 512]
                ini = 0.0 if T == 0 else BR[:, 8192 + (1 - sl) * 512 + 511:
                                            8192 + (1 - sl) * 512 + 512]
                prev = p.emit(
                    "vector", lambda e, oc=oc, T=T, dst=dst, ini=ini:
                    e.tensor_tensor_scan(
                        out=dst,
                        data0=CB[:, 8 + oc:9 + oc].to_broadcast([128, 512]),
                        data1=QT[:, oc * TPC + T * 512:oc * TPC + T * 512 + 512],
                        initial=ini, op0=ALU.mult, op1=ALU.add),
                    deps=[vt_toks[(oc, T)], t_lb, prev], inc="sv")
            cp = p.emit("vector", lambda e, oc=oc: e.tensor_copy(
                CB[:, 24 + oc:25 + oc], BR[:, 8192 + 512 + 511:8192 + 1024]),
                deps=[prev], inc="sv")
            p1_toks.append(cp)

        t_ssav = p.dma("gpsimd", cc_s_in[:, :], CB[:, 24:24 + NHEAD],
                       deps=p1_toks)
        f_sv = p.fence("gpsimd", "sg")
        t_ag = p.emit("gpsimd", lambda e: e.collective_compute(
            "AllGather", ALU.bypass, replica_groups=[list(range(NCORES))],
            ins=[cc_s_in[:, :]], outs=[cc_s_out[:, :]]),
            deps=[f_sv], inc="scc")
        t_csb = p.dma("sync",
                      BR[:, 9216:9216 + 128].rearrange("p (j h) -> p j h", j=8),
                      cc_s_out.rearrange("(j p) h -> p j h", p=128),
                      deps=[t_ag], allow_slow_non_contiguous=True)
        t_csb = p.fence("sync", "sd")
        t_llb = p.dma("gpsimd", rowscr[3:4, 16:16 + NHEAD], RW[0:1, 80:80 + NHEAD],
                      deps=[t_ll])
        f_llb = p.fence("gpsimd", "sg")
        t_ll8 = p.dma("gpsimd", RW[0:8, 144:144 + NHEAD],
                      rowscr[3:4, 16:16 + NHEAD].to_broadcast([8, NHEAD]),
                      deps=[f_llb], allow_slow_non_contiguous=True)
        t_ll8 = p.fence("gpsimd", "sg")
        t_wm = p.emit("vector", lambda e: e.tensor_single_scalar(
            RW[0:8, 144:144 + NHEAD], RW[0:8, 144:144 + NHEAD],
            CB[0:8, 0:1], ALU.mult), deps=[t_ll8, t_coef], inc="sv")
        t_wme = p.emit("scalar", lambda e: e.activation(
            out=RW[0:8, 144:144 + NHEAD], in_=RW[0:8, 144:144 + NHEAD],
            func=AF.Exp), deps=[t_wm], inc="ss")
        t_wms = p.dma("gpsimd", rowscr[4:5, 0:128], RW[0:8, 144:144 + NHEAD],
                      deps=[t_wme])
        f_wms = p.fence("gpsimd", "sg")
        t_wmb = p.dma("gpsimd", BR[:, 9344:9344 + 128],
                      rowscr[4:5, 0:128].to_broadcast([128, 128]),
                      deps=[f_wms], allow_slow_non_contiguous=True)
        t_wmb = p.fence("gpsimd", "sg")
        t_pr = p.emit("vector", lambda e: e.tensor_mul(
            BR[:, 9216:9216 + 128], BR[:, 9216:9216 + 128],
            BR[:, 9344:9344 + 128]), deps=[t_csb, t_wmb], inc="sv")
        t_car = p.emit("vector", lambda e: e.tensor_reduce(
            CB[:, 44:44 + NHEAD],
            BR[:, 9216:9216 + 128].rearrange("p (j h) -> p h j", j=8),
            axis=AX, op=ALU.add), deps=[t_pr], inc="sv")

        pt_toks = {}
        ret_end_prev = {}
        for T in range(2):
            s_toks = {}
            ret_end = {}
            sum_last = None
            sq_mm_last = None
            sq_act_last = None
            qg_state = {"sum": sum_last, "sqa": sq_act_last,
                        "sqm": sq_mm_last}

            def q_set(os_, chunk, T=T, s_toks=s_toks, ret_end=ret_end,
                      ret_end_prev=ret_end_prev):
                last = None
                for oc in range(os_ * 4, os_ * 4 + 4):
                    sl = oc % 2
                    dst = BR[:, 8192 + sl * 512:8192 + (sl + 1) * 512]
                    ini = (CB[:, 44 + oc:45 + oc] if T == 0
                           else CB[:, 24 + oc:25 + oc])
                    ideps = ([t_car] if T == 0 else [ret_end_prev[oc]])
                    sc = p.emit(
                        "vector", lambda e, oc=oc, T=T, dst=dst, ini=ini:
                        e.tensor_tensor_scan(
                            out=dst,
                            data0=CB[:, 8 + oc:9 + oc].to_broadcast([128, 512]),
                            data1=QT[:, oc * TPC + T * 512:
                                     oc * TPC + T * 512 + 512],
                            initial=ini, op0=ALU.mult, op1=ALU.add),
                        deps=ideps + [vt_toks[(oc, T)],
                                      s_toks.get(oc - 2)], inc="sv")
                    sv = p.emit("vector", lambda e, oc=oc, dst=dst:
                                e.tensor_copy(CB[:, 24 + oc:25 + oc],
                                              dst[:, 511:512]),
                                deps=[sc], inc="sv")
                    ret_end[oc] = sv
                    pb = PB[4 + (oc % 2)]
                    for dc in range(DCH):
                        col, ftok = chunk(dc)
                        last = p.emit(
                            "tensor",
                            lambda e, col=col, oc=oc, T=T, dc=dc, os_=os_,
                            pb=pb:
                            e.matmul(pb[:, :],
                                     AR[:, col + (oc - os_ * 4) * 128:
                                        col + (oc - os_ * 4) * 128 + 128],
                                     LT[:, dc * TPC + T * 512:
                                        dc * TPC + T * 512 + 512],
                                     start=(dc == 0), stop=(dc == DCH - 1)),
                            deps=[ftok, lt_toks[dc], s_toks.get(oc - 2),
                                  pt_toks.get((NHEAD - 2 + (oc % 2), T - 1))],
                            inc="st")
                    stk = p.emit("vector", lambda e, oc=oc, pb=pb, dst=dst:
                                 e.tensor_mul(BR[:, oc * 512:(oc + 1) * 512],
                                              pb[:, :], dst),
                                 deps=[last, sc], inc="sv")
                    s_toks[oc] = stk
                    qg_state["sum"] = p.emit(
                        "tensor", lambda e, oc=oc: e.matmul(
                            PB[6][0:1, :], ONEC[:, :],
                            BR[:, oc * 512:(oc + 1) * 512],
                            start=(oc == 0), stop=(oc == NHEAD - 1)),
                        deps=[stk, t_ones], inc="st")
                    qg_state["sqa"] = p.emit(
                        "scalar", lambda e, oc=oc: e.activation(
                            out=BR[:, 9216:9216 + 512],
                            in_=BR[:, oc * 512:(oc + 1) * 512],
                            func=AF.Square),
                        deps=[stk, qg_state["sqm"]], inc="ss")
                    qg_state["sqm"] = p.emit(
                        "tensor", lambda e, oc=oc: e.matmul(
                            PB[7][0:1, :], ONEC[:, :], BR[:, 9216:9216 + 512],
                            start=(oc == 0), stop=(oc == NHEAD - 1)),
                        deps=[qg_state["sqa"], t_ones], inc="st")
                return last

            run_strip_phase(wq_bf, t_wq, q_set)
            sum_last = qg_state["sum"]
            sq_act_last = qg_state["sqa"]
            sq_mm_last = qg_state["sqm"]
            ret_end_prev = ret_end
            t_su = p.emit("vector", lambda e: e.tensor_scalar_mul(
                RW[0:1, 256:768], PB[6][0:1, :], 1.0 / D),
                deps=[sum_last], inc="sv")
            t_s2 = p.emit("vector", lambda e: e.tensor_scalar_mul(
                RW[0:1, 768:1280], PB[7][0:1, :], 1.0 / D),
                deps=[sq_mm_last], inc="sv")
            t_m2 = p.emit("vector", lambda e: e.tensor_mul(
                RW[0:1, 1280:1792], RW[0:1, 256:768], RW[0:1, 256:768]),
                deps=[t_su], inc="sv")
            t_va = p.emit("vector", lambda e: e.tensor_sub(
                RW[0:1, 768:1280], RW[0:1, 768:1280], RW[0:1, 1280:1792]),
                deps=[t_s2, t_m2], inc="sv")
            t_lv = p.emit("scalar", lambda e: e.activation(
                out=RW[0:1, 1280:1792], in_=RW[0:1, 768:1280], func=AF.Ln,
                bias=LN_EPS), deps=[t_va, t_c0], inc="ss")
            t_rr = p.emit("scalar", lambda e: e.activation(
                out=RW[0:1, 1280:1792], in_=RW[0:1, 1280:1792], func=AF.Exp,
                scale=-0.5), deps=[t_lv], inc="ss")
            t_nm = p.emit("vector", lambda e: e.tensor_scalar_mul(
                RW[0:1, 256:768], RW[0:1, 256:768], -1.0),
                deps=[t_su, t_m2], inc="sv")
            t_nb = p.emit("vector", lambda e: e.tensor_mul(
                RW[0:1, 256:768], RW[0:1, 256:768], RW[0:1, 1280:1792]),
                deps=[t_nm, t_rr], inc="sv")
            t_ab = p.dma("gpsimd", rowscr[6:7, 0:512], RW[0:1, 1280:1792],
                         deps=[t_nb])
            t_ab2 = p.dma("gpsimd", rowscr[6:7, 512:1024], RW[0:1, 256:768],
                          deps=[t_nb])
            f_ab = p.fence("gpsimd", "sg")
            t_abb = p.dma("gpsimd", BR[:, 9728:9728 + 512],
                          rowscr[6:7, 0:512].to_broadcast([128, 512]),
                          deps=[f_ab], allow_slow_non_contiguous=True)
            t_bbb = p.dma("gpsimd", BR[:, 10240:10240 + 512],
                          rowscr[6:7, 512:1024].to_broadcast([128, 512]),
                          deps=[f_ab], allow_slow_non_contiguous=True)
            f_ab2 = p.fence("gpsimd", "sg")
            t_abb = t_bbb = f_ab2
            ln_toks = {}
            for oc in range(NHEAD):
                a1 = p.emit("vector", lambda e, oc=oc: e.tensor_mul(
                    BR[:, oc * 512:(oc + 1) * 512],
                    BR[:, oc * 512:(oc + 1) * 512], BR[:, 9728:9728 + 512]),
                    deps=[t_abb, s_toks[oc]], inc="sv")
                a2 = p.emit("vector", lambda e, oc=oc: e.tensor_add(
                    BR[:, oc * 512:(oc + 1) * 512],
                    BR[:, oc * 512:(oc + 1) * 512], BR[:, 10240:10240 + 512]),
                    deps=[t_bbb, a1], inc="sv")
                a3 = p.emit("vector", lambda e, oc=oc: e.tensor_scalar(
                    BR[:, oc * 512:(oc + 1) * 512],
                    BR[:, oc * 512:(oc + 1) * 512],
                    STT[:, S_GAM + oc:S_GAM + oc + 1],
                    STT[:, S_BET + oc:S_BET + oc + 1], ALU.mult, ALU.add),
                    deps=[a2, t_gm, t_bt], inc="sv")
                ln_toks[oc] = a3
            def g_set(os_, chunk, T=T, ln_toks=ln_toks,
                      sq_mm_last=sq_mm_last):
                last = None
                for oc in range(os_ * 4, os_ * 4 + 4):
                    pb = PB[4 + (oc % 2)]
                    for dc in range(DCH):
                        col, ftok = chunk(dc)
                        last = p.emit(
                            "tensor",
                            lambda e, col=col, oc=oc, T=T, dc=dc, os_=os_,
                            pb=pb:
                            e.matmul(pb[:, :],
                                     AR[:, col + (oc - os_ * 4) * 128:
                                        col + (oc - os_ * 4) * 128 + 128],
                                     LT[:, dc * TPC + T * 512:
                                        dc * TPC + T * 512 + 512],
                                     start=(dc == 0), stop=(dc == DCH - 1)),
                            deps=[ftok, pt_toks.get((oc - 2, T)),
                                  sq_mm_last], inc="st")
                    gsil = p.emit("scalar", lambda e, pb=pb:
                                  e.activation(out=BR[:, 9216:9216 + 512],
                                               in_=pb[:, :], func=AF.Silu),
                                  deps=[last, sq_mm_last,
                                        pt_toks.get((oc - 1, T)) if oc >= 1
                                        else pt_toks.get((NHEAD - 1, T - 1))],
                                  inc="ss")
                    ptk = p.emit(
                        "vector", lambda e, oc=oc, T=T: e.tensor_mul(
                            QT[:, oc * TPC + T * 512:oc * TPC + T * 512 + 512],
                            BR[:, oc * 512:(oc + 1) * 512],
                            BR[:, 9216:9216 + 512]),
                        deps=[gsil, ln_toks[oc]], inc="sv")
                    pt_toks[(oc, T)] = ptk
                return last

            run_strip_phase(wg_bf, t_wg, g_set)
        all_pt = [max((pt_toks[(oc, T)] for oc in range(NHEAD)
                       for T in range(2)), key=lambda tk: tk[1])]
        res_toks = {t: [] for t in range(NT)}
        fin_state = {"last": None}

        def fin_set(es, chunk):
            last = None
            for tg in range(NT):
                pb = PB[tg % 4]
                for dc in range(DCH):
                    col, ftok = chunk(dc)
                    last = p.emit(
                        "tensor", lambda e, col=col, es=es, tg=tg, dc=dc,
                        pb=pb:
                        e.matmul(pb[:, :],
                                 QT[:, dc * TPC + tg * 128:
                                    dc * TPC + tg * 128 + 128],
                                 AR[:, col:col + 512],
                                 start=(dc == 0), stop=(dc == DCH - 1)),
                        deps=[ftok,
                              res_toks[tg - 4][es] if tg >= 4 else
                              (res_toks[tg + 4][es - 1] if es > 0 else None)]
                        + all_pt, inc="st")
                xs = X[:, tg * D + es * 512: tg * D + (es + 1) * 512]
                rv = p.emit(
                    "vector", lambda e, xs=xs, tg=tg, es=es, pb=pb:
                    e.scalar_tensor_tensor(
                        xs, pb[:, :], 1.0, xs, ALU.mult, ALU.add,
                        accum_out=STT[:, S_ACC + tg * 4 + es:
                                      S_ACC + tg * 4 + es + 1]),
                    deps=[last], inc="sv")
                res_toks[tg].append(rv)
                fin_state["last"] = last
            return last

        run_strip_phase(wo0_bf, t_wo0, fin_set)
        last_fin = fin_state["last"]
        for t in range(NT):
            finalize_sums(t, res_toks[t], 4)
        state["lt_free"] = [last_fin]
        state["qt_free"] = [last_fin]
        state["stream_guard"] = ("st", p.cnt["st"])

    def emit_l1attn():
        lt_toks = ln_prologue(rms=False)
        BIA = 8192
        wimg_tok = imgs_tok["l1a"]
        wp_prev = {0: state.get("stream_guard"), 1: state.get("stream_guard")}

        def proj_phase(wname, bias_name, bias_dep, psum_guards, evac):
            t_bias = p.dma("gpsimd", AR[0:1, BIA:BIA + D], I[bias_name][:, :],
                           deps=bias_dep)
            f_bias = p.fence("gpsimd", "sg")
            ev_toks = {}
            bt_toks = {}
            for seg in range(4):
                for dc in range(DCH):
                    bslot = dc % 2
                    buf = bslot * 512
                    wtk = p.dma("gpsimd", AR[:, buf:buf + 512],
                                l1_imgs[wname][dc * 128:(dc + 1) * 128,
                                               seg * 512:(seg + 1) * 512],
                                deps=[wimg_tok, wp_prev.get(bslot)],
                                sem="wp%d" % bslot)
                    tk = None
                    for t in range(NT):
                        guard = (psum_guards.get(t) if seg == 0
                                 else ev_toks.get((t, seg - 1)))
                        tk = p.emit(
                            "tensor",
                            lambda e, dc=dc, t=t, buf=buf:
                            e.matmul(PB[t][:, :],
                                     LT[:, dc * TPC + t * 128:
                                        dc * TPC + t * 128 + 128],
                                     AR[:, buf:buf + 512],
                                     start=(dc == 0), stop=False),
                            deps=[wtk, lt_toks[dc], guard], inc="st")
                    wp_prev[bslot] = tk
                for t in range(NT):
                    bt = p.emit(
                        "tensor", lambda e, t=t, seg=seg:
                        e.matmul(PB[t][:, :], ONEB[:, :],
                                 AR[0:1, BIA + seg * 512:
                                    BIA + (seg + 1) * 512],
                                 start=False, stop=True),
                        deps=[f_bias, t_onesb], inc="st")
                    bt_toks[(t, seg)] = bt
                    ev_toks[(t, seg)] = evac(t, seg, bt)
            return ev_toks, bt_toks

        k_store = {}
        kst = {"slot": None}

        def evac_k(t, seg, bt):
            mn = p.emit("vector", lambda e, t=t: e.tensor_scalar_min(
                BR[:, 0:512], PB[t][:, :], 0.0),
                deps=[bt, kst["slot"]], inc="sv")
            ex = p.emit("scalar", lambda e: e.activation(
                out=BR[:, 0:512], in_=BR[:, 0:512], func=AF.Exp),
                deps=[mn], inc="ss")
            mx = p.emit("vector", lambda e, t=t: e.tensor_scalar_max(
                BR[:, 512:1024], PB[t][:, :], 0.0),
                deps=[ex], inc="sv")
            kb = p.emit("vector", lambda e: e.scalar_tensor_tensor(
                AR[:, 1024:1536], BR[:, 0:512], -1.0, BR[:, 512:1024],
                ALU.add, ALU.add), deps=[mx], inc="sv")
            st = p.dma("scalar", scr2[t * 128:(t + 1) * 128,
                                      seg * 512:(seg + 1) * 512],
                       AR[:, 1024:1536], deps=[kb] + state["scr2_free"])
            state["scr2_free"] = []
            k_store[(t, seg)] = st
            kst["slot"] = st
            return mx

        kev, kbt = proj_phase("L1_attn_Wqk", "L1_attn_bqk",
                              state["qt_free"], {}, evac_k)
        f_k = p.fence("scalar", "so")

        v_store = {}
        vst = {"slot": None}

        def evac_v(t, seg, bt):
            sv_ = p.emit("scalar", lambda e, t=t: e.activation(
                out=AR[:, 1536:2048], in_=PB[t][:, :], func=AF.Silu),
                deps=[bt, vst["slot"]], inc="ss")
            st = p.dma("scalar", lnscr[t * 128:(t + 1) * 128,
                                       seg * 512:(seg + 1) * 512],
                       AR[:, 1536:2048], deps=[sv_] + state["lnscr_free"])
            state["lnscr_free"] = []
            v_store[(t, seg)] = st
            vst["slot"] = st
            return sv_

        vev, vbt = proj_phase("L1_attn_Wv", "L1_attn_bv",
                              [kbt[(NT - 1, 3)]],
                              {t: kev[(t, 3)] for t in range(NT)}, evac_v)
        f_v = p.fence("scalar", "so")

        for dc in range(DCH):
            p.dmat("sync", QT[:, dc * TPC:(dc + 1) * TPC],
                   scr2[:, dc * 128:(dc + 1) * 128],
                   deps=[f_k] + state["qt_free"])
        f_qt = p.fence("sync", "sd")
        kv_mm = {}
        khb_t = {}
        vhb_t = {}
        acc_tok = {}
        for t in range(NT):
            khb_t[t] = p.dma("sync", AR[:, 2048:2048 + D],
                             scr2[t * 128:(t + 1) * 128, :],
                             deps=[f_k, kv_mm.get((t - 1, NHEAD - 1))])
            vhb_t[t] = p.dma("sync", AR[:, 4096:4096 + D],
                             lnscr[t * 128:(t + 1) * 128, :],
                             deps=[f_v, kv_mm.get((t - 1, NHEAD - 1))])
            for h in range(NHEAD):
                kv_mm[(t, h)] = p.emit(
                    "tensor", lambda e, t=t, h=h:
                    e.matmul(PB[4 + h // 4][:, (h % 4) * 128:(h % 4) * 128 + 128],
                             AR[:, 2048 + h * 128:2048 + (h + 1) * 128],
                             AR[:, 4096 + h * 128:4096 + (h + 1) * 128],
                             start=True, stop=True),
                    deps=[khb_t[t], vhb_t[t],
                          acc_tok.get((t - 1, h // 4))], inc="st")
            for b in range(4):
                dst = BR[:, 2048 + b * 512:2048 + (b + 1) * 512]
                if t == 0:
                    acc_tok[(t, b)] = p.emit(
                        "vector", lambda e, dst=dst, b=b: e.tensor_copy(
                            dst, PB[4 + b][:, :]),
                        deps=[kv_mm[(t, 4 * b + 3)]], inc="sv")
                else:
                    acc_tok[(t, b)] = p.emit(
                        "vector", lambda e, dst=dst, b=b: e.tensor_add(
                            dst, dst, PB[4 + b][:, :]),
                        deps=[kv_mm[(t, 4 * b + 3)],
                              acc_tok[(t - 1, b)]], inc="sv")
        kv_last = kv_mm[(NT - 1, NHEAD - 1)]
        kv_ev = [acc_tok[(NT - 1, b)] for b in range(4)]
        t_kvs = p.dma("gpsimd", cc_kv_in[:, :], BR[:, 2048:2048 + D],
                      deps=kv_ev)
        f_kvs = p.fence("gpsimd", "sg")
        t_kvar = p.emit("gpsimd", lambda e: e.collective_compute(
            "AllReduce", ALU.add, replica_groups=[list(range(NCORES))],
            ins=[cc_kv_in[:, :]], outs=[cc_kv_out[:, :]]),
            deps=[f_kvs], inc="scc")
        t_kvb = p.dma("gpsimd", AR[:, 6144:6144 + D], cc_kv_out[:, :],
                      deps=[t_kvar])
        f_kvb = p.fence("gpsimd", "sg")
        u_store = {}
        ust = {"slot": None}

        def evac_u(t, seg, bt):
            su = p.emit("scalar", lambda e, t=t: e.activation(
                out=AR[:, 1536:2048], in_=PB[t][:, :], func=AF.Silu),
                deps=[bt, ust["slot"]], inc="ss")
            st = p.dma("scalar", lnscr[t * 128:(t + 1) * 128,
                                       seg * 512:(seg + 1) * 512],
                       AR[:, 1536:2048], deps=[su, kv_last])
            u_store[(t, seg)] = st
            ust["slot"] = st
            return su

        u_guards = {t: (kv_ev[t - 4] if t >= 4 else vev[(t, 3)])
                    for t in range(NT)}
        uev, ubt = proj_phase("L1_attn_Wu", "L1_attn_bu",
                              [vbt[(NT - 1, 3)]], u_guards, evac_u)
        f_u = p.fence("scalar", "so")
        t_gb = p.dma("gpsimd", BR[:, 4096:4096 + D],
                     I["L1_attn_ln_g"].to_broadcast([128, D]),
                     allow_slow_non_contiguous=True, deps=[kv_last])
        t_bb = p.dma("gpsimd", BR[:, 6144:6144 + D],
                     I["L1_attn_ln_b"].to_broadcast([128, D]),
                     allow_slow_non_contiguous=True)
        f_gbb = p.fence("gpsimd", "sg")
        prod_store = {}
        prev_ev = {}
        t_uf_prev = None
        t_pd_prev = None
        for t in range(NT):
            pb = PB[t % 2]
            seg_toks = []
            for seg in range(4):
                last = None
                for hh in range(4):
                    h = seg * 4 + hh
                    last = p.emit(
                        "tensor", lambda e, t=t, h=h, hh=hh, pb=pb:
                        e.matmul(pb[:, hh * 128:(hh + 1) * 128],
                                 QT[:, h * TPC + t * 128:h * TPC + t * 128 + 128],
                                 AR[:, 6144 + h * 128:6144 + (h + 1) * 128],
                                 start=True, stop=True),
                        deps=[f_qt, f_kvb,
                              uev[(t, 3)] if t < 2 else
                              prev_ev.get((t - 2, seg)),
                              seg_toks[seg - 1] if seg > 0 else
                              prev_ev.get((t - 2, 3))], inc="st")
                ev = p.emit("scalar", lambda e, t=t, seg=seg, pb=pb:
                            e.activation(out=BR[:, seg * 512:(seg + 1) * 512],
                                         in_=pb[:, :], func=AF.Copy),
                            deps=[last, t_pd_prev], inc="ss")
                prev_ev[(t, seg)] = ev
                seg_toks.append(ev)
            t_s1 = p.emit("vector", lambda e: e.reduce_sum(
                STT[:, S_M1:S_M1 + 1], BR[:, 0:D], axis=AX),
                deps=seg_toks, inc="sv")
            t_s2a = p.emit("vector", lambda e: e.tensor_mul(
                BR[:, 8192:8192 + D], BR[:, 0:D], BR[:, 0:D]),
                deps=seg_toks, inc="sv")
            t_s2 = p.emit("vector", lambda e: e.reduce_sum(
                STT[:, S_M1 + 1:S_M1 + 2], BR[:, 8192:8192 + D], axis=AX),
                deps=[t_s2a], inc="sv")
            t_mu = p.emit("vector", lambda e: e.tensor_scalar_mul(
                STT[:, S_M1 + 2:S_M1 + 3], STT[:, S_M1:S_M1 + 1], 1.0 / D),
                deps=[t_s1], inc="sv")
            t_e2 = p.emit("vector", lambda e: e.tensor_scalar_mul(
                STT[:, S_M1 + 3:S_M1 + 4], STT[:, S_M1 + 1:S_M1 + 2], 1.0 / D),
                deps=[t_s2], inc="sv")
            t_m2 = p.emit("vector", lambda e: e.tensor_tensor(
                STT[:, S_M1 + 4:S_M1 + 5], STT[:, S_M1 + 2:S_M1 + 3],
                STT[:, S_M1 + 2:S_M1 + 3], ALU.mult), deps=[t_mu], inc="sv")
            t_va = p.emit("vector", lambda e: e.tensor_sub(
                STT[:, S_M1 + 3:S_M1 + 4], STT[:, S_M1 + 3:S_M1 + 4],
                STT[:, S_M1 + 4:S_M1 + 5]), deps=[t_e2, t_m2], inc="sv")
            t_lv = p.emit("scalar", lambda e: e.activation(
                out=STT[:, S_M1 + 4:S_M1 + 5], in_=STT[:, S_M1 + 3:S_M1 + 4],
                func=AF.Ln, bias=LN_EPS), deps=[t_va, t_c0], inc="ss")
            t_rr = p.emit("scalar", lambda e: e.activation(
                out=STT[:, S_M1 + 5:S_M1 + 6], in_=STT[:, S_M1 + 4:S_M1 + 5],
                func=AF.Exp, scale=-0.5), deps=[t_lv], inc="ss")
            t_nm = p.emit("vector", lambda e: e.tensor_scalar_mul(
                STT[:, S_M1 + 6:S_M1 + 7], STT[:, S_M1 + 2:S_M1 + 3], -1.0),
                deps=[t_mu], inc="sv")
            t_nb = p.emit("vector", lambda e: e.tensor_tensor(
                STT[:, S_M1 + 6:S_M1 + 7], STT[:, S_M1 + 6:S_M1 + 7],
                STT[:, S_M1 + 5:S_M1 + 6], ALU.mult),
                deps=[t_nm, t_rr], inc="sv")
            t_af = p.emit("scalar", lambda e: e.activation(
                out=BR[:, 0:D], in_=BR[:, 0:D], func=AF.Identity,
                bias=STT[:, S_M1 + 6:S_M1 + 7],
                scale=STT[:, S_M1 + 5:S_M1 + 6]), deps=[t_nb, t_s2], inc="ss")
            t_g1 = p.emit("vector", lambda e: e.tensor_mul(
                BR[:, 0:D], BR[:, 0:D], BR[:, 4096:4096 + D]),
                deps=[t_af, f_gbb], inc="sv")
            t_g2 = p.emit("vector", lambda e: e.tensor_add(
                BR[:, 0:D], BR[:, 0:D], BR[:, 6144:6144 + D]),
                deps=[t_g1], inc="sv")
            t_ul = p.dma("sync", AR[:, 2048:2048 + D],
                         lnscr[t * 128:(t + 1) * 128, :],
                         deps=[f_u, kv_last, t_uf_prev])
            f_ul = p.fence("sync", "sd")
            t_uf = p.emit("vector", lambda e: e.tensor_copy(
                BR[:, 2048:2048 + D], AR[:, 2048:2048 + D]),
                deps=[f_ul], inc="sv")
            t_uf_prev = t_uf
            t_pd = p.emit("vector", lambda e: e.tensor_mul(
                AR[:, 4096:4096 + D], BR[:, 0:D], BR[:, 2048:2048 + D]),
                deps=[t_uf, t_g2, kv_last], inc="sv")
            t_pd_prev = t_pd
            pst = p.dma("scalar", scr2[t * 128:(t + 1) * 128, :],
                        AR[:, 4096:4096 + D],
                        deps=[t_pd] + ([f_qt] if t == 0 else []))
            prod_store[t] = pst
        t_bo = p.dma("gpsimd", AR[0:1, BIA:BIA + D], I["L1_attn_bo"][:, :],
                     deps=[ubt[(NT - 1, 3)]])
        f_bo = p.fence("gpsimd", "sg")
        f_ps = p.fence("scalar", "so")
        for dc in range(DCH):
            p.dmat("sync", LT[:, dc * TPC:(dc + 1) * TPC],
                   scr2[:, dc * 128:(dc + 1) * 128],
                   deps=[f_ps, ubt[(NT - 1, 3)]] + state["lt_free"])
        f_ltp = p.fence("sync", "sd")
        res_toks = {t: [] for t in range(NT)}
        for seg in range(4):
            for dc in range(DCH):
                bslot = dc % 2
                buf = bslot * 512
                wtk = p.dma("gpsimd", AR[:, buf:buf + 512],
                            wo1_bf[dc * 128:(dc + 1) * 128,
                                   seg * 512:(seg + 1) * 512],
                            deps=[wimg_tok, wp_prev.get(bslot)],
                            sem="wp%d" % bslot)
                tk = None
                for t in range(NT):
                    tk = p.emit(
                        "tensor",
                        lambda e, dc=dc, t=t, buf=buf:
                        e.matmul(PB[t][:, :],
                                 LT[:, dc * TPC + t * 128:
                                    dc * TPC + t * 128 + 128],
                                 AR[:, buf:buf + 512],
                                 start=(dc == 0), stop=False),
                        deps=[wtk, f_ltp,
                              res_toks[t][seg - 1] if seg > 0 else None],
                        inc="st")
                wp_prev[bslot] = tk
            for t in range(NT):
                bt = p.emit(
                    "tensor", lambda e, t=t, seg=seg: e.matmul(
                        PB[t][:, :], ONEB[:, :],
                        AR[0:1, BIA + seg * 512:BIA + (seg + 1) * 512],
                        start=False, stop=True),
                    deps=[f_bo, t_onesb], inc="st")
                xs = X[:, t * D + seg * 512: t * D + (seg + 1) * 512]
                rv = p.emit(
                    "vector", lambda e, xs=xs, t=t, seg=seg:
                    e.scalar_tensor_tensor(
                        xs, PB[t][:, :], 1.0, xs, ALU.mult, ALU.add,
                        accum_out=STT[:, S_ACC + t * 4 + seg:
                                      S_ACC + t * 4 + seg + 1]),
                    deps=[bt], inc="sv")
                res_toks[t].append(rv)
        for t in range(NT):
            finalize_sums(t, res_toks[t], 4)
        barrier = ("st", p.cnt["st"])
        state["lt_free"] = [barrier]
        state["qt_free"] = [barrier]
        state["scr2_free"] = [f_ltp]
        state["stream_guard"] = barrier

    def emit_l1ffn(call):
        t_wc = t_wd = t_wu = imgs_tok["wcdu"]
        if call == 0:
            p.dma("gpsimd", wqk1_bf[:, :], I["L1_attn_Wqk"][:, :], sem="sgw3")
            p.dma("gpsimd", wv1_bf[:, :], I["L1_attn_Wv"][:, :], sem="sgw3")
            p.dma("gpsimd", wu1_bf[:, :], I["L1_attn_Wu"][:, :], sem="sgw3")
            p.dma("gpsimd", wo1_bf[:, :], I["L1_attn_Wo"][:, :], sem="sgw3")
            imgs_tok["l1a"] = ("sgw3", p.cnt["sgw3"])
        t_nw = p.dma("gpsimd", BR[:, 8192:8192 + D],
                     I["L1_ffn_norm_w"].to_broadcast([128, D]),
                     allow_slow_non_contiguous=True)
        t_nw = p.fence("gpsimd", "sg")
        lt_toks = ln_prologue(rms=True, post_mul=BR[:, 8192:8192 + D],
                              post_mul_tok=t_nw)
        t_gen = p.dma("gpsimd",
                      QT[:, 0:8192].rearrange("p (gc c) -> p gc c", c=GD),
                      I["L1_ffn_genes"].rearrange("(gc p) c -> p gc c", p=128),
                      deps=state["qt_free"])
        t_gen = p.fence("gpsimd", "sg")
        for cc in range(2):
            p.dmat("sync", QT[:, 8192 + cc * 2048:8192 + (cc + 1) * 2048],
                   wd_bf[:, cc * 128:(cc + 1) * 128],
                   deps=[t_wd] + state["qt_free"])
        for cc in range(2):
            p.dmat("sync", QT[:, 12288 + cc * 2048:12288 + (cc + 1) * 2048],
                   wu_bf[:, cc * 128:(cc + 1) * 128],
                   deps=[t_wu] + state["qt_free"])
        f_wdu = p.fence("sync", "sd")
        wdt_toks = [f_wdu, f_wdu]
        wut_toks = [f_wdu, f_wdu]
        lg_stores = {}
        ev_prev = {}
        sbufg = {0: None, 1: None, 2: None}
        sq_state = {"j": 0}
        loaded = {}

        def load_hs(j, extra=()):
            buf = sq_state["j"] % 3
            sq_state["j"] += 1
            base = buf * 4096
            gs, h = divmod(j, 2)
            for k in range(8):
                dc = h * 8 + k
                p.dmat("sync", AR[:, base + k * 512: base + (k + 1) * 512],
                       wc_bf[gs * 512:(gs + 1) * 512,
                             dc * 128:(dc + 1) * 128],
                       deps=[t_wc, sbufg[buf]] + list(extra))
            loaded[j] = (buf, base, p.fence("sync", "sd"))
        for j in range(3):
            load_hs(j, extra=state["qt_free"])
        nxt = 3
        for gs in range(8):
            b0, base0, f0 = loaded[2 * gs]
            b1, base1, f1 = loaded[2 * gs + 1]
            last = None
            for t in range(NT):
                pb = PB[t % 2]
                for dc in range(DCH):
                    col = (base0 + dc * 512 if dc < 8
                           else base1 + (dc - 8) * 512)
                    ftok = f0 if dc < 8 else f1
                    last = p.emit(
                        "tensor", lambda e, col=col, gs=gs, t=t, dc=dc, pb=pb:
                        e.matmul(pb[:, :],
                                 LT[:, dc * TPC + t * 128:
                                    dc * TPC + t * 128 + 128],
                                 AR[:, col:col + 512],
                                 start=(dc == 0), stop=(dc == DCH - 1)),
                        deps=[ftok, lt_toks[dc],
                              ev_prev.get(t - 2) if t >= 2
                              else ev_prev.get(t + 6)], inc="st")
                sl = t % 2
                slot_guard = (lg_stores.get((gs, t - 2)) if t >= 2
                              else lg_stores.get((gs - 1, t + 6)))
                ev = p.emit("scalar", lambda e, sl=sl, pb=pb: e.activation(
                    out=BR[:, sl * 512:(sl + 1) * 512],
                    in_=pb[:, :], func=AF.Identity, scale=CB[:, 43:44]),
                    deps=[last, t_it, slot_guard], inc="ss")
                ev_prev[t] = ev
                st = p.dma("gpsimd", lgscr[t * 128:(t + 1) * 128,
                                           gs * 512:(gs + 1) * 512],
                           BR[:, sl * 512:(sl + 1) * 512],
                           deps=[ev], sem="wp%d" % (2 + t % 2))
                lg_stores[(gs, t)] = st
            sbufg[b0] = last
            sbufg[b1] = last
            while nxt < 16 and nxt <= 2 * gs + 4:
                load_hs(nxt)
                nxt += 1
        lg_ceil = [("wp2", p.cnt["wp2"]), ("wp3", p.cnt["wp3"])]
        prev_tile = {}
        for t in range(NT):
            t_ld = p.dma("sync", BR[:, 0:G],
                         lgscr[t * 128:(t + 1) * 128, :],
                         deps=lg_ceil + [prev_tile.get(t - 1)])
            t_lc = p.emit("vector", lambda e: e.tensor_copy(
                BR[:, 4096:4096 + G], BR[:, 0:G]), deps=[t_ld], inc="sv")
            rt = t_lc
            for r in range(4):
                mx = p.emit("vector", lambda e, r=r: e.max(
                    out=T8[:, r * 8:(r + 1) * 8], in_=BR[:, 4096:4096 + G]),
                    deps=[rt], inc="sv")
                dr = p.emit("vector", lambda e: e.drain(), deps=[], inc="sv")
                if r < 3:
                    cm = p.emit("vector", lambda e, r=r: e.tensor_single_scalar(
                        MSK[:, :], BR[:, 4096:4096 + G],
                        T8[:, r * 8 + 7:r * 8 + 8], ALU.is_ge),
                        deps=[dr], inc="sv")
                    rt = p.emit("vector", lambda e: e.copy_predicated(
                        BR[:, 4096:4096 + G], MSK[:, :],
                        NEGB[:, :].to_broadcast([128, G])),
                        deps=[cm, t_negb], inc="sv")
                else:
                    rt = dr
            t_nx = p.emit("vector", lambda e: e.tensor_scalar_mul(
                STT[:, S_DEN:S_DEN + 1], T8[:, 0:1], -1.0),
                deps=[rt], inc="sv")
            t_ex = p.emit("scalar", lambda e: e.activation(
                out=BR[:, 4096:4096 + G], in_=BR[:, 0:G], func=AF.Exp,
                bias=STT[:, S_DEN:S_DEN + 1]), deps=[t_nx], inc="ss")
            t_wk = p.emit("vector", lambda e: e.scalar_tensor_tensor(
                BR[:, 4096:4096 + G], BR[:, 0:G], T8[:, 31:32],
                BR[:, 4096:4096 + G], ALU.is_ge, ALU.mult,
                accum_out=STT[:, S_DEN + 1:S_DEN + 2]),
            deps=[t_ex], inc="sv")
            wt_toks = []
            for grp in range(8):
                pb = PB[6 + grp % 2]
                trs = []
                for b in range(4):
                    gc = grp * 4 + b
                    trs.append(p.emit(
                        "tensor", lambda e, gc=gc, b=b, pb=pb: e.transpose(
                            pb[:, b * 128:(b + 1) * 128],
                            BR[:, 4096 + gc * 128:4096 + (gc + 1) * 128],
                            IDF[:, :]),
                        deps=[t_wk, t_idf,
                              wt_toks[grp - 2] if grp >= 2 else None],
                        inc="st"))
                wt = p.emit("vector", lambda e, grp=grp, pb=pb: e.tensor_copy(
                    AR[:, grp * 512:(grp + 1) * 512], pb[:, :]),
                    deps=[trs[3]], inc="sv")
                wt_toks.append(wt)
            t_rc = p.emit("vector", lambda e: e.reciprocal(
                STT[:, S_DEN + 2:S_DEN + 3], STT[:, S_DEN + 1:S_DEN + 2]),
                deps=[t_wk], inc="sv")
            last = None
            for gc in range(GCH):
                last = p.emit(
                    "tensor", lambda e, gc=gc: e.matmul(
                        PB[5][:, 0:256],
                        AR[:, gc * 128:gc * 128 + 128],
                        QT[:, gc * 256:(gc + 1) * 256],
                        start=(gc == 0), stop=(gc == GCH - 1)),
                    deps=[wt_toks[gc // 4], t_gen], inc="st")
            t_eb = p.emit("scalar", lambda e: e.activation(
                out=BR[:, 2048:2048 + 256], in_=PB[5][:, 0:256],
                func=AF.Identity, scale=STT[:, S_DEN + 2:S_DEN + 3]),
                deps=[last, t_rc], inc="ss")
            tr1 = p.emit("tensor", lambda e: e.transpose(
                PB[6][:, 0:128], BR[:, 2048:2048 + 128], IDF[:, :]),
                deps=[t_eb, t_idf], inc="st")
            tr2 = p.emit("tensor", lambda e: e.transpose(
                PB[6][:, 128:256], BR[:, 2048 + 128:2048 + 256], IDF[:, :]),
                deps=[t_eb], inc="st")
            t_et = p.emit("vector", lambda e: e.tensor_copy(
                AR[:, 4352:4352 + 256], PB[6][:, 0:256]),
                deps=[tr1, tr2], inc="sv")
            dn_toks = []
            up_toks = []
            for ds in range(4):
                lastd = None
                for cc in range(2):
                    lastd = p.emit(
                        "tensor", lambda e, ds=ds, cc=cc: e.matmul(
                            PB[0][:, :],
                            AR[:, 4352 + cc * 128:4352 + (cc + 1) * 128],
                            QT[:, 8192 + cc * 2048 + ds * 512:
                               8192 + cc * 2048 + (ds + 1) * 512],
                            start=(cc == 0), stop=(cc == 1)),
                        deps=[t_et, wdt_toks[cc],
                              dn_toks[ds - 1] if ds > 0 else None], inc="st")
                dn = p.emit("scalar", lambda e, ds=ds: e.activation(
                    out=BR[:, 0 + ds * 512:(ds + 1) * 512], in_=PB[0][:, :],
                    func=AF.Tanh), deps=[lastd], inc="ss")
                dn_toks.append(dn)
                lastu = None
                for cc in range(2):
                    lastu = p.emit(
                        "tensor", lambda e, ds=ds, cc=cc: e.matmul(
                            PB[1][:, :],
                            AR[:, 4352 + cc * 128:4352 + (cc + 1) * 128],
                            QT[:, 12288 + cc * 2048 + ds * 512:
                               12288 + cc * 2048 + (ds + 1) * 512],
                            start=(cc == 0), stop=(cc == 1)),
                        deps=[t_et, wut_toks[cc],
                              up_toks[ds - 1] if ds > 0 else None], inc="st")
                up = p.emit("scalar", lambda e, ds=ds: e.activation(
                    out=BR[:, 4096 + ds * 512:4096 + (ds + 1) * 512],
                    in_=PB[1][:, :], func=AF.Copy), deps=[lastu], inc="ss")
                up_toks.append(up)
            xs = X[:, t * D:(t + 1) * D]
            t_xa = p.emit("scalar", lambda e, xs=xs, t=t: e.activation(
                out=BR[:, 2048:2048 + D], in_=xs, func=AF.Identity,
                bias=STT[:, S_NMR + t:S_NMR + t + 1],
                scale=STT[:, S_R + t:S_R + t + 1]),
                deps=dn_toks + up_toks, inc="ss")
            t_xn = p.emit("vector", lambda e: e.tensor_mul(
                BR[:, 2048:2048 + D], BR[:, 2048:2048 + D],
                BR[:, 8192:8192 + D]), deps=[t_xa, t_nw], inc="sv")
            t_p1 = p.emit("vector", lambda e: e.tensor_mul(
                BR[:, 0:D], BR[:, 0:D], BR[:, 2048:2048 + D]),
                deps=[t_xn] + dn_toks, inc="sv")
            t_p2 = p.emit("vector", lambda e: e.tensor_mul(
                BR[:, 0:D], BR[:, 0:D], BR[:, 4096:4096 + D]),
                deps=[t_p1] + up_toks, inc="sv")
            rv = p.emit("vector", lambda e, xs=xs, t=t:
                        e.scalar_tensor_tensor(
                            xs, BR[:, 0:D], CB[:, 42:43], xs,
                            ALU.mult, ALU.add,
                            accum_out=STT[:, S_ACC + t * 4:S_ACC + t * 4 + 1]),
                        deps=[t_p2, t_hs], inc="sv")
            prev_tile[t] = rv
            finalize_sums(t, [rv], 1)
        bar = ("st", p.cnt["st"])
        state["lt_free"] = [bar]
        state["qt_free"] = [bar]
        state["stream_guard"] = bar

    stages = [lambda: emit_l0ffn(0), emit_l0attn, lambda: emit_l0ffn(1),
              lambda: emit_l1ffn(0), emit_l1attn, lambda: emit_l1ffn(1)]
    global _STAGE_MARKS
    _STAGE_MARKS = [dict(p.cnt)]
    for i in range(min(upto, 6)):
        stages[i]()
        _STAGE_MARKS.append(dict(p.cnt))

    out_toks = []
    for t in range(NT):
        out_toks.append(p.dma("sync", y[t * 128:(t + 1) * 128, :],
                              X[:, t * D:(t + 1) * D],
                              deps=[state["x_tok"][t]] +
                              ([state["sum_tok"][t]] if state["sum_tok"][t]
                               else [])))
    for i, (tn, tsh, tf) in enumerate(taps):
        src = tf(ns)
        from concourse.bass_types import DRamTensorHandle
        eng = "gpsimd" if (src.dtype != F32 or
                           isinstance(src.tensor, DRamTensorHandle)) else "sync"
        out_toks.append(p.dma(eng, tap_dram[i], src, deps=p.all_toks()))
    final_deps = out_toks + [("sg", p.cnt["sg"]), ("so", p.cnt["so"])]
    p.emit("gpsimd", lambda e: e.engine_nop(), deps=final_deps, inc="sq")

    with nc.Block() as block:
        @block.sync
        def _(eng):
            for fn in p.q["sync"]:
                fn(eng)

        @block.scalar
        def _(eng):
            for fn in p.q["scalar"]:
                fn(eng)

        @block.vector
        def _(eng):
            for fn in p.q["vector"]:
                fn(eng)

        @block.tensor
        def _(eng):
            for fn in p.q["tensor"]:
                fn(eng)

        @block.gpsimd
        def _(eng):
            for fn in p.q["gpsimd"]:
                fn(eng)

    ctx.close()
    return nc


_NC_CACHE = {}


def kernel(**inputs):
    x = np.ascontiguousarray(np.asarray(inputs["L0_attn_x"],
                                        dtype=np.float32).reshape(8192, D))
    key = 6
    if key not in _NC_CACHE:
        _NC_CACHE[key] = build_program(upto=6)
    nc = _NC_CACHE[key]

    def arr(name, shape):
        return np.ascontiguousarray(
            np.asarray(inputs[name], dtype=np.float32).reshape(shape))

    common = {
        "L0_attn_Wq": arr("L0_attn_Wq", (D, D)),
        "L0_attn_Wv": arr("L0_attn_Wv", (D, D)),
        "L0_attn_Wo": arr("L0_attn_Wo", (D, D)),
        "L0_attn_Wg": arr("L0_attn_Wg", (D, D)),
        "L0_attn_ln_gamma": arr("L0_attn_ln_gamma", (1, D)),
        "L0_attn_ln_beta": arr("L0_attn_ln_beta", (1, D)),
        "L0_attn_beta": arr("L0_attn_beta", (1, NHEAD)),
        "L0_ffn_gate_W": arr("L0_ffn_gate_W", (NEXP, D)),
        "L0_ffn_W1": arr("L0_ffn_W1", (NEXP, D, DFF)),
        "L0_ffn_b1": arr("L0_ffn_b1", (NEXP, DFF)),
        "L0_ffn_W2": arr("L0_ffn_W2", (NEXP, DFF, D)),
        "L0_ffn_b2": arr("L0_ffn_b2", (NEXP, D)),
        "L1_attn_Wqk": arr("L1_attn_Wqk", (D, D)),
        "L1_attn_bqk": arr("L1_attn_bqk", (1, D)),
        "L1_attn_Wv": arr("L1_attn_Wv", (D, D)),
        "L1_attn_bv": arr("L1_attn_bv", (1, D)),
        "L1_attn_Wu": arr("L1_attn_Wu", (D, D)),
        "L1_attn_bu": arr("L1_attn_bu", (1, D)),
        "L1_attn_Wo": arr("L1_attn_Wo", (D, D)),
        "L1_attn_bo": arr("L1_attn_bo", (1, D)),
        "L1_attn_ln_g": arr("L1_attn_ln_g", (1, D)),
        "L1_attn_ln_b": arr("L1_attn_ln_b", (1, D)),
        "L1_ffn_Wc": arr("L1_ffn_Wc", (G, D)),
        "L1_ffn_temperature": arr("L1_ffn_temperature", (1, 1)),
        "L1_ffn_genes": arr("L1_ffn_genes", (G, GD)),
        "L1_ffn_Wd": arr("L1_ffn_Wd", (D, GD)),
        "L1_ffn_Wu": arr("L1_ffn_Wu", (D, GD)),
        "L1_ffn_norm_w": arr("L1_ffn_norm_w", (1, D)),
        "L1_ffn_scale": arr("L1_ffn_scale", (1, 1)),
    }
    in_maps = []
    for i in range(NCORES):
        m = dict(common)
        m["x"] = np.ascontiguousarray(x[i * TPC:(i + 1) * TPC])
        coef = np.array([[1024.0 * (i - 1 - j)] if j < i else [1e30]
                         for j in range(8)], dtype=np.float32)
        m["coef"] = coef
        in_maps.append(m)
    res = run_bass_kernel_spmd(nc, in_maps, core_ids=list(range(NCORES)))
    out = np.concatenate([res.results[i]["y"] for i in range(NCORES)], axis=0)
    return out.reshape(1, 8192, D)


# version 05
# speedup vs baseline: 1.0027x; 1.0027x over previous
import traceback
import numpy as np
import concourse.bass as bass
import concourse.mybir as mybir
from concourse.bass_utils import run_bass_kernel_spmd

F32 = mybir.dt.float32
BF16 = mybir.dt.bfloat16
U8 = mybir.dt.uint8
I32 = mybir.dt.int32
AF = mybir.ActivationFunctionType
ALU = mybir.AluOpType
AX = mybir.AxisListType.X

D = 2048
NCORES = 8
TPC = 1024
NT = TPC // 128
DCH = D // 128
DFF = 1408
FCH = DFF // 128
NEXP = 6
G = 4096
GCH = G // 128
GD = 256
NHEAD = 16
LN_EPS = 1e-5
RMS_EPS = float(np.finfo(np.float32).eps)
F32_TINY = float(np.finfo(np.float32).tiny)
LN2 = float(np.log(2.0))

ENGINES = ("sync", "scalar", "vector", "tensor", "gpsimd")


class Prog:
    def __init__(self, nc):
        self.nc = nc
        self.q = {e: [] for e in ENGINES}
        self.sems = {}
        self.cnt = {}
        self.last_wait = {}

    def sem(self, ctx, name):
        h = ctx.enter_context(self.nc.semaphore(name))
        self.sems[name] = h
        self.cnt[name] = 0

    def emit(self, eng, fn, deps=(), inc=None, amt=1):
        deps = [d for d in deps if d is not None]
        real = []
        for (s, v) in deps:
            if v <= 0:
                continue
            key = (eng, s)
            if self.last_wait.get(key, 0) >= v:
                continue
            self.last_wait[key] = v
            real.append((self.sems[s], v))
        tok = None
        semh = None
        if inc is not None:
            self.cnt[inc] += amt
            tok = (inc, self.cnt[inc])
            semh = self.sems[inc]
        a = amt

        site = traceback.extract_stack()[-3:-1]

        def run(e, real=real, fn=fn, semh=semh, a=a, site=site):
            for (h, v) in real:
                e.wait_ge(h, v)
            try:
                inst = fn(e)
            except Exception:
                print("EMIT SITE:", "".join(traceback.format_list(site)))
                traceback.print_exc()
                raise
            if semh is not None:
                inst.then_inc(semh, a)
        self.q[eng].append(run)
        return tok

    def dma(self, eng, out, in_, deps=(), sem=None, **kw):
        if sem is None:
            sem = {"sync": "sd", "gpsimd": "sg", "scalar": "so"}[eng]

        def f(e, out=out, in_=in_, kw=kw):
            src = in_() if callable(in_) else in_
            try:
                return e.dma_start(out=out, in_=src, **kw)
            except Exception:
                print("DMA FAIL out=", out, "src=", src, "kw=", kw)
                raise
        return self.emit(eng, f, deps=deps, inc=sem, amt=16)

    def dmat(self, eng, out, in_, deps=(), sem=None):
        if sem is None:
            sem = {"sync": "sd", "scalar": "so"}[eng]
        return self.emit(eng, lambda e: e.dma_start_transpose(out, in_),
                         deps=deps, inc=sem, amt=16)

    def fence(self, eng, dma_sem):
        fsem = {"sync": "sfd", "gpsimd": "sfg", "scalar": "sfs"}[eng]
        ceil = self.cnt[dma_sem]
        self.cnt[fsem] += 1
        tok = (fsem, self.cnt[fsem])
        fh = self.sems[fsem]
        real = []
        key = (eng, dma_sem)
        if self.last_wait.get(key, 0) < ceil:
            self.last_wait[key] = ceil
            real.append((self.sems[dma_sem], ceil))

        def run(e, real=real, fh=fh):
            for (h, v) in real:
                e.wait_ge(h, v)
            e.sem_inc(fh, 1)
        self.q[eng].append(run)
        return tok

    def all_toks(self):
        return [(s, c) for s, c in self.cnt.items()]


def build_program(upto=6, taps=None):
    taps = taps or []
    nc = bass.Bass("TRN2", target_bir_lowering=False)

    I = {}

    def di(name, shape):
        I[name] = nc.dram_tensor(name, list(shape), F32, kind="ExternalInput").ap()

    di("x", (TPC, D))
    di("coef", (8, 1))
    di("L0_attn_Wq", (D, D)); di("L0_attn_Wv", (D, D))
    di("L0_attn_Wo", (D, D)); di("L0_attn_Wg", (D, D))
    di("L0_attn_ln_gamma", (1, D)); di("L0_attn_ln_beta", (1, D))
    di("L0_attn_beta", (1, NHEAD))
    di("L0_ffn_gate_W", (NEXP, D))
    di("L0_ffn_W1T", (NEXP, DFF, D)); di("L0_ffn_b1", (NEXP, DFF))
    di("L0_ffn_W2T", (NEXP, D, DFF)); di("L0_ffn_b2", (NEXP, D))
    di("L1_attn_WqkT", (D, D)); di("L1_attn_bqk", (1, D))
    di("L1_attn_WvT", (D, D)); di("L1_attn_bv", (1, D))
    di("L1_attn_WuT", (D, D)); di("L1_attn_bu", (1, D))
    di("L1_attn_WoT", (D, D)); di("L1_attn_bo", (1, D))
    di("L1_attn_ln_g", (1, D)); di("L1_attn_ln_b", (1, D))
    di("L1_ffn_Wc", (G, D)); di("L1_ffn_temperature", (1, 1))
    di("L1_ffn_genes", (G, GD)); di("L1_ffn_Wd", (D, GD)); di("L1_ffn_Wu", (D, GD))
    di("L1_ffn_norm_w", (1, D)); di("L1_ffn_scale", (1, 1))

    y = nc.dram_tensor("y", [TPC, D], F32, kind="ExternalOutput").ap()

    wq_bf = nc.dram_tensor("wq_bf", [D, D], BF16).ap()
    wv_bf = nc.dram_tensor("wv_bf", [D, D], BF16).ap()
    wg_bf = nc.dram_tensor("wg_bf", [D, D], BF16).ap()
    wo0_bf = nc.dram_tensor("wo0_bf", [D, D], BF16).ap()
    wc_bf = nc.dram_tensor("wc_bf", [G, D], BF16).ap()
    wd_bf = nc.dram_tensor("wd_bf", [D, GD], BF16).ap()
    wu_bf = nc.dram_tensor("wu_bf", [D, GD], BF16).ap()
    wqk1_bf = nc.dram_tensor("wqk1_bf", [D, D], BF16)
    wv1_bf = nc.dram_tensor("wv1_bf", [D, D], BF16)
    wu1_bf = nc.dram_tensor("wu1_bf", [D, D], BF16)
    wo1_bf = nc.dram_tensor("wo1_bf", [D, D], BF16)
    w1sel = [nc.dram_tensor("w1sel%d" % c, [2, DFF, D], BF16).ap()
             for c in range(2)]
    w2sel = [nc.dram_tensor("w2sel%d" % c, [2, D, DFF], BF16).ap()
             for c in range(2)]
    lnscr = nc.dram_tensor("lnscr", [TPC, D], BF16).ap()
    scr2 = nc.dram_tensor("scr2", [TPC, D], BF16).ap()
    lgscr = nc.dram_tensor("lgscr", [TPC, G], F32).ap()
    rowscr = nc.dram_tensor("rowscr", [8, 4096], F32).ap()
    cc_g_in = nc.dram_tensor("cc_g_in", [1, D], F32).ap()
    cc_g_out = nc.dram_tensor("cc_g_out", [1, D], F32).ap()
    cc_s_in = nc.dram_tensor("cc_s_in", [128, NHEAD], F32).ap()
    cc_s_out = nc.dram_tensor("cc_s_out", [128 * NCORES, NHEAD], F32).ap()
    cc_kv_in = nc.dram_tensor("cc_kv_in", [128, D], F32).ap()
    cc_kv_out = nc.dram_tensor("cc_kv_out", [128, D], F32).ap()
    tap_dram = [nc.dram_tensor("tap_" + tn, list(tsh), F32,
                               kind="ExternalOutput").ap()
                for (tn, tsh, _f) in taps]

    from contextlib import ExitStack
    ctx = ExitStack()

    def sbt(name, shape, dt):
        return ctx.enter_context(nc.sbuf_tensor(name, list(shape), dt))

    X = sbt("X", (128, NT * D), F32)
    LT = sbt("LT", (128, DCH * TPC), BF16)
    QT = sbt("QT", (128, DCH * TPC), BF16)
    AR = sbt("AR", (128, 12288), BF16)
    BR = sbt("BR", (128, 10752), F32)
    STT = sbt("STT", (128, 176), F32)
    RW = sbt("RW", (16, 1792), F32)
    MSK = sbt("MSK", (128, G), U8)
    ONEC = sbt("ONEC", (128, 1), F32)
    ONEB = sbt("ONEB", (1, 128), BF16)
    IDB = sbt("IDB", (128, 128), BF16)
    IDF = sbt("IDF", (128, 128), F32)
    NEGB = sbt("NEGB", (128, 1), F32)
    ICOL = sbt("ICOL", (1, 4), I32)
    CB = sbt("CB", (128, 64), F32)
    T8 = sbt("T8", (128, 40), F32)
    GWT = sbt("GWT", (128, 96), F32)
    MEC = sbt("MEC", (128, 16), F32)
    CONSTS = sbt("CONSTS", (128, 2), F32)
    nc.const_aps.register(F32, LN_EPS, CONSTS[:, 0:1])
    nc.const_aps.register(F32, RMS_EPS, CONSTS[:, 1:2])

    PB = [ctx.enter_context(nc.psum_tensor("pb%d" % i, [128, 512], F32))
          for i in range(8)]

    p = Prog(nc)
    for s in ("sd", "sg", "so", "sv", "ss", "st", "sq", "scc",
              "sfd", "sfg", "sfs", "wp0", "wp1", "wp2", "wp3", "sgw", "sgc",
              "sgw2", "sgw3"):
        p.sem(ctx, s)

    S_SUM, S_SS, S_MU, S_VAR, S_R, S_NMR, S_M1, S_M2 = 0, 8, 16, 24, 32, 40, 48, 56
    S_ACC = 64
    S_GAM = 96
    S_BET = 112
    S_DEN = 128
    S_MISC = 144

    ns = dict(X=X, LT=LT, QT=QT, AR=AR, BR=BR, STT=STT, RW=RW, CB=CB, T8=T8,
              MSK=MSK, IDB=IDB, MEC=MEC, GWT=GWT, cc_g_in=cc_g_in,
              cc_g_out=cc_g_out, lnscr=lnscr, scr2=scr2, lgscr=lgscr)

    x_toks = [p.dma("sync", X[:, t * D:(t + 1) * D],
                    I["x"][t * 128:(t + 1) * 128, :]) for t in range(NT)]
    t_ones = p.emit("vector", lambda e: e.memset(ONEC[:, :], 1.0), inc="sv")
    t_onesb = p.emit("vector", lambda e: e.memset(ONEB[:, :], 1.0), inc="sv")
    t_negb = p.emit("vector", lambda e: e.memset(NEGB[:, :], -1e30), inc="sv")
    t_id0 = p.emit("vector", lambda e: e.memset(IDB[:, :], 1.0), inc="sv")
    t_idb = p.emit("gpsimd", lambda e: e.affine_select(
        IDB[:, :], IDB[:, :], pattern=[[1, 128]], compare_op=ALU.is_equal,
        fill=0.0, base=0, channel_multiplier=-1), deps=[t_id0], inc="sq")
    t_if0 = p.emit("vector", lambda e: e.memset(IDF[:, :], 1.0), inc="sv")
    t_idf = p.emit("gpsimd", lambda e: e.affine_select(
        IDF[:, :], IDF[:, :], pattern=[[1, 128]], compare_op=ALU.is_equal,
        fill=0.0, base=0, channel_multiplier=-1), deps=[t_if0], inc="sq")
    t_coef = p.dma("sync", CB[0:8, 0:1], I["coef"][:, :])
    t_c0 = p.emit("vector", lambda e: e.memset(CONSTS[:, 0:1], LN_EPS), inc="sv")
    t_c1 = p.emit("vector", lambda e: e.memset(CONSTS[:, 1:2], RMS_EPS),
                  inc="sv")

    t_wq = p.dma("gpsimd", wq_bf[:, :], I["L0_attn_Wq"][:, :], sem="sgw")
    t_wv = p.dma("gpsimd", wv_bf[:, :], I["L0_attn_Wv"][:, :], sem="sgw")
    t_wg = p.dma("gpsimd", wg_bf[:, :], I["L0_attn_Wg"][:, :], sem="sgw")
    t_wo0 = p.dma("gpsimd", wo0_bf[:, :], I["L0_attn_Wo"][:, :], sem="sgw")
    t_wimg = ("sgw", p.cnt["sgw"])
    t_wq = t_wv = t_wg = t_wo0 = t_wimg
    l1_imgs = {"L1_attn_Wqk": wqk1_bf, "L1_attn_Wv": wv1_bf,
               "L1_attn_Wu": wu1_bf, "L1_attn_Wo": wo1_bf}
    imgs_tok = {}

    t_tmp = p.dma("sync", RW[0:1, 128:129], I["L1_ffn_temperature"][:, :])
    t_scl = p.dma("sync", RW[0:1, 129:130], I["L1_ffn_scale"][:, :])
    f_setup_d = p.fence("sync", "sd")
    t_tm = p.emit("vector", lambda e: e.tensor_scalar_max(
        RW[0:1, 128:129], RW[0:1, 128:129], 0.1), deps=[f_setup_d], inc="sv")
    t_tr = p.emit("vector", lambda e: e.reciprocal(
        RW[0:1, 130:131], RW[0:1, 128:129]), deps=[t_tm], inc="sv")
    t_sh = p.emit("vector", lambda e: e.tensor_scalar_mul(
        RW[0:1, 131:132], RW[0:1, 129:130], 0.5), deps=[f_setup_d, t_tr],
        inc="sv")
    t_rs = p.dma("gpsimd", rowscr[5:6, 0:4], RW[0:1, 128:132], deps=[t_tr, t_sh])
    f_rs = p.fence("gpsimd", "sg")
    t_hs = p.dma("gpsimd", CB[:, 42:43], rowscr[5:6, 3:4].to_broadcast([128, 1]),
                 deps=[f_rs], allow_slow_non_contiguous=True)
    t_it = p.dma("gpsimd", CB[:, 43:44], rowscr[5:6, 2:3].to_broadcast([128, 1]),
                 deps=[f_rs], allow_slow_non_contiguous=True)
    f_setup_g = p.fence("gpsimd", "sg")
    t_hs = t_it = f_setup_g
    t_coef = f_setup_d

    state = {"sum_tok": [None] * NT, "x_tok": [f_setup_d] * NT,
             "lt_free": [], "qt_free": [], "lnscr_free": [], "scr2_free": [],
             "stream_guard": None}

    def ln_prologue(rms=False, f32_consumer=None, post_mul=None,
                    post_mul_tok=None):
        sum_toks, ss_toks = [], []
        for t in range(NT):
            xs = X[:, t * D:(t + 1) * D]
            if state["sum_tok"][t] is None:
                sum_toks.append(p.emit(
                    "vector", lambda e, xs=xs, t=t: e.reduce_sum(
                        STT[:, S_SUM + t:S_SUM + t + 1], xs, axis=AX),
                    deps=[state["x_tok"][t]], inc="sv"))
            else:
                sum_toks.append(state["sum_tok"][t])
            sqsl = 4096 if (t % 2 == 0) else 6144
            sq_t = p.emit(
                "vector", lambda e, xs=xs, sqsl=sqsl: e.tensor_mul(
                    BR[:, sqsl:sqsl + D], xs, xs),
                deps=[state["x_tok"][t], ss_toks[-2] if len(ss_toks) >= 2
                      else None], inc="sv")
            ss_toks.append(p.emit(
                "vector", lambda e, t=t, sqsl=sqsl: e.reduce_sum(
                    STT[:, S_SS + t:S_SS + t + 1], BR[:, sqsl:sqsl + D],
                    axis=AX),
                deps=[sq_t], inc="sv"))
        mu_t = p.emit("vector", lambda e: e.tensor_scalar_mul(
            STT[:, S_MU:S_MU + 8], STT[:, S_SUM:S_SUM + 8], 1.0 / D),
            deps=sum_toks, inc="sv")
        ex2_t = p.emit("vector", lambda e: e.tensor_scalar_mul(
            STT[:, S_VAR:S_VAR + 8], STT[:, S_SS:S_SS + 8], 1.0 / D),
            deps=ss_toks, inc="sv")
        mu2_t = p.emit("vector", lambda e: e.tensor_tensor(
            STT[:, S_M1:S_M1 + 8], STT[:, S_MU:S_MU + 8],
            STT[:, S_MU:S_MU + 8], ALU.mult), deps=[mu_t], inc="sv")
        var_t = p.emit("vector", lambda e: e.tensor_sub(
            STT[:, S_VAR:S_VAR + 8], STT[:, S_VAR:S_VAR + 8],
            STT[:, S_M1:S_M1 + 8]), deps=[ex2_t, mu2_t], inc="sv")
        lnv_t = p.emit("scalar", lambda e: e.activation(
            out=STT[:, S_M1:S_M1 + 8], in_=STT[:, S_VAR:S_VAR + 8],
            func=AF.Ln, bias=LN_EPS), deps=[var_t, t_c0], inc="ss")
        r_t = p.emit("scalar", lambda e: e.activation(
            out=STT[:, S_R:S_R + 8], in_=STT[:, S_M1:S_M1 + 8],
            func=AF.Exp, scale=-0.5), deps=[lnv_t], inc="ss")
        if rms:
            r2 = p.emit("vector", lambda e: e.tensor_tensor(
                STT[:, S_M1:S_M1 + 8], STT[:, S_R:S_R + 8],
                STT[:, S_R:S_R + 8], ALU.mult), deps=[r_t], inc="sv")
            msq = p.emit("vector", lambda e: e.tensor_tensor(
                STT[:, S_M1:S_M1 + 8], STT[:, S_M1:S_M1 + 8],
                STT[:, S_VAR:S_VAR + 8], ALU.mult), deps=[r2], inc="sv")
            lq = p.emit("scalar", lambda e: e.activation(
                out=STT[:, S_M1:S_M1 + 8], in_=STT[:, S_M1:S_M1 + 8],
                func=AF.Ln, bias=RMS_EPS), deps=[msq, t_c1], inc="ss")
            q_t = p.emit("scalar", lambda e: e.activation(
                out=STT[:, S_M1:S_M1 + 8], in_=STT[:, S_M1:S_M1 + 8],
                func=AF.Exp, scale=-0.5), deps=[lq], inc="ss")
            r_t = p.emit("vector", lambda e: e.tensor_tensor(
                STT[:, S_R:S_R + 8], STT[:, S_R:S_R + 8],
                STT[:, S_M1:S_M1 + 8], ALU.mult), deps=[q_t, r_t], inc="sv")
        nmu_t = p.emit("vector", lambda e: e.tensor_scalar_mul(
            STT[:, S_NMR:S_NMR + 8], STT[:, S_MU:S_MU + 8], -1.0),
            deps=[mu_t], inc="sv")
        nmr_t = p.emit("vector", lambda e: e.tensor_tensor(
            STT[:, S_NMR:S_NMR + 8], STT[:, S_NMR:S_NMR + 8],
            STT[:, S_R:S_R + 8], ALU.mult), deps=[nmu_t, r_t], inc="sv")

        st_toks = []
        guard_br = [[], []]
        guard_ar = [[], []]
        lt_deps = state["lt_free"]
        for t in range(NT):
            xs = X[:, t * D:(t + 1) * D]
            sl = t % 2
            need_f32 = (f32_consumer is not None) or (post_mul is not None)
            if need_f32:
                dst = BR[:, sl * D:(sl + 1) * D]
                aff = p.emit("scalar", lambda e, xs=xs, t=t, dst=dst:
                             e.activation(out=dst, in_=xs, func=AF.Identity,
                                          bias=STT[:, S_NMR + t:S_NMR + t + 1],
                                          scale=STT[:, S_R + t:S_R + t + 1]),
                             deps=[nmr_t] + guard_br[sl], inc="ss")
                cons = None
                if f32_consumer is not None:
                    cons = f32_consumer(t, dst, aff)
                if post_mul is not None:
                    cast = p.emit("vector", lambda e, dst=dst, sl=sl:
                                  e.tensor_tensor(AR[:, sl * D:(sl + 1) * D],
                                                  dst, post_mul, ALU.mult),
                                  deps=[aff, post_mul_tok] + guard_ar[sl],
                                  inc="sv")
                else:
                    cast = p.emit("vector", lambda e, dst=dst, sl=sl:
                                  e.tensor_copy(AR[:, sl * D:(sl + 1) * D],
                                                dst),
                                  deps=[aff] + guard_ar[sl], inc="sv")
                guard_br[sl] = [x for x in (cons, cast) if x is not None]
            else:
                cast = p.emit("scalar", lambda e, xs=xs, t=t, sl=sl:
                              e.activation(out=AR[:, sl * D:(sl + 1) * D],
                                           in_=xs, func=AF.Identity,
                                           bias=STT[:, S_NMR + t:S_NMR + t + 1],
                                           scale=STT[:, S_R + t:S_R + t + 1]),
                              deps=[nmr_t] + guard_ar[sl], inc="ss")
            stk = p.dma("gpsimd", lnscr[t * 128:(t + 1) * 128, :],
                        AR[:, sl * D:(sl + 1) * D],
                        deps=[cast] + state["lnscr_free"])
            guard_ar[sl] = [stk]
            st_toks.append(stk)
        f_store = p.fence("gpsimd", "sg")
        for dc in range(DCH):
            p.dmat("sync", LT[:, dc * TPC:(dc + 1) * TPC],
                   lnscr[:, dc * 128:(dc + 1) * 128],
                   deps=[f_store] + lt_deps)
        f_lt = p.fence("sync", "sd")
        lt_toks = [f_lt] * DCH
        state["lnscr_free"] = [f_lt]
        return lt_toks

    def finalize_sums(tile, toks, nparts):
        tk = p.emit("vector", lambda e, tile=tile, nparts=nparts: e.reduce_sum(
            STT[:, S_SUM + tile:S_SUM + tile + 1],
            STT[:, S_ACC + tile * 4:S_ACC + tile * 4 + nparts], axis=AX),
            deps=toks, inc="sv")
        state["sum_tok"][tile] = tk
        state["x_tok"][tile] = tk

    def emit_l0ffn(call):
        import os as _os
        _part = int(_os.environ.get("L0FFN_PART", "9"))
        gate_mm = {}

        def gate_consumer(t, dst, aff_tok):
            last = None
            for seg in range(4):
                last = p.emit("tensor", lambda e, dst=dst, seg=seg, t=t:
                              e.matmul(PB[4 + seg][0:1, :], ONEC[:, :],
                                       dst[:, seg * 512:(seg + 1) * 512],
                                       start=(t == 0), stop=(t == NT - 1)),
                              deps=[aff_tok, t_ones], inc="st")
            gate_mm[t] = last
            return last

        lt_toks = ln_prologue(rms=False, f32_consumer=gate_consumer
                              if _part >= 2 else None)
        if _part < 3:
            return
        ev = [p.emit("vector", lambda e, seg=seg: e.tensor_copy(
            BR[0:1, 6144 + seg * 512:6144 + (seg + 1) * 512],
            PB[4 + seg][0:1, :]),
            deps=[gate_mm[NT - 1]], inc="sv") for seg in range(4)]
        t_in = p.dma("gpsimd", cc_g_in[:, :], BR[0:1, 6144:6144 + D], deps=ev)
        t_ar = p.emit("gpsimd", lambda e: e.collective_compute(
            "AllReduce", ALU.add, replica_groups=[list(range(NCORES))],
            ins=[cc_g_in[:, :]], outs=[cc_g_out[:, :]]),
            deps=[t_in], inc="scc")
        t_mc = p.dma("gpsimd", MEC[:, :],
                     cc_g_out[0, :].rearrange("(c p) -> p c", p=128),
                     deps=[t_ar], allow_slow_non_contiguous=True)
        t_mc = p.fence("gpsimd", "sg")
        for ex in range(6):
            p.dma("sync", GWT[:, ex * 16:(ex + 1) * 16],
                  I["L0_ffn_gate_W"][ex, :].rearrange("(c p) -> p c", p=128),
                  allow_slow_non_contiguous=True)
        t_gw = [p.fence("sync", "sd")]
        glast = None
        for dc in range(DCH):
            glast = p.emit("tensor", lambda e, dc=dc: e.matmul(
                PB[0][0:1, 0:6], MEC[:, dc:dc + 1],
                GWT[:, dc:dc + 81:16],
                start=(dc == 0), stop=(dc == DCH - 1)),
                deps=[t_mc] + t_gw, inc="st")
        t_l = p.emit("vector", lambda e: e.tensor_scalar_mul(
            RW[0:1, 0:6], PB[0][0:1, 0:6], 1.0 / 8192.0),
            deps=[glast], inc="sv")
        t_mx1 = p.emit("vector", lambda e: e.reduce_max(
            RW[0:1, 8:9], RW[0:1, 0:6], axis=AX), deps=[t_l], inc="sv")
        t_o1 = p.emit("vector", lambda e: e.tensor_single_scalar(
            RW[0:1, 10:16], RW[0:1, 0:6], RW[0:1, 8:9], ALU.is_ge),
            deps=[t_mx1], inc="sv")
        t_msk = p.emit("vector", lambda e: e.scalar_tensor_tensor(
            RW[0:1, 16:22], RW[0:1, 10:16], -1e30, RW[0:1, 0:6],
            ALU.mult, ALU.add), deps=[t_o1], inc="sv")
        t_mx2 = p.emit("vector", lambda e: e.reduce_max(
            RW[0:1, 9:10], RW[0:1, 16:22], axis=AX), deps=[t_msk], inc="sv")
        t_o2 = p.emit("vector", lambda e: e.tensor_single_scalar(
            RW[0:1, 24:30], RW[0:1, 16:22], RW[0:1, 9:10], ALU.is_ge),
            deps=[t_mx2], inc="sv")
        t_io = p.emit("gpsimd", lambda e: e.iota(
            RW[0:1, 56:62], pattern=[[1, 6]], base=0, channel_multiplier=0,
            allow_small_or_imprecise_dtypes=True), inc="sq")
        t_i1 = p.emit("vector", lambda e: e.tensor_mul(
            RW[0:1, 34:40], RW[0:1, 10:16], RW[0:1, 56:62]),
            deps=[t_o1, t_io], inc="sv")
        t_i1r = p.emit("vector", lambda e: e.reduce_sum(
            RW[0:1, 44:45], RW[0:1, 34:40], axis=AX), deps=[t_i1], inc="sv")
        t_i2 = p.emit("vector", lambda e: e.tensor_mul(
            RW[0:1, 34:40], RW[0:1, 24:30], RW[0:1, 56:62]),
            deps=[t_o2, t_i1r], inc="sv")
        t_i2r = p.emit("vector", lambda e: e.reduce_sum(
            RW[0:1, 45:46], RW[0:1, 34:40], axis=AX), deps=[t_i2], inc="sv")
        t_ic = p.emit("vector", lambda e: e.tensor_copy(
            ICOL[0:1, 0:2], RW[0:1, 44:46]), deps=[t_i2r], inc="sv")
        t_d = p.emit("vector", lambda e: e.tensor_sub(
            RW[0:1, 30:31], RW[0:1, 9:10], RW[0:1, 8:9]),
            deps=[t_mx2], inc="sv")
        t_e = p.emit("scalar", lambda e: e.activation(
            out=RW[0:1, 31:32], in_=RW[0:1, 30:31], func=AF.Exp),
            deps=[t_d], inc="ss")
        t_e1 = p.emit("vector", lambda e: e.tensor_scalar_add(
            RW[0:1, 31:32], RW[0:1, 31:32], 1.0), deps=[t_e], inc="sv")
        t_wa = p.emit("vector", lambda e: e.reciprocal(
            RW[0:1, 32:33], RW[0:1, 31:32]), deps=[t_e1], inc="sv")
        t_wb = p.emit("vector", lambda e: e.tensor_scalar(
            RW[0:1, 33:34], RW[0:1, 32:33], -1.0, 1.0, ALU.mult, ALU.add),
            deps=[t_wa], inc="sv")
        cell = {}
        t_rv = p.emit("gpsimd", lambda e: (
            cell.__setitem__("rA", e.value_load(ICOL[0:1, 0:1])),
            cell.__setitem__("rB", e.value_load(ICOL[0:1, 1:2])),
            e.engine_nop())[-1], deps=[t_ic], inc="sq")
        t_b2a = p.dma("gpsimd", BR[0:1, 4096:4096 + D],
                      lambda: I["L0_ffn_b2"][cell["rA"], :], deps=[t_rv])
        t_b2b = p.dma("gpsimd", BR[0:1, 6144:6144 + D],
                      lambda: I["L0_ffn_b2"][cell["rB"], :], deps=[t_rv])
        fb2 = p.fence("gpsimd", "sg")
        t_b2a = t_b2b = fb2
        t_c1 = p.emit("vector", lambda e: e.tensor_single_scalar(
            BR[0:1, 4096:4096 + D], BR[0:1, 4096:4096 + D],
            RW[0:1, 32:33], ALU.mult), deps=[t_b2a, t_wa], inc="sv")
        t_c2 = p.emit("vector", lambda e: e.tensor_single_scalar(
            BR[0:1, 6144:6144 + D], BR[0:1, 6144:6144 + D],
            RW[0:1, 33:34], ALU.mult), deps=[t_b2b, t_wb], inc="sv")
        t_c3 = p.emit("vector", lambda e: e.tensor_add(
            BR[0:1, 4096:4096 + D], BR[0:1, 4096:4096 + D],
            BR[0:1, 6144:6144 + D]), deps=[t_c1, t_c2], inc="sv")
        t_b2s = p.dma("gpsimd", rowscr[1:2, 0:D], BR[0:1, 4096:4096 + D],
                      deps=[t_c3])
        t_ws = p.dma("gpsimd", rowscr[2:3, 0:2], RW[0:1, 32:34],
                     deps=[t_wa, t_wb])
        fst = p.fence("gpsimd", "sg")
        t_b2r = p.dma("gpsimd", AR[0:1, 6656:6656 + D], rowscr[1:2, 0:D],
                      deps=[fst] + state["qt_free"])
        t_b1a = p.dma("gpsimd", CB[:, 16:16 + FCH],
                      lambda: I["L0_ffn_b1"][cell["rA"], :]
                      .rearrange("(c p) -> p c", p=128), deps=[t_rv],
                      allow_slow_non_contiguous=True)
        t_b1b = p.dma("gpsimd", CB[:, 27:27 + FCH],
                      lambda: I["L0_ffn_b1"][cell["rB"], :]
                      .rearrange("(c p) -> p c", p=128), deps=[t_rv],
                      allow_slow_non_contiguous=True)
        t_wac = p.dma("gpsimd", CB[:, 40:41],
                      rowscr[2:3, 0:1].to_broadcast([128, 1]), deps=[fst],
                      allow_slow_non_contiguous=True)
        t_wbc = p.dma("gpsimd", CB[:, 41:42],
                      rowscr[2:3, 1:2].to_broadcast([128, 1]), deps=[fst],
                      allow_slow_non_contiguous=True)
        fsm = p.fence("gpsimd", "sg")
        t_b2r = t_b1a = t_b1b = t_wac = t_wbc = fsm
        w1c_toks = []
        for (r0, r1) in ((0, 768), (768, DFF)):
            for si, rn in ((0, "rA"), (1, "rB")):
                p.dma("gpsimd", w1sel[call][si, r0:r1, :],
                      (lambda rn=rn, r0=r0, r1=r1:
                       I["L0_ffn_W1T"][cell[rn], r0:r1, :]),
                      deps=[t_rv, state.get("stream_guard")], sem="sgc")
            w1c_toks.append(p.fence("gpsimd", "sgc"))
        for si, rn in ((0, "rA"), (1, "rB")):
            p.dma("gpsimd", w2sel[call][si, :, :],
                  (lambda rn=rn: I["L0_ffn_W2T"][cell[rn], :, :]),
                  deps=[t_rv], sem="sgc")
        f_w2c = p.fence("gpsimd", "sgc")
        if _part < 4:
            return
        grps = [(0, 6), (6, 11)]
        if _os.environ.get("L0FFN_LIM"):
            _lim = _os.environ["L0FFN_LIM"]
            grps = grps[:1] if "g" in _lim else grps
        res_toks = {t: [] for t in range(NT)}
        ht_free = {}
        qt_guard = state["qt_free"]
        last_s1 = None
        pb_guard = {0: t_l, 4: ev[0], 5: ev[1], 6: ev[2], 7: ev[3]}
        w1_guard = {}
        w2_prev = {}
        _tts = range(1) if _os.environ.get("L0FFN_LIM", "").count("t") else range(2)
        _lv = _os.environ.get("L0FFN_LIM", "")
        _ss = (0,) if "s" in _lv else ((1,) if "B" in _lv else (0, 1))
        for tt in _tts:
            h_done = {}
            for s in _ss:
                rn = "rA" if s == 0 else "rB"
                for (g0, g1) in grps:
                    nfc = g1 - g0
                    mm_last = {}
                    w1_t = {}
                    for dc in range(DCH):
                        bslot = dc % 2
                        buf = 4096 + bslot * 768
                        w1_t[dc] = p.dmat(
                            "sync", AR[:, buf:buf + nfc * 128],
                            w1sel[call][s, g0 * 128:g1 * 128,
                                        dc * 128:(dc + 1) * 128],
                            deps=[w1c_toks[0 if g0 == 0 else 1],
                                  w1_guard.get(bslot),
                                  state.get("stream_guard")],
                            sem="wp%d" % bslot)
                        last = None
                        for fc in range(g0, g1):
                            last = p.emit(
                                "tensor",
                                lambda e, dc=dc, fc=fc, g0=g0, buf=buf, tt=tt:
                                e.matmul(
                                    PB[fc - g0][:, :],
                                    AR[:, buf + (fc - g0) * 128:
                                       buf + (fc - g0) * 128 + 128],
                                    LT[:, dc * TPC + tt * 512:
                                       dc * TPC + tt * 512 + 512],
                                    start=(dc == 0), stop=(dc == DCH - 1)),
                                deps=[w1_t[dc], lt_toks[dc],
                                      ht_free.get((s, fc)),
                                      pb_guard.get(fc - g0) if dc == 0
                                      else None], inc="st")
                        mm_last[dc] = last
                        w1_guard[bslot] = last
                        last_s1 = last
                    for fc in range(g0, g1):
                        bcol = CB[:, (16 if s == 0 else 27) + fc:
                                  (17 if s == 0 else 28) + fc]
                        ge = p.emit(
                            "scalar",
                            lambda e, fc=fc, g0=g0, s=s, bcol=bcol:
                            e.activation(
                                out=QT[:, s * 5632 + fc * 512:
                                       s * 5632 + fc * 512 + 512],
                                in_=PB[fc - g0][:, :], func=AF.Gelu,
                                bias=bcol),
                            deps=[mm_last[DCH - 1], t_b1a, t_b1b] + qt_guard,
                            inc="ss")
                        pb_guard[fc - g0] = ge
                        wm = p.emit(
                            "vector",
                            lambda e, fc=fc, s=s: e.tensor_single_scalar(
                                QT[:, s * 5632 + fc * 512:
                                   s * 5632 + fc * 512 + 512],
                                QT[:, s * 5632 + fc * 512:
                                   s * 5632 + fc * 512 + 512],
                                CB[:, 40 + s:41 + s], ALU.mult),
                            deps=[ge, t_wac, t_wbc], inc="sv")
                        h_done[(s, fc)] = wm
            for d2s in range(4 if _part >= 5 else 0):
                last_by_tg = {}
                for s in (0, 1):
                    rn = "rA" if s == 0 else "rB"
                    for fc in range(FCH):
                        bslot = (s * FCH + fc) % 2
                        buf2 = 5632 + bslot * 512
                        w2_t = p.dmat(
                            "sync", AR[:, buf2:buf2 + 512],
                            w2sel[call][s, d2s * 512:(d2s + 1) * 512,
                                        fc * 128:(fc + 1) * 128],
                            deps=[f_w2c, w2_prev.get(bslot)] + ev,
                            sem="wp%d" % (2 + bslot))
                        for tg in range(4):
                            tk = p.emit(
                                "tensor",
                                lambda e, s=s, fc=fc, tg=tg, buf2=buf2:
                                e.matmul(
                                    PB[4 + tg][:, :],
                                    QT[:, s * 5632 + fc * 512 + tg * 128:
                                       s * 5632 + fc * 512 + tg * 128 + 128],
                                    AR[:, buf2:buf2 + 512],
                                    start=(s == 0 and fc == 0), stop=False),
                                deps=[w2_t, h_done[(s, fc)],
                                      pb_guard.get(4 + tg)
                                      if (s == 0 and fc == 0) else None],
                                inc="st")
                            last_by_tg[tg] = tk
                        w2_prev[bslot] = last_by_tg[3]
                        if d2s == 3 and tt == 1:
                            pass
                        if d2s == 3:
                            ht_free[(s, fc)] = last_by_tg[3]
                for tg in range(4):
                    bt = p.emit(
                        "tensor", lambda e, tg=tg, d2s=d2s: e.matmul(
                            PB[4 + tg][:, :], ONEB[:, :],
                            AR[0:1, 6656 + d2s * 512:6656 + (d2s + 1) * 512],
                            start=False, stop=True),
                        deps=[last_by_tg[tg], t_b2r, t_onesb], inc="st")
                    tile = tt * 4 + tg
                    xs = X[:, tile * D + d2s * 512: tile * D + (d2s + 1) * 512]
                    rv = p.emit(
                        "vector", lambda e, tg=tg, xs=xs, tile=tile, d2s=d2s:
                        e.scalar_tensor_tensor(
                            xs, PB[4 + tg][:, :], 0.5, xs, ALU.mult, ALU.add,
                            accum_out=STT[:, S_ACC + tile * 4 + d2s:
                                          S_ACC + tile * 4 + d2s + 1]),
                        deps=[bt], inc="sv")
                    pb_guard[4 + tg] = rv
                    res_toks[tile].append(rv)
        if _part >= 5:
            for t in range(NT):
                finalize_sums(t, res_toks[t], 4)
        state["lt_free"] = [last_s1]
        state["qt_free"] = [v for v in ht_free.values()]
        state["stream_guard"] = ("st", p.cnt["st"])

    def emit_l0attn():
        p.dma("gpsimd", wc_bf[:, :], I["L1_ffn_Wc"][:, :], sem="sgw2")
        p.dma("gpsimd", wd_bf[:, :], I["L1_ffn_Wd"][:, :], sem="sgw2")
        p.dma("gpsimd", wu_bf[:, :], I["L1_ffn_Wu"][:, :], sem="sgw2")
        imgs_tok["wcdu"] = ("sgw2", p.cnt["sgw2"])
        lt_toks = ln_prologue(rms=False)
        t_gm = p.dma("sync", STT[:, S_GAM:S_GAM + 16],
                     I["L0_attn_ln_gamma"][0, :].rearrange("(c p) -> p c", p=128),
                     allow_slow_non_contiguous=True)
        t_bt = p.dma("sync", STT[:, S_BET:S_BET + 16],
                     I["L0_attn_ln_beta"][0, :].rearrange("(c p) -> p c", p=128),
                     allow_slow_non_contiguous=True)
        t_be = p.dma("sync", RW[0:1, 112:112 + NHEAD], I["L0_attn_beta"][:, :])
        f_sm = p.fence("sync", "sd")
        t_gm = t_bt = t_be = f_sm
        t_p2 = p.emit("scalar", lambda e: e.activation(
            out=RW[0:1, 96:96 + NHEAD], in_=RW[0:1, 112:112 + NHEAD],
            func=AF.Exp, scale=-LN2), deps=[t_be], inc="ss")
        t_lam = p.emit("vector", lambda e: e.tensor_scalar(
            RW[0:1, 64:64 + NHEAD], RW[0:1, 96:96 + NHEAD], -1.0, 1.0,
            ALU.mult, ALU.add), deps=[t_p2], inc="sv")
        t_cl1 = p.emit("vector", lambda e: e.tensor_scalar_min(
            RW[0:1, 64:64 + NHEAD], RW[0:1, 64:64 + NHEAD], 1.0 - 1e-9),
            deps=[t_lam], inc="sv")
        t_cl2 = p.emit("vector", lambda e: e.tensor_scalar_max(
            RW[0:1, 64:64 + NHEAD], RW[0:1, 64:64 + NHEAD], F32_TINY),
            deps=[t_cl1], inc="sv")
        t_ll = p.emit("scalar", lambda e: e.activation(
            out=RW[0:1, 80:80 + NHEAD], in_=RW[0:1, 64:64 + NHEAD],
            func=AF.Ln), deps=[t_cl2], inc="ss")
        t_ls = p.dma("gpsimd", rowscr[3:4, 0:NHEAD], RW[0:1, 64:64 + NHEAD],
                     deps=[t_cl2])
        f_ls = p.fence("gpsimd", "sg")
        t_lb = p.dma("gpsimd", CB[:, 8:8 + NHEAD],
                     rowscr[3:4, 0:NHEAD].to_broadcast([128, NHEAD]),
                     deps=[f_ls], allow_slow_non_contiguous=True)
        t_lb = p.fence("gpsimd", "sg")

        vt_toks = {}
        qt_guard = state["qt_free"]
        sbufg = {0: None, 1: None, 2: None}
        sq_state = {"j": 0}

        def run_strip_phase(img, imgtok, compute_set, nsets=4, entry=()):
            loaded = {}

            def load(j, extra=()):
                buf = sq_state["j"] % 3
                sq_state["j"] += 1
                base = buf * 4096
                os_, h = divmod(j, 2)
                for k in range(8):
                    dc = h * 8 + k
                    p.dmat("sync",
                           AR[:, base + k * 512: base + (k + 1) * 512],
                           img[os_ * 512:(os_ + 1) * 512,
                               dc * 128:(dc + 1) * 128],
                           deps=[imgtok, sbufg[buf]] + list(extra))
                loaded[j] = (buf, base, p.fence("sync", "sd"))
            for j in range(min(3, nsets * 2)):
                load(j, extra=entry)
            nxt = 3
            for os_ in range(nsets):
                b0, base0, f0 = loaded[2 * os_]
                b1, base1, f1 = loaded[2 * os_ + 1]

                def chunk(dc, base0=base0, base1=base1, f0=f0, f1=f1):
                    if dc < 8:
                        return base0 + dc * 512, f0
                    return base1 + (dc - 8) * 512, f1
                lastmm = compute_set(os_, chunk)
                sbufg[b0] = lastmm
                sbufg[b1] = lastmm
                while nxt < nsets * 2 and nxt <= 2 * os_ + 4:
                    load(nxt)
                    nxt += 1

        def vt_set(os_, chunk):
            last = None
            for oc in range(os_ * 4, os_ * 4 + 4):
                for T in range(2):
                    pb = PB[(oc * 2 + T) % 4]
                    for dc in range(DCH):
                        col, ftok = chunk(dc)
                        last = p.emit(
                            "tensor",
                            lambda e, col=col, oc=oc, T=T, dc=dc, os_=os_,
                            pb=pb:
                            e.matmul(pb[:, :],
                                     AR[:, col + (oc - os_ * 4) * 128:
                                        col + (oc - os_ * 4) * 128 + 128],
                                     LT[:, dc * TPC + T * 512:
                                        dc * TPC + T * 512 + 512],
                                     start=(dc == 0), stop=(dc == DCH - 1)),
                            deps=[ftok, lt_toks[dc],
                                  vt_toks.get((oc - 2, T))], inc="st")
                    ev = p.emit("vector", lambda e, oc=oc, T=T, pb=pb:
                                e.tensor_copy(QT[:, oc * TPC + T * 512:
                                                 oc * TPC + T * 512 + 512],
                                              pb[:, :]),
                                deps=[last] + qt_guard, inc="sv")
                    vt_toks[(oc, T)] = ev
            return last

        run_strip_phase(wv_bf, t_wv, vt_set, entry=qt_guard)
        p1_toks = []
        for oc in range(NHEAD):
            prev = None
            for T in range(2):
                sl = T % 2
                dst = BR[:, 8192 + sl * 512:8192 + (sl + 1) * 512]
                ini = 0.0 if T == 0 else BR[:, 8192 + (1 - sl) * 512 + 511:
                                            8192 + (1 - sl) * 512 + 512]
                prev = p.emit(
                    "vector", lambda e, oc=oc, T=T, dst=dst, ini=ini:
                    e.tensor_tensor_scan(
                        out=dst,
                        data0=CB[:, 8 + oc:9 + oc].to_broadcast([128, 512]),
                        data1=QT[:, oc * TPC + T * 512:oc * TPC + T * 512 + 512],
                        initial=ini, op0=ALU.mult, op1=ALU.add),
                    deps=[vt_toks[(oc, T)], t_lb, prev], inc="sv")
            cp = p.emit("vector", lambda e, oc=oc: e.tensor_copy(
                CB[:, 24 + oc:25 + oc], BR[:, 8192 + 512 + 511:8192 + 1024]),
                deps=[prev], inc="sv")
            p1_toks.append(cp)

        t_ssav = p.dma("gpsimd", cc_s_in[:, :], CB[:, 24:24 + NHEAD],
                       deps=p1_toks)
        f_sv = p.fence("gpsimd", "sg")
        t_ag = p.emit("gpsimd", lambda e: e.collective_compute(
            "AllGather", ALU.bypass, replica_groups=[list(range(NCORES))],
            ins=[cc_s_in[:, :]], outs=[cc_s_out[:, :]]),
            deps=[f_sv], inc="scc")
        t_csb = p.dma("sync",
                      BR[:, 9216:9216 + 128].rearrange("p (j h) -> p j h", j=8),
                      cc_s_out.rearrange("(j p) h -> p j h", p=128),
                      deps=[t_ag], allow_slow_non_contiguous=True)
        t_csb = p.fence("sync", "sd")
        t_llb = p.dma("gpsimd", rowscr[3:4, 16:16 + NHEAD], RW[0:1, 80:80 + NHEAD],
                      deps=[t_ll])
        f_llb = p.fence("gpsimd", "sg")
        t_ll8 = p.dma("gpsimd", RW[0:8, 144:144 + NHEAD],
                      rowscr[3:4, 16:16 + NHEAD].to_broadcast([8, NHEAD]),
                      deps=[f_llb], allow_slow_non_contiguous=True)
        t_ll8 = p.fence("gpsimd", "sg")
        t_wm = p.emit("vector", lambda e: e.tensor_single_scalar(
            RW[0:8, 144:144 + NHEAD], RW[0:8, 144:144 + NHEAD],
            CB[0:8, 0:1], ALU.mult), deps=[t_ll8, t_coef], inc="sv")
        t_wme = p.emit("scalar", lambda e: e.activation(
            out=RW[0:8, 144:144 + NHEAD], in_=RW[0:8, 144:144 + NHEAD],
            func=AF.Exp), deps=[t_wm], inc="ss")
        t_wms = p.dma("gpsimd", rowscr[4:5, 0:128], RW[0:8, 144:144 + NHEAD],
                      deps=[t_wme])
        f_wms = p.fence("gpsimd", "sg")
        t_wmb = p.dma("gpsimd", BR[:, 9344:9344 + 128],
                      rowscr[4:5, 0:128].to_broadcast([128, 128]),
                      deps=[f_wms], allow_slow_non_contiguous=True)
        t_wmb = p.fence("gpsimd", "sg")
        t_pr = p.emit("vector", lambda e: e.tensor_mul(
            BR[:, 9216:9216 + 128], BR[:, 9216:9216 + 128],
            BR[:, 9344:9344 + 128]), deps=[t_csb, t_wmb], inc="sv")
        t_car = p.emit("vector", lambda e: e.tensor_reduce(
            CB[:, 44:44 + NHEAD],
            BR[:, 9216:9216 + 128].rearrange("p (j h) -> p h j", j=8),
            axis=AX, op=ALU.add), deps=[t_pr], inc="sv")

        pt_toks = {}
        ret_end_prev = {}
        for T in range(2):
            s_toks = {}
            ret_end = {}
            sum_last = None
            sq_mm_last = None
            sq_act_last = None
            qg_state = {"sum": sum_last, "sqa": sq_act_last,
                        "sqm": sq_mm_last}

            def q_set(os_, chunk, T=T, s_toks=s_toks, ret_end=ret_end,
                      ret_end_prev=ret_end_prev):
                last = None
                for oc in range(os_ * 4, os_ * 4 + 4):
                    sl = oc % 2
                    dst = BR[:, 8192 + sl * 512:8192 + (sl + 1) * 512]
                    ini = (CB[:, 44 + oc:45 + oc] if T == 0
                           else CB[:, 24 + oc:25 + oc])
                    ideps = ([t_car] if T == 0 else [ret_end_prev[oc]])
                    sc = p.emit(
                        "vector", lambda e, oc=oc, T=T, dst=dst, ini=ini:
                        e.tensor_tensor_scan(
                            out=dst,
                            data0=CB[:, 8 + oc:9 + oc].to_broadcast([128, 512]),
                            data1=QT[:, oc * TPC + T * 512:
                                     oc * TPC + T * 512 + 512],
                            initial=ini, op0=ALU.mult, op1=ALU.add),
                        deps=ideps + [vt_toks[(oc, T)],
                                      s_toks.get(oc - 2)], inc="sv")
                    sv = p.emit("vector", lambda e, oc=oc, dst=dst:
                                e.tensor_copy(CB[:, 24 + oc:25 + oc],
                                              dst[:, 511:512]),
                                deps=[sc], inc="sv")
                    ret_end[oc] = sv
                    pb = PB[4 + (oc % 2)]
                    for dc in range(DCH):
                        col, ftok = chunk(dc)
                        last = p.emit(
                            "tensor",
                            lambda e, col=col, oc=oc, T=T, dc=dc, os_=os_,
                            pb=pb:
                            e.matmul(pb[:, :],
                                     AR[:, col + (oc - os_ * 4) * 128:
                                        col + (oc - os_ * 4) * 128 + 128],
                                     LT[:, dc * TPC + T * 512:
                                        dc * TPC + T * 512 + 512],
                                     start=(dc == 0), stop=(dc == DCH - 1)),
                            deps=[ftok, lt_toks[dc], s_toks.get(oc - 2),
                                  pt_toks.get((NHEAD - 2 + (oc % 2), T - 1))],
                            inc="st")
                    stk = p.emit("vector", lambda e, oc=oc, pb=pb, dst=dst:
                                 e.tensor_mul(BR[:, oc * 512:(oc + 1) * 512],
                                              pb[:, :], dst),
                                 deps=[last, sc], inc="sv")
                    s_toks[oc] = stk
                    qg_state["sum"] = p.emit(
                        "tensor", lambda e, oc=oc: e.matmul(
                            PB[6][0:1, :], ONEC[:, :],
                            BR[:, oc * 512:(oc + 1) * 512],
                            start=(oc == 0), stop=(oc == NHEAD - 1)),
                        deps=[stk, t_ones], inc="st")
                    qg_state["sqa"] = p.emit(
                        "scalar", lambda e, oc=oc: e.activation(
                            out=BR[:, 9216:9216 + 512],
                            in_=BR[:, oc * 512:(oc + 1) * 512],
                            func=AF.Square),
                        deps=[stk, qg_state["sqm"]], inc="ss")
                    qg_state["sqm"] = p.emit(
                        "tensor", lambda e, oc=oc: e.matmul(
                            PB[7][0:1, :], ONEC[:, :], BR[:, 9216:9216 + 512],
                            start=(oc == 0), stop=(oc == NHEAD - 1)),
                        deps=[qg_state["sqa"], t_ones], inc="st")
                return last

            run_strip_phase(wq_bf, t_wq, q_set)
            sum_last = qg_state["sum"]
            sq_act_last = qg_state["sqa"]
            sq_mm_last = qg_state["sqm"]
            ret_end_prev = ret_end
            t_su = p.emit("vector", lambda e: e.tensor_scalar_mul(
                RW[0:1, 256:768], PB[6][0:1, :], 1.0 / D),
                deps=[sum_last], inc="sv")
            t_s2 = p.emit("vector", lambda e: e.tensor_scalar_mul(
                RW[0:1, 768:1280], PB[7][0:1, :], 1.0 / D),
                deps=[sq_mm_last], inc="sv")
            t_m2 = p.emit("vector", lambda e: e.tensor_mul(
                RW[0:1, 1280:1792], RW[0:1, 256:768], RW[0:1, 256:768]),
                deps=[t_su], inc="sv")
            t_va = p.emit("vector", lambda e: e.tensor_sub(
                RW[0:1, 768:1280], RW[0:1, 768:1280], RW[0:1, 1280:1792]),
                deps=[t_s2, t_m2], inc="sv")
            t_lv = p.emit("scalar", lambda e: e.activation(
                out=RW[0:1, 1280:1792], in_=RW[0:1, 768:1280], func=AF.Ln,
                bias=LN_EPS), deps=[t_va, t_c0], inc="ss")
            t_rr = p.emit("scalar", lambda e: e.activation(
                out=RW[0:1, 1280:1792], in_=RW[0:1, 1280:1792], func=AF.Exp,
                scale=-0.5), deps=[t_lv], inc="ss")
            t_nm = p.emit("vector", lambda e: e.tensor_scalar_mul(
                RW[0:1, 256:768], RW[0:1, 256:768], -1.0),
                deps=[t_su, t_m2], inc="sv")
            t_nb = p.emit("vector", lambda e: e.tensor_mul(
                RW[0:1, 256:768], RW[0:1, 256:768], RW[0:1, 1280:1792]),
                deps=[t_nm, t_rr], inc="sv")
            t_ab = p.dma("gpsimd", rowscr[6:7, 0:512], RW[0:1, 1280:1792],
                         deps=[t_nb])
            t_ab2 = p.dma("gpsimd", rowscr[6:7, 512:1024], RW[0:1, 256:768],
                          deps=[t_nb])
            f_ab = p.fence("gpsimd", "sg")
            t_abb = p.dma("gpsimd", BR[:, 9728:9728 + 512],
                          rowscr[6:7, 0:512].to_broadcast([128, 512]),
                          deps=[f_ab], allow_slow_non_contiguous=True)
            t_bbb = p.dma("gpsimd", BR[:, 10240:10240 + 512],
                          rowscr[6:7, 512:1024].to_broadcast([128, 512]),
                          deps=[f_ab], allow_slow_non_contiguous=True)
            f_ab2 = p.fence("gpsimd", "sg")
            t_abb = t_bbb = f_ab2
            ln_toks = {}
            for oc in range(NHEAD):
                a1 = p.emit("vector", lambda e, oc=oc: e.tensor_mul(
                    BR[:, oc * 512:(oc + 1) * 512],
                    BR[:, oc * 512:(oc + 1) * 512], BR[:, 9728:9728 + 512]),
                    deps=[t_abb, s_toks[oc]], inc="sv")
                a2 = p.emit("vector", lambda e, oc=oc: e.tensor_add(
                    BR[:, oc * 512:(oc + 1) * 512],
                    BR[:, oc * 512:(oc + 1) * 512], BR[:, 10240:10240 + 512]),
                    deps=[t_bbb, a1], inc="sv")
                a3 = p.emit("vector", lambda e, oc=oc: e.tensor_scalar(
                    BR[:, oc * 512:(oc + 1) * 512],
                    BR[:, oc * 512:(oc + 1) * 512],
                    STT[:, S_GAM + oc:S_GAM + oc + 1],
                    STT[:, S_BET + oc:S_BET + oc + 1], ALU.mult, ALU.add),
                    deps=[a2, t_gm, t_bt], inc="sv")
                ln_toks[oc] = a3
            def g_set(os_, chunk, T=T, ln_toks=ln_toks,
                      sq_mm_last=sq_mm_last):
                last = None
                for oc in range(os_ * 4, os_ * 4 + 4):
                    pb = PB[4 + (oc % 2)]
                    for dc in range(DCH):
                        col, ftok = chunk(dc)
                        last = p.emit(
                            "tensor",
                            lambda e, col=col, oc=oc, T=T, dc=dc, os_=os_,
                            pb=pb:
                            e.matmul(pb[:, :],
                                     AR[:, col + (oc - os_ * 4) * 128:
                                        col + (oc - os_ * 4) * 128 + 128],
                                     LT[:, dc * TPC + T * 512:
                                        dc * TPC + T * 512 + 512],
                                     start=(dc == 0), stop=(dc == DCH - 1)),
                            deps=[ftok, pt_toks.get((oc - 2, T)),
                                  sq_mm_last], inc="st")
                    gsil = p.emit("scalar", lambda e, pb=pb:
                                  e.activation(out=BR[:, 9216:9216 + 512],
                                               in_=pb[:, :], func=AF.Silu),
                                  deps=[last, sq_mm_last,
                                        pt_toks.get((oc - 1, T)) if oc >= 1
                                        else pt_toks.get((NHEAD - 1, T - 1))],
                                  inc="ss")
                    ptk = p.emit(
                        "vector", lambda e, oc=oc, T=T: e.tensor_mul(
                            QT[:, oc * TPC + T * 512:oc * TPC + T * 512 + 512],
                            BR[:, oc * 512:(oc + 1) * 512],
                            BR[:, 9216:9216 + 512]),
                        deps=[gsil, ln_toks[oc]], inc="sv")
                    pt_toks[(oc, T)] = ptk
                return last

            run_strip_phase(wg_bf, t_wg, g_set)
        all_pt = [max((pt_toks[(oc, T)] for oc in range(NHEAD)
                       for T in range(2)), key=lambda tk: tk[1])]
        res_toks = {t: [] for t in range(NT)}
        fin_state = {"last": None}

        def fin_set(es, chunk):
            last = None
            for tg in range(NT):
                pb = PB[tg % 4]
                for dc in range(DCH):
                    col, ftok = chunk(dc)
                    last = p.emit(
                        "tensor", lambda e, col=col, es=es, tg=tg, dc=dc,
                        pb=pb:
                        e.matmul(pb[:, :],
                                 QT[:, dc * TPC + tg * 128:
                                    dc * TPC + tg * 128 + 128],
                                 AR[:, col:col + 512],
                                 start=(dc == 0), stop=(dc == DCH - 1)),
                        deps=[ftok,
                              res_toks[tg - 4][es] if tg >= 4 else
                              (res_toks[tg + 4][es - 1] if es > 0 else None)]
                        + all_pt, inc="st")
                xs = X[:, tg * D + es * 512: tg * D + (es + 1) * 512]
                rv = p.emit(
                    "vector", lambda e, xs=xs, tg=tg, es=es, pb=pb:
                    e.scalar_tensor_tensor(
                        xs, pb[:, :], 1.0, xs, ALU.mult, ALU.add,
                        accum_out=STT[:, S_ACC + tg * 4 + es:
                                      S_ACC + tg * 4 + es + 1]),
                    deps=[last], inc="sv")
                res_toks[tg].append(rv)
                fin_state["last"] = last
            return last

        run_strip_phase(wo0_bf, t_wo0, fin_set)
        last_fin = fin_state["last"]
        for t in range(NT):
            finalize_sums(t, res_toks[t], 4)
        state["lt_free"] = [last_fin]
        state["qt_free"] = [last_fin]
        state["stream_guard"] = ("st", p.cnt["st"])

    def emit_l1attn():
        lt_toks = ln_prologue(rms=False)
        BIA = 8192
        wimg_tok = imgs_tok["l1a"]
        wp_prev = {0: state.get("stream_guard"), 1: state.get("stream_guard")}

        def proj_phase(wname, bias_name, bias_dep, psum_guards, evac):
            t_bias = p.dma("gpsimd", AR[0:1, BIA:BIA + D], I[bias_name][:, :],
                           deps=bias_dep)
            f_bias = p.fence("gpsimd", "sg")
            ev_toks = {}
            bt_toks = {}
            for seg in range(4):
                for dc in range(DCH):
                    bslot = dc % 2
                    buf = bslot * 512
                    wtk = p.dmat("sync", AR[:, buf:buf + 512],
                                 l1_imgs[wname][seg * 512:(seg + 1) * 512,
                                                dc * 128:(dc + 1) * 128],
                                 deps=[wimg_tok, wp_prev.get(bslot)],
                                 sem="wp%d" % bslot)
                    tk = None
                    for t in range(NT):
                        guard = (psum_guards.get(t) if seg == 0
                                 else ev_toks.get((t, seg - 1)))
                        tk = p.emit(
                            "tensor",
                            lambda e, dc=dc, t=t, buf=buf:
                            e.matmul(PB[t][:, :],
                                     LT[:, dc * TPC + t * 128:
                                        dc * TPC + t * 128 + 128],
                                     AR[:, buf:buf + 512],
                                     start=(dc == 0), stop=False),
                            deps=[wtk, lt_toks[dc], guard], inc="st")
                    wp_prev[bslot] = tk
                for t in range(NT):
                    bt = p.emit(
                        "tensor", lambda e, t=t, seg=seg:
                        e.matmul(PB[t][:, :], ONEB[:, :],
                                 AR[0:1, BIA + seg * 512:
                                    BIA + (seg + 1) * 512],
                                 start=False, stop=True),
                        deps=[f_bias, t_onesb], inc="st")
                    bt_toks[(t, seg)] = bt
                    ev_toks[(t, seg)] = evac(t, seg, bt)
            return ev_toks, bt_toks

        k_store = {}
        kst = {"slot": None}

        def evac_k(t, seg, bt):
            mn = p.emit("vector", lambda e, t=t: e.tensor_scalar_min(
                BR[:, 0:512], PB[t][:, :], 0.0),
                deps=[bt, kst["slot"]], inc="sv")
            ex = p.emit("scalar", lambda e: e.activation(
                out=BR[:, 0:512], in_=BR[:, 0:512], func=AF.Exp),
                deps=[mn], inc="ss")
            mx = p.emit("vector", lambda e, t=t: e.tensor_scalar_max(
                BR[:, 512:1024], PB[t][:, :], 0.0),
                deps=[ex], inc="sv")
            kb = p.emit("vector", lambda e: e.scalar_tensor_tensor(
                AR[:, 1024:1536], BR[:, 0:512], -1.0, BR[:, 512:1024],
                ALU.add, ALU.add), deps=[mx], inc="sv")
            st = p.dma("scalar", scr2[t * 128:(t + 1) * 128,
                                      seg * 512:(seg + 1) * 512],
                       AR[:, 1024:1536], deps=[kb] + state["scr2_free"])
            state["scr2_free"] = []
            k_store[(t, seg)] = st
            kst["slot"] = st
            return mx

        kev, kbt = proj_phase("L1_attn_Wqk", "L1_attn_bqk",
                              state["qt_free"], {}, evac_k)
        f_k = p.fence("scalar", "so")

        v_store = {}
        vst = {"slot": None}

        def evac_v(t, seg, bt):
            sv_ = p.emit("scalar", lambda e, t=t: e.activation(
                out=AR[:, 1536:2048], in_=PB[t][:, :], func=AF.Silu),
                deps=[bt, vst["slot"]], inc="ss")
            st = p.dma("scalar", lnscr[t * 128:(t + 1) * 128,
                                       seg * 512:(seg + 1) * 512],
                       AR[:, 1536:2048], deps=[sv_] + state["lnscr_free"])
            state["lnscr_free"] = []
            v_store[(t, seg)] = st
            vst["slot"] = st
            return sv_

        vev, vbt = proj_phase("L1_attn_Wv", "L1_attn_bv",
                              [kbt[(NT - 1, 3)]],
                              {t: kev[(t, 3)] for t in range(NT)}, evac_v)
        f_v = p.fence("scalar", "so")

        for dc in range(DCH):
            p.dmat("sync", QT[:, dc * TPC:(dc + 1) * TPC],
                   scr2[:, dc * 128:(dc + 1) * 128],
                   deps=[f_k] + state["qt_free"])
        f_qt = p.fence("sync", "sd")
        kv_mm = {}
        khb_t = {}
        vhb_t = {}
        acc_tok = {}
        for t in range(NT):
            khb_t[t] = p.dma("sync", AR[:, 2048:2048 + D],
                             scr2[t * 128:(t + 1) * 128, :],
                             deps=[f_k, kv_mm.get((t - 1, NHEAD - 1))])
            vhb_t[t] = p.dma("sync", AR[:, 4096:4096 + D],
                             lnscr[t * 128:(t + 1) * 128, :],
                             deps=[f_v, kv_mm.get((t - 1, NHEAD - 1))])
            for h in range(NHEAD):
                kv_mm[(t, h)] = p.emit(
                    "tensor", lambda e, t=t, h=h:
                    e.matmul(PB[4 + h // 4][:, (h % 4) * 128:(h % 4) * 128 + 128],
                             AR[:, 2048 + h * 128:2048 + (h + 1) * 128],
                             AR[:, 4096 + h * 128:4096 + (h + 1) * 128],
                             start=True, stop=True),
                    deps=[khb_t[t], vhb_t[t],
                          acc_tok.get((t - 1, h // 4))], inc="st")
            for b in range(4):
                dst = BR[:, 2048 + b * 512:2048 + (b + 1) * 512]
                if t == 0:
                    acc_tok[(t, b)] = p.emit(
                        "vector", lambda e, dst=dst, b=b: e.tensor_copy(
                            dst, PB[4 + b][:, :]),
                        deps=[kv_mm[(t, 4 * b + 3)]], inc="sv")
                else:
                    acc_tok[(t, b)] = p.emit(
                        "vector", lambda e, dst=dst, b=b: e.tensor_add(
                            dst, dst, PB[4 + b][:, :]),
                        deps=[kv_mm[(t, 4 * b + 3)],
                              acc_tok[(t - 1, b)]], inc="sv")
        kv_last = kv_mm[(NT - 1, NHEAD - 1)]
        kv_ev = [acc_tok[(NT - 1, b)] for b in range(4)]
        t_kvs = p.dma("gpsimd", cc_kv_in[:, :], BR[:, 2048:2048 + D],
                      deps=kv_ev)
        f_kvs = p.fence("gpsimd", "sg")
        t_kvar = p.emit("gpsimd", lambda e: e.collective_compute(
            "AllReduce", ALU.add, replica_groups=[list(range(NCORES))],
            ins=[cc_kv_in[:, :]], outs=[cc_kv_out[:, :]]),
            deps=[f_kvs], inc="scc")
        t_kvb = p.dma("gpsimd", AR[:, 6144:6144 + D], cc_kv_out[:, :],
                      deps=[t_kvar])
        f_kvb = p.fence("gpsimd", "sg")
        u_store = {}
        ust = {"slot": None}

        def evac_u(t, seg, bt):
            su = p.emit("scalar", lambda e, t=t: e.activation(
                out=AR[:, 1536:2048], in_=PB[t][:, :], func=AF.Silu),
                deps=[bt, ust["slot"]], inc="ss")
            st = p.dma("scalar", lnscr[t * 128:(t + 1) * 128,
                                       seg * 512:(seg + 1) * 512],
                       AR[:, 1536:2048], deps=[su, kv_last])
            u_store[(t, seg)] = st
            ust["slot"] = st
            return su

        u_guards = {t: (kv_ev[t - 4] if t >= 4 else vev[(t, 3)])
                    for t in range(NT)}
        uev, ubt = proj_phase("L1_attn_Wu", "L1_attn_bu",
                              [vbt[(NT - 1, 3)]], u_guards, evac_u)
        f_u = p.fence("scalar", "so")
        t_gb = p.dma("gpsimd", BR[:, 4096:4096 + D],
                     I["L1_attn_ln_g"].to_broadcast([128, D]),
                     allow_slow_non_contiguous=True, deps=[kv_last])
        t_bb = p.dma("gpsimd", BR[:, 6144:6144 + D],
                     I["L1_attn_ln_b"].to_broadcast([128, D]),
                     allow_slow_non_contiguous=True)
        f_gbb = p.fence("gpsimd", "sg")
        prod_store = {}
        prev_ev = {}
        t_uf_prev = None
        t_pd_prev = None
        for t in range(NT):
            pb = PB[t % 2]
            seg_toks = []
            for seg in range(4):
                last = None
                for hh in range(4):
                    h = seg * 4 + hh
                    last = p.emit(
                        "tensor", lambda e, t=t, h=h, hh=hh, pb=pb:
                        e.matmul(pb[:, hh * 128:(hh + 1) * 128],
                                 QT[:, h * TPC + t * 128:h * TPC + t * 128 + 128],
                                 AR[:, 6144 + h * 128:6144 + (h + 1) * 128],
                                 start=True, stop=True),
                        deps=[f_qt, f_kvb,
                              uev[(t, 3)] if t < 2 else
                              prev_ev.get((t - 2, seg)),
                              seg_toks[seg - 1] if seg > 0 else
                              prev_ev.get((t - 2, 3))], inc="st")
                ev = p.emit("scalar", lambda e, t=t, seg=seg, pb=pb:
                            e.activation(out=BR[:, seg * 512:(seg + 1) * 512],
                                         in_=pb[:, :], func=AF.Copy),
                            deps=[last, t_pd_prev], inc="ss")
                prev_ev[(t, seg)] = ev
                seg_toks.append(ev)
            t_s1 = p.emit("vector", lambda e: e.reduce_sum(
                STT[:, S_M1:S_M1 + 1], BR[:, 0:D], axis=AX),
                deps=seg_toks, inc="sv")
            t_s2a = p.emit("vector", lambda e: e.tensor_mul(
                BR[:, 8192:8192 + D], BR[:, 0:D], BR[:, 0:D]),
                deps=seg_toks, inc="sv")
            t_s2 = p.emit("vector", lambda e: e.reduce_sum(
                STT[:, S_M1 + 1:S_M1 + 2], BR[:, 8192:8192 + D], axis=AX),
                deps=[t_s2a], inc="sv")
            t_mu = p.emit("vector", lambda e: e.tensor_scalar_mul(
                STT[:, S_M1 + 2:S_M1 + 3], STT[:, S_M1:S_M1 + 1], 1.0 / D),
                deps=[t_s1], inc="sv")
            t_e2 = p.emit("vector", lambda e: e.tensor_scalar_mul(
                STT[:, S_M1 + 3:S_M1 + 4], STT[:, S_M1 + 1:S_M1 + 2], 1.0 / D),
                deps=[t_s2], inc="sv")
            t_m2 = p.emit("vector", lambda e: e.tensor_tensor(
                STT[:, S_M1 + 4:S_M1 + 5], STT[:, S_M1 + 2:S_M1 + 3],
                STT[:, S_M1 + 2:S_M1 + 3], ALU.mult), deps=[t_mu], inc="sv")
            t_va = p.emit("vector", lambda e: e.tensor_sub(
                STT[:, S_M1 + 3:S_M1 + 4], STT[:, S_M1 + 3:S_M1 + 4],
                STT[:, S_M1 + 4:S_M1 + 5]), deps=[t_e2, t_m2], inc="sv")
            t_lv = p.emit("scalar", lambda e: e.activation(
                out=STT[:, S_M1 + 4:S_M1 + 5], in_=STT[:, S_M1 + 3:S_M1 + 4],
                func=AF.Ln, bias=LN_EPS), deps=[t_va, t_c0], inc="ss")
            t_rr = p.emit("scalar", lambda e: e.activation(
                out=STT[:, S_M1 + 5:S_M1 + 6], in_=STT[:, S_M1 + 4:S_M1 + 5],
                func=AF.Exp, scale=-0.5), deps=[t_lv], inc="ss")
            t_nm = p.emit("vector", lambda e: e.tensor_scalar_mul(
                STT[:, S_M1 + 6:S_M1 + 7], STT[:, S_M1 + 2:S_M1 + 3], -1.0),
                deps=[t_mu], inc="sv")
            t_nb = p.emit("vector", lambda e: e.tensor_tensor(
                STT[:, S_M1 + 6:S_M1 + 7], STT[:, S_M1 + 6:S_M1 + 7],
                STT[:, S_M1 + 5:S_M1 + 6], ALU.mult),
                deps=[t_nm, t_rr], inc="sv")
            t_af = p.emit("scalar", lambda e: e.activation(
                out=BR[:, 0:D], in_=BR[:, 0:D], func=AF.Identity,
                bias=STT[:, S_M1 + 6:S_M1 + 7],
                scale=STT[:, S_M1 + 5:S_M1 + 6]), deps=[t_nb, t_s2], inc="ss")
            t_g1 = p.emit("vector", lambda e: e.tensor_mul(
                BR[:, 0:D], BR[:, 0:D], BR[:, 4096:4096 + D]),
                deps=[t_af, f_gbb], inc="sv")
            t_g2 = p.emit("vector", lambda e: e.tensor_add(
                BR[:, 0:D], BR[:, 0:D], BR[:, 6144:6144 + D]),
                deps=[t_g1], inc="sv")
            t_ul = p.dma("sync", AR[:, 2048:2048 + D],
                         lnscr[t * 128:(t + 1) * 128, :],
                         deps=[f_u, kv_last, t_uf_prev])
            f_ul = p.fence("sync", "sd")
            t_uf = p.emit("vector", lambda e: e.tensor_copy(
                BR[:, 2048:2048 + D], AR[:, 2048:2048 + D]),
                deps=[f_ul], inc="sv")
            t_uf_prev = t_uf
            t_pd = p.emit("vector", lambda e: e.tensor_mul(
                AR[:, 4096:4096 + D], BR[:, 0:D], BR[:, 2048:2048 + D]),
                deps=[t_uf, t_g2, kv_last], inc="sv")
            t_pd_prev = t_pd
            pst = p.dma("scalar", scr2[t * 128:(t + 1) * 128, :],
                        AR[:, 4096:4096 + D],
                        deps=[t_pd] + ([f_qt] if t == 0 else []))
            prod_store[t] = pst
        t_bo = p.dma("gpsimd", AR[0:1, BIA:BIA + D], I["L1_attn_bo"][:, :],
                     deps=[ubt[(NT - 1, 3)]])
        f_bo = p.fence("gpsimd", "sg")
        f_ps = p.fence("scalar", "so")
        for dc in range(DCH):
            p.dmat("sync", LT[:, dc * TPC:(dc + 1) * TPC],
                   scr2[:, dc * 128:(dc + 1) * 128],
                   deps=[f_ps, ubt[(NT - 1, 3)]] + state["lt_free"])
        f_ltp = p.fence("sync", "sd")
        res_toks = {t: [] for t in range(NT)}
        for seg in range(4):
            for dc in range(DCH):
                bslot = dc % 2
                buf = bslot * 512
                wtk = p.dmat("sync", AR[:, buf:buf + 512],
                             wo1_bf[seg * 512:(seg + 1) * 512,
                                    dc * 128:(dc + 1) * 128],
                             deps=[wimg_tok, wp_prev.get(bslot)],
                             sem="wp%d" % bslot)
                tk = None
                for t in range(NT):
                    tk = p.emit(
                        "tensor",
                        lambda e, dc=dc, t=t, buf=buf:
                        e.matmul(PB[t][:, :],
                                 LT[:, dc * TPC + t * 128:
                                    dc * TPC + t * 128 + 128],
                                 AR[:, buf:buf + 512],
                                 start=(dc == 0), stop=False),
                        deps=[wtk, f_ltp,
                              res_toks[t][seg - 1] if seg > 0 else None],
                        inc="st")
                wp_prev[bslot] = tk
            for t in range(NT):
                bt = p.emit(
                    "tensor", lambda e, t=t, seg=seg: e.matmul(
                        PB[t][:, :], ONEB[:, :],
                        AR[0:1, BIA + seg * 512:BIA + (seg + 1) * 512],
                        start=False, stop=True),
                    deps=[f_bo, t_onesb], inc="st")
                xs = X[:, t * D + seg * 512: t * D + (seg + 1) * 512]
                rv = p.emit(
                    "vector", lambda e, xs=xs, t=t, seg=seg:
                    e.scalar_tensor_tensor(
                        xs, PB[t][:, :], 1.0, xs, ALU.mult, ALU.add,
                        accum_out=STT[:, S_ACC + t * 4 + seg:
                                      S_ACC + t * 4 + seg + 1]),
                    deps=[bt], inc="sv")
                res_toks[t].append(rv)
        for t in range(NT):
            finalize_sums(t, res_toks[t], 4)
        barrier = ("st", p.cnt["st"])
        state["lt_free"] = [barrier]
        state["qt_free"] = [barrier]
        state["scr2_free"] = [f_ltp]
        state["stream_guard"] = barrier

    def emit_l1ffn(call):
        t_wc = t_wd = t_wu = imgs_tok["wcdu"]
        if call == 0:
            p.dma("gpsimd", wqk1_bf[:, :], I["L1_attn_WqkT"][:, :], sem="sgw3")
            p.dma("gpsimd", wv1_bf[:, :], I["L1_attn_WvT"][:, :], sem="sgw3")
            p.dma("gpsimd", wu1_bf[:, :], I["L1_attn_WuT"][:, :], sem="sgw3")
            p.dma("gpsimd", wo1_bf[:, :], I["L1_attn_WoT"][:, :], sem="sgw3")
            imgs_tok["l1a"] = ("sgw3", p.cnt["sgw3"])
        t_nw = p.dma("gpsimd", BR[:, 8192:8192 + D],
                     I["L1_ffn_norm_w"].to_broadcast([128, D]),
                     allow_slow_non_contiguous=True,
                     deps=state["qt_free"])
        t_nw = p.fence("gpsimd", "sg")
        lt_toks = ln_prologue(rms=True, post_mul=BR[:, 8192:8192 + D],
                              post_mul_tok=t_nw)
        t_gen = p.dma("gpsimd",
                      QT[:, 0:8192].rearrange("p (gc c) -> p gc c", c=GD),
                      I["L1_ffn_genes"].rearrange("(gc p) c -> p gc c", p=128),
                      deps=state["qt_free"])
        t_gen = p.fence("gpsimd", "sg")
        for cc in range(2):
            p.dmat("sync", QT[:, 8192 + cc * 2048:8192 + (cc + 1) * 2048],
                   wd_bf[:, cc * 128:(cc + 1) * 128],
                   deps=[t_wd] + state["qt_free"])
        for cc in range(2):
            p.dmat("sync", QT[:, 12288 + cc * 2048:12288 + (cc + 1) * 2048],
                   wu_bf[:, cc * 128:(cc + 1) * 128],
                   deps=[t_wu] + state["qt_free"])
        f_wdu = p.fence("sync", "sd")
        wdt_toks = [f_wdu, f_wdu]
        wut_toks = [f_wdu, f_wdu]
        lg_stores = {}
        ev_prev = {}
        sbufg = {0: None, 1: None, 2: None}
        sq_state = {"j": 0}
        loaded = {}

        def load_hs(j, extra=()):
            buf = sq_state["j"] % 3
            sq_state["j"] += 1
            base = buf * 4096
            gs, h = divmod(j, 2)
            for k in range(8):
                dc = h * 8 + k
                p.dmat("sync", AR[:, base + k * 512: base + (k + 1) * 512],
                       wc_bf[gs * 512:(gs + 1) * 512,
                             dc * 128:(dc + 1) * 128],
                       deps=[t_wc, sbufg[buf]] + list(extra))
            loaded[j] = (buf, base, p.fence("sync", "sd"))
        for j in range(3):
            load_hs(j, extra=state["qt_free"])
        nxt = 3
        for gs in range(8):
            b0, base0, f0 = loaded[2 * gs]
            b1, base1, f1 = loaded[2 * gs + 1]
            last = None
            for t in range(NT):
                pb = PB[t % 2]
                for dc in range(DCH):
                    col = (base0 + dc * 512 if dc < 8
                           else base1 + (dc - 8) * 512)
                    ftok = f0 if dc < 8 else f1
                    last = p.emit(
                        "tensor", lambda e, col=col, gs=gs, t=t, dc=dc, pb=pb:
                        e.matmul(pb[:, :],
                                 LT[:, dc * TPC + t * 128:
                                    dc * TPC + t * 128 + 128],
                                 AR[:, col:col + 512],
                                 start=(dc == 0), stop=(dc == DCH - 1)),
                        deps=[ftok, lt_toks[dc],
                              ev_prev.get(t - 2) if t >= 2
                              else ev_prev.get(t + 6)], inc="st")
                sl = t % 2
                slot_guard = (lg_stores.get((gs, t - 2)) if t >= 2
                              else lg_stores.get((gs - 1, t + 6)))
                ev = p.emit("scalar", lambda e, sl=sl, pb=pb: e.activation(
                    out=BR[:, sl * 512:(sl + 1) * 512],
                    in_=pb[:, :], func=AF.Identity, scale=CB[:, 43:44]),
                    deps=[last, t_it, slot_guard], inc="ss")
                ev_prev[t] = ev
                st = p.dma("gpsimd", lgscr[t * 128:(t + 1) * 128,
                                           gs * 512:(gs + 1) * 512],
                           BR[:, sl * 512:(sl + 1) * 512],
                           deps=[ev], sem="wp%d" % (2 + t % 2))
                lg_stores[(gs, t)] = st
            sbufg[b0] = last
            sbufg[b1] = last
            while nxt < 16 and nxt <= 2 * gs + 4:
                load_hs(nxt)
                nxt += 1
        lg_ceil = [("wp2", p.cnt["wp2"]), ("wp3", p.cnt["wp3"])]
        prev_tile = {}
        for t in range(NT):
            t_ld = p.dma("sync", BR[:, 0:G],
                         lgscr[t * 128:(t + 1) * 128, :],
                         deps=lg_ceil + [prev_tile.get(t - 1)])
            t_lc = p.emit("vector", lambda e: e.tensor_copy(
                BR[:, 4096:4096 + G], BR[:, 0:G]), deps=[t_ld], inc="sv")
            rt = t_lc
            for r in range(4):
                mx = p.emit("vector", lambda e, r=r: e.max(
                    out=T8[:, r * 8:(r + 1) * 8], in_=BR[:, 4096:4096 + G]),
                    deps=[rt], inc="sv")
                dr = p.emit("vector", lambda e: e.drain(), deps=[], inc="sv")
                if r < 3:
                    cm = p.emit("vector", lambda e, r=r: e.tensor_single_scalar(
                        MSK[:, :], BR[:, 4096:4096 + G],
                        T8[:, r * 8 + 7:r * 8 + 8], ALU.is_ge),
                        deps=[dr], inc="sv")
                    rt = p.emit("vector", lambda e: e.copy_predicated(
                        BR[:, 4096:4096 + G], MSK[:, :],
                        NEGB[:, :].to_broadcast([128, G])),
                        deps=[cm, t_negb], inc="sv")
                else:
                    rt = dr
            t_nx = p.emit("vector", lambda e: e.tensor_scalar_mul(
                STT[:, S_DEN:S_DEN + 1], T8[:, 0:1], -1.0),
                deps=[rt], inc="sv")
            t_ex = p.emit("scalar", lambda e: e.activation(
                out=BR[:, 4096:4096 + G], in_=BR[:, 0:G], func=AF.Exp,
                bias=STT[:, S_DEN:S_DEN + 1]), deps=[t_nx], inc="ss")
            t_wk = p.emit("vector", lambda e: e.scalar_tensor_tensor(
                BR[:, 4096:4096 + G], BR[:, 0:G], T8[:, 31:32],
                BR[:, 4096:4096 + G], ALU.is_ge, ALU.mult,
                accum_out=STT[:, S_DEN + 1:S_DEN + 2]),
            deps=[t_ex], inc="sv")
            wt_toks = []
            for grp in range(8):
                pb = PB[6 + grp % 2]
                trs = []
                for b in range(4):
                    gc = grp * 4 + b
                    trs.append(p.emit(
                        "tensor", lambda e, gc=gc, b=b, pb=pb: e.transpose(
                            pb[:, b * 128:(b + 1) * 128],
                            BR[:, 4096 + gc * 128:4096 + (gc + 1) * 128],
                            IDF[:, :]),
                        deps=[t_wk, t_idf,
                              wt_toks[grp - 2] if grp >= 2 else None],
                        inc="st"))
                wt = p.emit("vector", lambda e, grp=grp, pb=pb: e.tensor_copy(
                    AR[:, grp * 512:(grp + 1) * 512], pb[:, :]),
                    deps=[trs[3]], inc="sv")
                wt_toks.append(wt)
            t_rc = p.emit("vector", lambda e: e.reciprocal(
                STT[:, S_DEN + 2:S_DEN + 3], STT[:, S_DEN + 1:S_DEN + 2]),
                deps=[t_wk], inc="sv")
            last = None
            for gc in range(GCH):
                last = p.emit(
                    "tensor", lambda e, gc=gc: e.matmul(
                        PB[5][:, 0:256],
                        AR[:, gc * 128:gc * 128 + 128],
                        QT[:, gc * 256:(gc + 1) * 256],
                        start=(gc == 0), stop=(gc == GCH - 1)),
                    deps=[wt_toks[gc // 4], t_gen], inc="st")
            t_eb = p.emit("scalar", lambda e: e.activation(
                out=BR[:, 2048:2048 + 256], in_=PB[5][:, 0:256],
                func=AF.Identity, scale=STT[:, S_DEN + 2:S_DEN + 3]),
                deps=[last, t_rc], inc="ss")
            tr1 = p.emit("tensor", lambda e: e.transpose(
                PB[6][:, 0:128], BR[:, 2048:2048 + 128], IDF[:, :]),
                deps=[t_eb, t_idf], inc="st")
            tr2 = p.emit("tensor", lambda e: e.transpose(
                PB[6][:, 128:256], BR[:, 2048 + 128:2048 + 256], IDF[:, :]),
                deps=[t_eb], inc="st")
            t_et = p.emit("vector", lambda e: e.tensor_copy(
                AR[:, 4352:4352 + 256], PB[6][:, 0:256]),
                deps=[tr1, tr2], inc="sv")
            dn_toks = []
            up_toks = []
            for ds in range(4):
                lastd = None
                for cc in range(2):
                    lastd = p.emit(
                        "tensor", lambda e, ds=ds, cc=cc: e.matmul(
                            PB[0][:, :],
                            AR[:, 4352 + cc * 128:4352 + (cc + 1) * 128],
                            QT[:, 8192 + cc * 2048 + ds * 512:
                               8192 + cc * 2048 + (ds + 1) * 512],
                            start=(cc == 0), stop=(cc == 1)),
                        deps=[t_et, wdt_toks[cc],
                              dn_toks[ds - 1] if ds > 0 else None], inc="st")
                dn = p.emit("scalar", lambda e, ds=ds: e.activation(
                    out=BR[:, 0 + ds * 512:(ds + 1) * 512], in_=PB[0][:, :],
                    func=AF.Tanh), deps=[lastd], inc="ss")
                dn_toks.append(dn)
                lastu = None
                for cc in range(2):
                    lastu = p.emit(
                        "tensor", lambda e, ds=ds, cc=cc: e.matmul(
                            PB[1][:, :],
                            AR[:, 4352 + cc * 128:4352 + (cc + 1) * 128],
                            QT[:, 12288 + cc * 2048 + ds * 512:
                               12288 + cc * 2048 + (ds + 1) * 512],
                            start=(cc == 0), stop=(cc == 1)),
                        deps=[t_et, wut_toks[cc],
                              up_toks[ds - 1] if ds > 0 else None], inc="st")
                up = p.emit("scalar", lambda e, ds=ds: e.activation(
                    out=BR[:, 4096 + ds * 512:4096 + (ds + 1) * 512],
                    in_=PB[1][:, :], func=AF.Copy), deps=[lastu], inc="ss")
                up_toks.append(up)
            xs = X[:, t * D:(t + 1) * D]
            t_xa = p.emit("scalar", lambda e, xs=xs, t=t: e.activation(
                out=BR[:, 2048:2048 + D], in_=xs, func=AF.Identity,
                bias=STT[:, S_NMR + t:S_NMR + t + 1],
                scale=STT[:, S_R + t:S_R + t + 1]),
                deps=dn_toks + up_toks, inc="ss")
            t_xn = p.emit("vector", lambda e: e.tensor_mul(
                BR[:, 2048:2048 + D], BR[:, 2048:2048 + D],
                BR[:, 8192:8192 + D]), deps=[t_xa, t_nw], inc="sv")
            t_p1 = p.emit("vector", lambda e: e.tensor_mul(
                BR[:, 0:D], BR[:, 0:D], BR[:, 2048:2048 + D]),
                deps=[t_xn] + dn_toks, inc="sv")
            t_p2 = p.emit("vector", lambda e: e.tensor_mul(
                BR[:, 0:D], BR[:, 0:D], BR[:, 4096:4096 + D]),
                deps=[t_p1] + up_toks, inc="sv")
            rv = p.emit("vector", lambda e, xs=xs, t=t:
                        e.scalar_tensor_tensor(
                            xs, BR[:, 0:D], CB[:, 42:43], xs,
                            ALU.mult, ALU.add,
                            accum_out=STT[:, S_ACC + t * 4:S_ACC + t * 4 + 1]),
                        deps=[t_p2, t_hs], inc="sv")
            prev_tile[t] = rv
            finalize_sums(t, [rv], 1)
        bar = ("st", p.cnt["st"])
        state["lt_free"] = [bar]
        state["qt_free"] = [bar]
        state["stream_guard"] = bar

    stages = [lambda: emit_l0ffn(0), emit_l0attn, lambda: emit_l0ffn(1),
              lambda: emit_l1ffn(0), emit_l1attn, lambda: emit_l1ffn(1)]
    global _STAGE_MARKS
    _STAGE_MARKS = [dict(p.cnt)]
    for i in range(min(upto, 6)):
        stages[i]()
        _STAGE_MARKS.append(dict(p.cnt))

    out_toks = []
    for t in range(NT):
        out_toks.append(p.dma("sync", y[t * 128:(t + 1) * 128, :],
                              X[:, t * D:(t + 1) * D],
                              deps=[state["x_tok"][t]] +
                              ([state["sum_tok"][t]] if state["sum_tok"][t]
                               else [])))
    for i, (tn, tsh, tf) in enumerate(taps):
        src = tf(ns)
        from concourse.bass_types import DRamTensorHandle
        eng = "gpsimd" if (src.dtype != F32 or
                           isinstance(src.tensor, DRamTensorHandle)) else "sync"
        out_toks.append(p.dma(eng, tap_dram[i], src, deps=p.all_toks()))
    final_deps = out_toks + [("sg", p.cnt["sg"]), ("so", p.cnt["so"])]
    p.emit("gpsimd", lambda e: e.engine_nop(), deps=final_deps, inc="sq")

    with nc.Block() as block:
        @block.sync
        def _(eng):
            for fn in p.q["sync"]:
                fn(eng)

        @block.scalar
        def _(eng):
            for fn in p.q["scalar"]:
                fn(eng)

        @block.vector
        def _(eng):
            for fn in p.q["vector"]:
                fn(eng)

        @block.tensor
        def _(eng):
            for fn in p.q["tensor"]:
                fn(eng)

        @block.gpsimd
        def _(eng):
            for fn in p.q["gpsimd"]:
                fn(eng)

    ctx.close()
    return nc


_NC_CACHE = {}


def kernel(**inputs):
    x = np.ascontiguousarray(np.asarray(inputs["L0_attn_x"],
                                        dtype=np.float32).reshape(8192, D))
    key = 6
    if key not in _NC_CACHE:
        _NC_CACHE[key] = build_program(upto=6)
    nc = _NC_CACHE[key]

    def arr(name, shape):
        return np.ascontiguousarray(
            np.asarray(inputs[name], dtype=np.float32).reshape(shape))

    common = {
        "L0_attn_Wq": arr("L0_attn_Wq", (D, D)),
        "L0_attn_Wv": arr("L0_attn_Wv", (D, D)),
        "L0_attn_Wo": arr("L0_attn_Wo", (D, D)),
        "L0_attn_Wg": arr("L0_attn_Wg", (D, D)),
        "L0_attn_ln_gamma": arr("L0_attn_ln_gamma", (1, D)),
        "L0_attn_ln_beta": arr("L0_attn_ln_beta", (1, D)),
        "L0_attn_beta": arr("L0_attn_beta", (1, NHEAD)),
        "L0_ffn_gate_W": arr("L0_ffn_gate_W", (NEXP, D)),
        "L0_ffn_W1T": np.ascontiguousarray(
            arr("L0_ffn_W1", (NEXP, D, DFF)).transpose(0, 2, 1)),
        "L0_ffn_b1": arr("L0_ffn_b1", (NEXP, DFF)),
        "L0_ffn_W2T": np.ascontiguousarray(
            arr("L0_ffn_W2", (NEXP, DFF, D)).transpose(0, 2, 1)),
        "L0_ffn_b2": arr("L0_ffn_b2", (NEXP, D)),
        "L1_attn_WqkT": np.ascontiguousarray(arr("L1_attn_Wqk", (D, D)).T),
        "L1_attn_bqk": arr("L1_attn_bqk", (1, D)),
        "L1_attn_WvT": np.ascontiguousarray(arr("L1_attn_Wv", (D, D)).T),
        "L1_attn_bv": arr("L1_attn_bv", (1, D)),
        "L1_attn_WuT": np.ascontiguousarray(arr("L1_attn_Wu", (D, D)).T),
        "L1_attn_bu": arr("L1_attn_bu", (1, D)),
        "L1_attn_WoT": np.ascontiguousarray(arr("L1_attn_Wo", (D, D)).T),
        "L1_attn_bo": arr("L1_attn_bo", (1, D)),
        "L1_attn_ln_g": arr("L1_attn_ln_g", (1, D)),
        "L1_attn_ln_b": arr("L1_attn_ln_b", (1, D)),
        "L1_ffn_Wc": arr("L1_ffn_Wc", (G, D)),
        "L1_ffn_temperature": arr("L1_ffn_temperature", (1, 1)),
        "L1_ffn_genes": arr("L1_ffn_genes", (G, GD)),
        "L1_ffn_Wd": arr("L1_ffn_Wd", (D, GD)),
        "L1_ffn_Wu": arr("L1_ffn_Wu", (D, GD)),
        "L1_ffn_norm_w": arr("L1_ffn_norm_w", (1, D)),
        "L1_ffn_scale": arr("L1_ffn_scale", (1, 1)),
    }
    in_maps = []
    for i in range(NCORES):
        m = dict(common)
        m["x"] = np.ascontiguousarray(x[i * TPC:(i + 1) * TPC])
        coef = np.array([[1024.0 * (i - 1 - j)] if j < i else [1e30]
                         for j in range(8)], dtype=np.float32)
        m["coef"] = coef
        in_maps.append(m)
    res = run_bass_kernel_spmd(nc, in_maps, core_ids=list(range(NCORES)))
    out = np.concatenate([res.results[i]["y"] for i in range(NCORES)], axis=0)
    return out.reshape(1, 8192, D)


# version 06 (final)
# speedup vs baseline: 1.0212x; 1.0184x over previous
import traceback
import numpy as np
import concourse.bass as bass
import concourse.mybir as mybir
from concourse.bass_utils import run_bass_kernel_spmd

F32 = mybir.dt.float32
BF16 = mybir.dt.bfloat16
U8 = mybir.dt.uint8
I32 = mybir.dt.int32
AF = mybir.ActivationFunctionType
ALU = mybir.AluOpType
AX = mybir.AxisListType.X

D = 2048
NCORES = 8
TPC = 1024
NT = TPC // 128
DCH = D // 128
DFF = 1408
FCH = DFF // 128
NEXP = 6
G = 4096
GCH = G // 128
GD = 256
NHEAD = 16
LN_EPS = 1e-5
RMS_EPS = float(np.finfo(np.float32).eps)
F32_TINY = float(np.finfo(np.float32).tiny)
LN2 = float(np.log(2.0))

ENGINES = ("sync", "scalar", "vector", "tensor", "gpsimd")


class Prog:
    def __init__(self, nc):
        self.nc = nc
        self.q = {e: [] for e in ENGINES}
        self.sems = {}
        self.cnt = {}
        self.last_wait = {}

    def sem(self, ctx, name):
        h = ctx.enter_context(self.nc.semaphore(name))
        self.sems[name] = h
        self.cnt[name] = 0

    def emit(self, eng, fn, deps=(), inc=None, amt=1):
        deps = [d for d in deps if d is not None]
        real = []
        for (s, v) in deps:
            if v <= 0:
                continue
            key = (eng, s)
            if self.last_wait.get(key, 0) >= v:
                continue
            self.last_wait[key] = v
            real.append((self.sems[s], v))
        tok = None
        semh = None
        if inc is not None:
            self.cnt[inc] += amt
            tok = (inc, self.cnt[inc])
            semh = self.sems[inc]
        a = amt

        site = traceback.extract_stack()[-3:-1]

        def run(e, real=real, fn=fn, semh=semh, a=a, site=site):
            for (h, v) in real:
                e.wait_ge(h, v)
            try:
                inst = fn(e)
            except Exception:
                print("EMIT SITE:", "".join(traceback.format_list(site)))
                traceback.print_exc()
                raise
            if semh is not None:
                inst.then_inc(semh, a)
        self.q[eng].append(run)
        return tok

    def dma(self, eng, out, in_, deps=(), sem=None, **kw):
        if sem is None:
            sem = {"sync": "sd", "gpsimd": "sg", "scalar": "so"}[eng]

        def f(e, out=out, in_=in_, kw=kw):
            src = in_() if callable(in_) else in_
            try:
                return e.dma_start(out=out, in_=src, **kw)
            except Exception:
                print("DMA FAIL out=", out, "src=", src, "kw=", kw)
                raise
        return self.emit(eng, f, deps=deps, inc=sem, amt=16)

    def dmat(self, eng, out, in_, deps=(), sem=None):
        if sem is None:
            sem = {"sync": "sd", "scalar": "so"}[eng]
        return self.emit(eng, lambda e: e.dma_start_transpose(out, in_),
                         deps=deps, inc=sem, amt=16)

    def fence(self, eng, dma_sem):
        fsem = {"sync": "sfd", "gpsimd": "sfg", "scalar": "sfs"}[eng]
        ceil = self.cnt[dma_sem]
        self.cnt[fsem] += 1
        tok = (fsem, self.cnt[fsem])
        fh = self.sems[fsem]
        real = []
        key = (eng, dma_sem)
        if self.last_wait.get(key, 0) < ceil:
            self.last_wait[key] = ceil
            real.append((self.sems[dma_sem], ceil))

        def run(e, real=real, fh=fh):
            for (h, v) in real:
                e.wait_ge(h, v)
            e.sem_inc(fh, 1)
        self.q[eng].append(run)
        return tok

    def all_toks(self):
        return [(s, c) for s, c in self.cnt.items()]


def build_program(upto=6, taps=None):
    taps = taps or []
    nc = bass.Bass("TRN2", target_bir_lowering=False)

    I = {}

    def di(name, shape):
        I[name] = nc.dram_tensor(name, list(shape), F32, kind="ExternalInput").ap()

    di("x", (TPC, D))
    di("coef", (8, 1))
    di("L0_attn_Wq", (D, D)); di("L0_attn_Wv", (D, D))
    di("L0_attn_Wo", (D, D)); di("L0_attn_Wg", (D, D))
    di("L0_attn_ln_gamma", (1, D)); di("L0_attn_ln_beta", (1, D))
    di("L0_attn_beta", (1, NHEAD))
    di("L0_ffn_gate_W", (NEXP, D))
    di("L0_ffn_W1T", (NEXP, DFF, D)); di("L0_ffn_b1", (NEXP, DFF))
    di("L0_ffn_W2T", (NEXP, D, DFF)); di("L0_ffn_b2", (NEXP, D))
    di("L1_attn_WqkT", (D, D)); di("L1_attn_bqk", (1, D))
    di("L1_attn_WvT", (D, D)); di("L1_attn_bv", (1, D))
    di("L1_attn_WuT", (D, D)); di("L1_attn_bu", (1, D))
    di("L1_attn_WoT", (D, D)); di("L1_attn_bo", (1, D))
    di("L1_attn_ln_g", (1, D)); di("L1_attn_ln_b", (1, D))
    di("L1_ffn_Wc", (G, D)); di("L1_ffn_temperature", (1, 1))
    di("L1_ffn_genes", (G, GD)); di("L1_ffn_Wd", (D, GD)); di("L1_ffn_Wu", (D, GD))
    di("L1_ffn_norm_w", (1, D)); di("L1_ffn_scale", (1, 1))

    y = nc.dram_tensor("y", [TPC, D], F32, kind="ExternalOutput").ap()

    wq_bf = nc.dram_tensor("wq_bf", [D, D], BF16).ap()
    wv_bf = nc.dram_tensor("wv_bf", [D, D], BF16).ap()
    wg_bf = nc.dram_tensor("wg_bf", [D, D], BF16).ap()
    wo0_bf = nc.dram_tensor("wo0_bf", [D, D], BF16).ap()
    wc_bf = nc.dram_tensor("wc_bf", [G, D], BF16).ap()
    wd_bf = nc.dram_tensor("wd_bf", [D, GD], BF16).ap()
    wu_bf = nc.dram_tensor("wu_bf", [D, GD], BF16).ap()
    wqk1_bf = nc.dram_tensor("wqk1_bf", [D, D], BF16)
    wv1_bf = nc.dram_tensor("wv1_bf", [D, D], BF16)
    wu1_bf = nc.dram_tensor("wu1_bf", [D, D], BF16)
    wo1_bf = nc.dram_tensor("wo1_bf", [D, D], BF16)
    w1sel = [nc.dram_tensor("w1sel%d" % c, [2, DFF, D], BF16).ap()
             for c in range(2)]
    w2sel = [nc.dram_tensor("w2sel%d" % c, [2, D, DFF], BF16).ap()
             for c in range(2)]
    lnscr = nc.dram_tensor("lnscr", [TPC, D], BF16).ap()
    scr2 = nc.dram_tensor("scr2", [TPC, D], BF16).ap()
    lgscr = nc.dram_tensor("lgscr", [TPC, G], F32).ap()
    rowscr = nc.dram_tensor("rowscr", [8, 4096], F32).ap()
    cc_g_in = nc.dram_tensor("cc_g_in", [1, D], F32).ap()
    cc_g_out = nc.dram_tensor("cc_g_out", [1, D], F32).ap()
    cc_s_in = nc.dram_tensor("cc_s_in", [128, NHEAD], F32).ap()
    cc_s_out = nc.dram_tensor("cc_s_out", [128 * NCORES, NHEAD], F32).ap()
    cc_kv_in = nc.dram_tensor("cc_kv_in", [128, D], F32).ap()
    cc_kv_out = nc.dram_tensor("cc_kv_out", [128, D], F32).ap()
    tap_dram = [nc.dram_tensor("tap_" + tn, list(tsh), F32,
                               kind="ExternalOutput").ap()
                for (tn, tsh, _f) in taps]

    from contextlib import ExitStack
    ctx = ExitStack()

    def sbt(name, shape, dt):
        return ctx.enter_context(nc.sbuf_tensor(name, list(shape), dt))

    X = sbt("X", (128, NT * D), F32)
    LT = sbt("LT", (128, DCH * TPC), BF16)
    QT = sbt("QT", (128, DCH * TPC), BF16)
    AR = sbt("AR", (128, 12288), BF16)
    BR = sbt("BR", (128, 10752), F32)
    STT = sbt("STT", (128, 176), F32)
    RW = sbt("RW", (16, 1792), F32)
    MSK = sbt("MSK", (128, G), U8)
    ONEC = sbt("ONEC", (128, 1), F32)
    ONEB = sbt("ONEB", (1, 128), BF16)
    IDB = sbt("IDB", (128, 128), BF16)
    IDF = sbt("IDF", (128, 128), F32)
    NEGB = sbt("NEGB", (128, 1), F32)
    ICOL = sbt("ICOL", (1, 4), I32)
    CB = sbt("CB", (128, 64), F32)
    T8 = sbt("T8", (128, 40), F32)
    GWT = sbt("GWT", (128, 96), F32)
    MEC = sbt("MEC", (128, 16), F32)
    CONSTS = sbt("CONSTS", (128, 2), F32)
    nc.const_aps.register(F32, LN_EPS, CONSTS[:, 0:1])
    nc.const_aps.register(F32, RMS_EPS, CONSTS[:, 1:2])

    PB = [ctx.enter_context(nc.psum_tensor("pb%d" % i, [128, 512], F32))
          for i in range(8)]

    p = Prog(nc)
    for s in ("sd", "sg", "so", "sv", "ss", "st", "sq", "scc",
              "sfd", "sfg", "sfs", "wp0", "wp1", "wp2", "wp3", "sgw", "sgc",
              "sgw2", "sgw3"):
        p.sem(ctx, s)

    S_SUM, S_SS, S_MU, S_VAR, S_R, S_NMR, S_M1, S_M2 = 0, 8, 16, 24, 32, 40, 48, 56
    S_ACC = 64
    S_GAM = 96
    S_BET = 112
    S_DEN = 128
    S_MISC = 144

    ns = dict(X=X, LT=LT, QT=QT, AR=AR, BR=BR, STT=STT, RW=RW, CB=CB, T8=T8,
              MSK=MSK, IDB=IDB, MEC=MEC, GWT=GWT, cc_g_in=cc_g_in,
              cc_g_out=cc_g_out, lnscr=lnscr, scr2=scr2, lgscr=lgscr)

    x_toks = [p.dma("sync", X[:, t * D:(t + 1) * D],
                    I["x"][t * 128:(t + 1) * 128, :]) for t in range(NT)]
    t_ones = p.emit("vector", lambda e: e.memset(ONEC[:, :], 1.0), inc="sv")
    t_onesb = p.emit("vector", lambda e: e.memset(ONEB[:, :], 1.0), inc="sv")
    t_negb = p.emit("vector", lambda e: e.memset(NEGB[:, :], -1e30), inc="sv")
    t_id0 = p.emit("vector", lambda e: e.memset(IDB[:, :], 1.0), inc="sv")
    t_idb = p.emit("gpsimd", lambda e: e.affine_select(
        IDB[:, :], IDB[:, :], pattern=[[1, 128]], compare_op=ALU.is_equal,
        fill=0.0, base=0, channel_multiplier=-1), deps=[t_id0], inc="sq")
    t_if0 = p.emit("vector", lambda e: e.memset(IDF[:, :], 1.0), inc="sv")
    t_idf = p.emit("gpsimd", lambda e: e.affine_select(
        IDF[:, :], IDF[:, :], pattern=[[1, 128]], compare_op=ALU.is_equal,
        fill=0.0, base=0, channel_multiplier=-1), deps=[t_if0], inc="sq")
    t_coef = p.dma("sync", CB[0:8, 0:1], I["coef"][:, :])
    t_c0 = p.emit("vector", lambda e: e.memset(CONSTS[:, 0:1], LN_EPS), inc="sv")
    t_c1 = p.emit("vector", lambda e: e.memset(CONSTS[:, 1:2], RMS_EPS),
                  inc="sv")

    t_wq = p.dma("gpsimd", wq_bf[:, :], I["L0_attn_Wq"][:, :], sem="sgw")
    t_wv = p.dma("gpsimd", wv_bf[:, :], I["L0_attn_Wv"][:, :], sem="sgw")
    t_wg = p.dma("gpsimd", wg_bf[:, :], I["L0_attn_Wg"][:, :], sem="sgw")
    t_wo0 = p.dma("gpsimd", wo0_bf[:, :], I["L0_attn_Wo"][:, :], sem="sgw")
    t_wimg = ("sgw", p.cnt["sgw"])
    t_wq = t_wv = t_wg = t_wo0 = t_wimg
    l1_imgs = {"L1_attn_Wqk": wqk1_bf, "L1_attn_Wv": wv1_bf,
               "L1_attn_Wu": wu1_bf, "L1_attn_Wo": wo1_bf}
    imgs_tok = {}

    t_tmp = p.dma("sync", RW[0:1, 128:129], I["L1_ffn_temperature"][:, :])
    t_scl = p.dma("sync", RW[0:1, 129:130], I["L1_ffn_scale"][:, :])
    f_setup_d = p.fence("sync", "sd")
    t_tm = p.emit("vector", lambda e: e.tensor_scalar_max(
        RW[0:1, 128:129], RW[0:1, 128:129], 0.1), deps=[f_setup_d], inc="sv")
    t_tr = p.emit("vector", lambda e: e.reciprocal(
        RW[0:1, 130:131], RW[0:1, 128:129]), deps=[t_tm], inc="sv")
    t_sh = p.emit("vector", lambda e: e.tensor_scalar_mul(
        RW[0:1, 131:132], RW[0:1, 129:130], 0.5), deps=[f_setup_d, t_tr],
        inc="sv")
    t_rs = p.dma("gpsimd", rowscr[5:6, 0:4], RW[0:1, 128:132], deps=[t_tr, t_sh])
    f_rs = p.fence("gpsimd", "sg")
    t_hs = p.dma("gpsimd", CB[:, 42:43], rowscr[5:6, 3:4].to_broadcast([128, 1]),
                 deps=[f_rs], allow_slow_non_contiguous=True)
    t_it = p.dma("gpsimd", CB[:, 43:44], rowscr[5:6, 2:3].to_broadcast([128, 1]),
                 deps=[f_rs], allow_slow_non_contiguous=True)
    f_setup_g = p.fence("gpsimd", "sg")
    t_hs = t_it = f_setup_g
    t_coef = f_setup_d

    state = {"sum_tok": [None] * NT, "x_tok": [f_setup_d] * NT,
             "lt_free": [], "qt_free": [], "lnscr_free": [], "scr2_free": [],
             "stream_guard": None}

    def ln_prologue(rms=False, f32_consumer=None, post_mul=None,
                    post_mul_tok=None):
        sum_toks, ss_toks = [], []
        for t in range(NT):
            xs = X[:, t * D:(t + 1) * D]
            if state["sum_tok"][t] is None:
                sum_toks.append(p.emit(
                    "vector", lambda e, xs=xs, t=t: e.reduce_sum(
                        STT[:, S_SUM + t:S_SUM + t + 1], xs, axis=AX),
                    deps=[state["x_tok"][t]], inc="sv"))
            else:
                sum_toks.append(state["sum_tok"][t])
            sqsl = 4096 if (t % 2 == 0) else 6144
            sq_t = p.emit(
                "vector", lambda e, xs=xs, sqsl=sqsl: e.tensor_mul(
                    BR[:, sqsl:sqsl + D], xs, xs),
                deps=[state["x_tok"][t], ss_toks[-2] if len(ss_toks) >= 2
                      else None], inc="sv")
            ss_toks.append(p.emit(
                "vector", lambda e, t=t, sqsl=sqsl: e.reduce_sum(
                    STT[:, S_SS + t:S_SS + t + 1], BR[:, sqsl:sqsl + D],
                    axis=AX),
                deps=[sq_t], inc="sv"))
        mu_t = p.emit("vector", lambda e: e.tensor_scalar_mul(
            STT[:, S_MU:S_MU + 8], STT[:, S_SUM:S_SUM + 8], 1.0 / D),
            deps=sum_toks, inc="sv")
        ex2_t = p.emit("vector", lambda e: e.tensor_scalar_mul(
            STT[:, S_VAR:S_VAR + 8], STT[:, S_SS:S_SS + 8], 1.0 / D),
            deps=ss_toks, inc="sv")
        mu2_t = p.emit("vector", lambda e: e.tensor_tensor(
            STT[:, S_M1:S_M1 + 8], STT[:, S_MU:S_MU + 8],
            STT[:, S_MU:S_MU + 8], ALU.mult), deps=[mu_t], inc="sv")
        var_t = p.emit("vector", lambda e: e.tensor_sub(
            STT[:, S_VAR:S_VAR + 8], STT[:, S_VAR:S_VAR + 8],
            STT[:, S_M1:S_M1 + 8]), deps=[ex2_t, mu2_t], inc="sv")
        lnv_t = p.emit("scalar", lambda e: e.activation(
            out=STT[:, S_M1:S_M1 + 8], in_=STT[:, S_VAR:S_VAR + 8],
            func=AF.Ln, bias=LN_EPS), deps=[var_t, t_c0], inc="ss")
        r_t = p.emit("scalar", lambda e: e.activation(
            out=STT[:, S_R:S_R + 8], in_=STT[:, S_M1:S_M1 + 8],
            func=AF.Exp, scale=-0.5), deps=[lnv_t], inc="ss")
        if rms:
            r2 = p.emit("vector", lambda e: e.tensor_tensor(
                STT[:, S_M1:S_M1 + 8], STT[:, S_R:S_R + 8],
                STT[:, S_R:S_R + 8], ALU.mult), deps=[r_t], inc="sv")
            msq = p.emit("vector", lambda e: e.tensor_tensor(
                STT[:, S_M1:S_M1 + 8], STT[:, S_M1:S_M1 + 8],
                STT[:, S_VAR:S_VAR + 8], ALU.mult), deps=[r2], inc="sv")
            lq = p.emit("scalar", lambda e: e.activation(
                out=STT[:, S_M1:S_M1 + 8], in_=STT[:, S_M1:S_M1 + 8],
                func=AF.Ln, bias=RMS_EPS), deps=[msq, t_c1], inc="ss")
            q_t = p.emit("scalar", lambda e: e.activation(
                out=STT[:, S_M1:S_M1 + 8], in_=STT[:, S_M1:S_M1 + 8],
                func=AF.Exp, scale=-0.5), deps=[lq], inc="ss")
            r_t = p.emit("vector", lambda e: e.tensor_tensor(
                STT[:, S_R:S_R + 8], STT[:, S_R:S_R + 8],
                STT[:, S_M1:S_M1 + 8], ALU.mult), deps=[q_t, r_t], inc="sv")
        nmu_t = p.emit("vector", lambda e: e.tensor_scalar_mul(
            STT[:, S_NMR:S_NMR + 8], STT[:, S_MU:S_MU + 8], -1.0),
            deps=[mu_t], inc="sv")
        nmr_t = p.emit("vector", lambda e: e.tensor_tensor(
            STT[:, S_NMR:S_NMR + 8], STT[:, S_NMR:S_NMR + 8],
            STT[:, S_R:S_R + 8], ALU.mult), deps=[nmu_t, r_t], inc="sv")

        st_toks = []
        guard_br = [[], []]
        guard_ar = [[], []]
        lt_deps = state["lt_free"]
        for t in range(NT):
            xs = X[:, t * D:(t + 1) * D]
            sl = t % 2
            need_f32 = (f32_consumer is not None) or (post_mul is not None)
            if need_f32:
                dst = BR[:, sl * D:(sl + 1) * D]
                aff = p.emit("scalar", lambda e, xs=xs, t=t, dst=dst:
                             e.activation(out=dst, in_=xs, func=AF.Identity,
                                          bias=STT[:, S_NMR + t:S_NMR + t + 1],
                                          scale=STT[:, S_R + t:S_R + t + 1]),
                             deps=[nmr_t] + guard_br[sl], inc="ss")
                cons = None
                if f32_consumer is not None:
                    cons = f32_consumer(t, dst, aff)
                if post_mul is not None:
                    cast = p.emit("vector", lambda e, dst=dst, sl=sl:
                                  e.tensor_tensor(AR[:, sl * D:(sl + 1) * D],
                                                  dst, post_mul, ALU.mult),
                                  deps=[aff, post_mul_tok] + guard_ar[sl],
                                  inc="sv")
                else:
                    cast = p.emit("vector", lambda e, dst=dst, sl=sl:
                                  e.tensor_copy(AR[:, sl * D:(sl + 1) * D],
                                                dst),
                                  deps=[aff] + guard_ar[sl], inc="sv")
                guard_br[sl] = [x for x in (cons, cast) if x is not None]
            else:
                cast = p.emit("scalar", lambda e, xs=xs, t=t, sl=sl:
                              e.activation(out=AR[:, sl * D:(sl + 1) * D],
                                           in_=xs, func=AF.Identity,
                                           bias=STT[:, S_NMR + t:S_NMR + t + 1],
                                           scale=STT[:, S_R + t:S_R + t + 1]),
                              deps=[nmr_t] + guard_ar[sl], inc="ss")
            stk = p.dma("gpsimd", lnscr[t * 128:(t + 1) * 128, :],
                        AR[:, sl * D:(sl + 1) * D],
                        deps=[cast] + state["lnscr_free"])
            guard_ar[sl] = [stk]
            st_toks.append(stk)
        f_store = p.fence("gpsimd", "sg")
        for dc in range(DCH):
            p.dmat("sync", LT[:, dc * TPC:(dc + 1) * TPC],
                   lnscr[:, dc * 128:(dc + 1) * 128],
                   deps=[f_store] + lt_deps)
        f_lt = p.fence("sync", "sd")
        lt_toks = [f_lt] * DCH
        state["lnscr_free"] = [f_lt]
        return lt_toks

    def finalize_sums(tile, toks, nparts):
        tk = p.emit("vector", lambda e, tile=tile, nparts=nparts: e.reduce_sum(
            STT[:, S_SUM + tile:S_SUM + tile + 1],
            STT[:, S_ACC + tile * 4:S_ACC + tile * 4 + nparts], axis=AX),
            deps=toks, inc="sv")
        state["sum_tok"][tile] = tk
        state["x_tok"][tile] = tk

    def emit_l0ffn(call):
        import os as _os
        _part = int(_os.environ.get("L0FFN_PART", "9"))
        gate_mm = {}

        def gate_consumer(t, dst, aff_tok):
            last = None
            for seg in range(4):
                last = p.emit("tensor", lambda e, dst=dst, seg=seg, t=t:
                              e.matmul(PB[4 + seg][0:1, :], ONEC[:, :],
                                       dst[:, seg * 512:(seg + 1) * 512],
                                       start=(t == 0), stop=(t == NT - 1)),
                              deps=[aff_tok, t_ones], inc="st")
            gate_mm[t] = last
            return last

        lt_toks = ln_prologue(rms=False, f32_consumer=gate_consumer
                              if _part >= 2 else None)
        if _part < 3:
            return
        ev = [p.emit("vector", lambda e, seg=seg: e.tensor_copy(
            BR[0:1, 6144 + seg * 512:6144 + (seg + 1) * 512],
            PB[4 + seg][0:1, :]),
            deps=[gate_mm[NT - 1]], inc="sv") for seg in range(4)]
        t_in = p.dma("gpsimd", cc_g_in[:, :], BR[0:1, 6144:6144 + D], deps=ev)
        t_ar = p.emit("gpsimd", lambda e: e.collective_compute(
            "AllReduce", ALU.add, replica_groups=[list(range(NCORES))],
            ins=[cc_g_in[:, :]], outs=[cc_g_out[:, :]]),
            deps=[t_in], inc="scc")
        t_mc = p.dma("gpsimd", MEC[:, :],
                     cc_g_out[0, :].rearrange("(c p) -> p c", p=128),
                     deps=[t_ar], allow_slow_non_contiguous=True)
        t_mc = p.fence("gpsimd", "sg")
        for ex in range(6):
            p.dma("sync", GWT[:, ex * 16:(ex + 1) * 16],
                  I["L0_ffn_gate_W"][ex, :].rearrange("(c p) -> p c", p=128),
                  allow_slow_non_contiguous=True)
        t_gw = [p.fence("sync", "sd")]
        glast = None
        for dc in range(DCH):
            glast = p.emit("tensor", lambda e, dc=dc: e.matmul(
                PB[0][0:1, 0:6], MEC[:, dc:dc + 1],
                GWT[:, dc:dc + 81:16],
                start=(dc == 0), stop=(dc == DCH - 1)),
                deps=[t_mc] + t_gw, inc="st")
        t_l = p.emit("vector", lambda e: e.tensor_scalar_mul(
            RW[0:1, 0:6], PB[0][0:1, 0:6], 1.0 / 8192.0),
            deps=[glast], inc="sv")
        t_mx1 = p.emit("vector", lambda e: e.reduce_max(
            RW[0:1, 8:9], RW[0:1, 0:6], axis=AX), deps=[t_l], inc="sv")
        t_o1 = p.emit("vector", lambda e: e.tensor_single_scalar(
            RW[0:1, 10:16], RW[0:1, 0:6], RW[0:1, 8:9], ALU.is_ge),
            deps=[t_mx1], inc="sv")
        t_msk = p.emit("vector", lambda e: e.scalar_tensor_tensor(
            RW[0:1, 16:22], RW[0:1, 10:16], -1e30, RW[0:1, 0:6],
            ALU.mult, ALU.add), deps=[t_o1], inc="sv")
        t_mx2 = p.emit("vector", lambda e: e.reduce_max(
            RW[0:1, 9:10], RW[0:1, 16:22], axis=AX), deps=[t_msk], inc="sv")
        t_o2 = p.emit("vector", lambda e: e.tensor_single_scalar(
            RW[0:1, 24:30], RW[0:1, 16:22], RW[0:1, 9:10], ALU.is_ge),
            deps=[t_mx2], inc="sv")
        t_io = p.emit("gpsimd", lambda e: e.iota(
            RW[0:1, 56:62], pattern=[[1, 6]], base=0, channel_multiplier=0,
            allow_small_or_imprecise_dtypes=True), inc="sq")
        t_i1 = p.emit("vector", lambda e: e.tensor_mul(
            RW[0:1, 34:40], RW[0:1, 10:16], RW[0:1, 56:62]),
            deps=[t_o1, t_io], inc="sv")
        t_i1r = p.emit("vector", lambda e: e.reduce_sum(
            RW[0:1, 44:45], RW[0:1, 34:40], axis=AX), deps=[t_i1], inc="sv")
        t_i2 = p.emit("vector", lambda e: e.tensor_mul(
            RW[0:1, 34:40], RW[0:1, 24:30], RW[0:1, 56:62]),
            deps=[t_o2, t_i1r], inc="sv")
        t_i2r = p.emit("vector", lambda e: e.reduce_sum(
            RW[0:1, 45:46], RW[0:1, 34:40], axis=AX), deps=[t_i2], inc="sv")
        t_ic = p.emit("vector", lambda e: e.tensor_copy(
            ICOL[0:1, 0:2], RW[0:1, 44:46]), deps=[t_i2r], inc="sv")
        t_d = p.emit("vector", lambda e: e.tensor_sub(
            RW[0:1, 30:31], RW[0:1, 9:10], RW[0:1, 8:9]),
            deps=[t_mx2], inc="sv")
        t_e = p.emit("scalar", lambda e: e.activation(
            out=RW[0:1, 31:32], in_=RW[0:1, 30:31], func=AF.Exp),
            deps=[t_d], inc="ss")
        t_e1 = p.emit("vector", lambda e: e.tensor_scalar_add(
            RW[0:1, 31:32], RW[0:1, 31:32], 1.0), deps=[t_e], inc="sv")
        t_wa = p.emit("vector", lambda e: e.reciprocal(
            RW[0:1, 32:33], RW[0:1, 31:32]), deps=[t_e1], inc="sv")
        t_wb = p.emit("vector", lambda e: e.tensor_scalar(
            RW[0:1, 33:34], RW[0:1, 32:33], -1.0, 1.0, ALU.mult, ALU.add),
            deps=[t_wa], inc="sv")
        cell = {}
        t_rv = p.emit("gpsimd", lambda e: (
            cell.__setitem__("rA", e.value_load(ICOL[0:1, 0:1])),
            cell.__setitem__("rB", e.value_load(ICOL[0:1, 1:2])),
            e.engine_nop())[-1], deps=[t_ic], inc="sq")
        t_b2a = p.dma("gpsimd", BR[0:1, 4096:4096 + D],
                      lambda: I["L0_ffn_b2"][cell["rA"], :], deps=[t_rv])
        t_b2b = p.dma("gpsimd", BR[0:1, 6144:6144 + D],
                      lambda: I["L0_ffn_b2"][cell["rB"], :], deps=[t_rv])
        fb2 = p.fence("gpsimd", "sg")
        t_b2a = t_b2b = fb2
        t_c1 = p.emit("vector", lambda e: e.tensor_single_scalar(
            BR[0:1, 4096:4096 + D], BR[0:1, 4096:4096 + D],
            RW[0:1, 32:33], ALU.mult), deps=[t_b2a, t_wa], inc="sv")
        t_c2 = p.emit("vector", lambda e: e.tensor_single_scalar(
            BR[0:1, 6144:6144 + D], BR[0:1, 6144:6144 + D],
            RW[0:1, 33:34], ALU.mult), deps=[t_b2b, t_wb], inc="sv")
        t_c3 = p.emit("vector", lambda e: e.tensor_add(
            BR[0:1, 4096:4096 + D], BR[0:1, 4096:4096 + D],
            BR[0:1, 6144:6144 + D]), deps=[t_c1, t_c2], inc="sv")
        t_b2s = p.dma("gpsimd", rowscr[1:2, 0:D], BR[0:1, 4096:4096 + D],
                      deps=[t_c3])
        t_ws = p.dma("gpsimd", rowscr[2:3, 0:2], RW[0:1, 32:34],
                     deps=[t_wa, t_wb])
        fst = p.fence("gpsimd", "sg")
        t_b2r = p.dma("gpsimd", AR[0:1, 6656:6656 + D], rowscr[1:2, 0:D],
                      deps=[fst] + state["qt_free"])
        t_b1a = p.dma("gpsimd", CB[:, 16:16 + FCH],
                      lambda: I["L0_ffn_b1"][cell["rA"], :]
                      .rearrange("(c p) -> p c", p=128), deps=[t_rv],
                      allow_slow_non_contiguous=True)
        t_b1b = p.dma("gpsimd", CB[:, 27:27 + FCH],
                      lambda: I["L0_ffn_b1"][cell["rB"], :]
                      .rearrange("(c p) -> p c", p=128), deps=[t_rv],
                      allow_slow_non_contiguous=True)
        t_wac = p.dma("gpsimd", CB[:, 40:41],
                      rowscr[2:3, 0:1].to_broadcast([128, 1]), deps=[fst],
                      allow_slow_non_contiguous=True)
        t_wbc = p.dma("gpsimd", CB[:, 41:42],
                      rowscr[2:3, 1:2].to_broadcast([128, 1]), deps=[fst],
                      allow_slow_non_contiguous=True)
        fsm = p.fence("gpsimd", "sg")
        t_b2r = t_b1a = t_b1b = t_wac = t_wbc = fsm
        w1c_toks = []
        for (r0, r1) in ((0, 768), (768, DFF)):
            for si, rn in ((0, "rA"), (1, "rB")):
                p.dma("gpsimd", w1sel[call][si, r0:r1, :],
                      (lambda rn=rn, r0=r0, r1=r1:
                       I["L0_ffn_W1T"][cell[rn], r0:r1, :]),
                      deps=[t_rv, state.get("stream_guard")], sem="sgc")
            w1c_toks.append(p.fence("gpsimd", "sgc"))
        for si, rn in ((0, "rA"), (1, "rB")):
            p.dma("gpsimd", w2sel[call][si, :, :],
                  (lambda rn=rn: I["L0_ffn_W2T"][cell[rn], :, :]),
                  deps=[t_rv], sem="sgc")
        f_w2c = p.fence("gpsimd", "sgc")
        if _part < 4:
            return
        grps = [(0, 6), (6, 11)]
        if _os.environ.get("L0FFN_LIM"):
            _lim = _os.environ["L0FFN_LIM"]
            grps = grps[:1] if "g" in _lim else grps
        res_toks = {t: [] for t in range(NT)}
        ht_free = {}
        qt_guard = state["qt_free"]
        last_s1 = None
        pb_guard = {0: t_l, 4: ev[0], 5: ev[1], 6: ev[2], 7: ev[3]}
        w1_guard = {}
        w2_prev = {}
        _tts = range(1) if _os.environ.get("L0FFN_LIM", "").count("t") else range(2)
        _lv = _os.environ.get("L0FFN_LIM", "")
        _ss = (0,) if "s" in _lv else ((1,) if "B" in _lv else (0, 1))
        for tt in _tts:
            h_done = {}
            for s in _ss:
                rn = "rA" if s == 0 else "rB"
                for (g0, g1) in grps:
                    nfc = g1 - g0
                    mm_last = {}
                    w1_t = {}
                    for dc in range(DCH):
                        bslot = dc % 2
                        buf = 4096 + bslot * 768
                        w1_t[dc] = p.dmat(
                            "sync", AR[:, buf:buf + nfc * 128],
                            w1sel[call][s, g0 * 128:g1 * 128,
                                        dc * 128:(dc + 1) * 128],
                            deps=[w1c_toks[0 if g0 == 0 else 1],
                                  w1_guard.get(bslot),
                                  state.get("stream_guard")],
                            sem="wp%d" % bslot)
                        last = None
                        for fc in range(g0, g1):
                            last = p.emit(
                                "tensor",
                                lambda e, dc=dc, fc=fc, g0=g0, buf=buf, tt=tt:
                                e.matmul(
                                    PB[fc - g0][:, :],
                                    AR[:, buf + (fc - g0) * 128:
                                       buf + (fc - g0) * 128 + 128],
                                    LT[:, dc * TPC + tt * 512:
                                       dc * TPC + tt * 512 + 512],
                                    start=(dc == 0), stop=(dc == DCH - 1)),
                                deps=[w1_t[dc], lt_toks[dc],
                                      ht_free.get((s, fc)),
                                      pb_guard.get(fc - g0) if dc == 0
                                      else None], inc="st")
                        mm_last[dc] = last
                        w1_guard[bslot] = last
                        last_s1 = last
                    for fc in range(g0, g1):
                        bcol = CB[:, (16 if s == 0 else 27) + fc:
                                  (17 if s == 0 else 28) + fc]
                        ge = p.emit(
                            "scalar",
                            lambda e, fc=fc, g0=g0, s=s, bcol=bcol:
                            e.activation(
                                out=QT[:, s * 5632 + fc * 512:
                                       s * 5632 + fc * 512 + 512],
                                in_=PB[fc - g0][:, :], func=AF.Gelu,
                                bias=bcol),
                            deps=[mm_last[DCH - 1], t_b1a, t_b1b] + qt_guard,
                            inc="ss")
                        pb_guard[fc - g0] = ge
                        wm = p.emit(
                            "vector",
                            lambda e, fc=fc, s=s: e.tensor_single_scalar(
                                QT[:, s * 5632 + fc * 512:
                                   s * 5632 + fc * 512 + 512],
                                QT[:, s * 5632 + fc * 512:
                                   s * 5632 + fc * 512 + 512],
                                CB[:, 40 + s:41 + s], ALU.mult),
                            deps=[ge, t_wac, t_wbc], inc="sv")
                        h_done[(s, fc)] = wm
            for d2s in range(4 if _part >= 5 else 0):
                last_by_tg = {}
                for s in (0, 1):
                    rn = "rA" if s == 0 else "rB"
                    for fc in range(FCH):
                        bslot = (s * FCH + fc) % 2
                        buf2 = 5632 + bslot * 512
                        w2_t = p.dmat(
                            "sync", AR[:, buf2:buf2 + 512],
                            w2sel[call][s, d2s * 512:(d2s + 1) * 512,
                                        fc * 128:(fc + 1) * 128],
                            deps=[f_w2c, w2_prev.get(bslot)] + ev,
                            sem="wp%d" % (2 + bslot))
                        for tg in range(4):
                            tk = p.emit(
                                "tensor",
                                lambda e, s=s, fc=fc, tg=tg, buf2=buf2:
                                e.matmul(
                                    PB[4 + tg][:, :],
                                    QT[:, s * 5632 + fc * 512 + tg * 128:
                                       s * 5632 + fc * 512 + tg * 128 + 128],
                                    AR[:, buf2:buf2 + 512],
                                    start=(s == 0 and fc == 0), stop=False),
                                deps=[w2_t, h_done[(s, fc)],
                                      pb_guard.get(4 + tg)
                                      if (s == 0 and fc == 0) else None],
                                inc="st")
                            last_by_tg[tg] = tk
                        w2_prev[bslot] = last_by_tg[3]
                        if d2s == 3 and tt == 1:
                            pass
                        if d2s == 3:
                            ht_free[(s, fc)] = last_by_tg[3]
                for tg in range(4):
                    bt = p.emit(
                        "tensor", lambda e, tg=tg, d2s=d2s: e.matmul(
                            PB[4 + tg][:, :], ONEB[:, :],
                            AR[0:1, 6656 + d2s * 512:6656 + (d2s + 1) * 512],
                            start=False, stop=True),
                        deps=[last_by_tg[tg], t_b2r, t_onesb], inc="st")
                    tile = tt * 4 + tg
                    xs = X[:, tile * D + d2s * 512: tile * D + (d2s + 1) * 512]
                    rv = p.emit(
                        "vector", lambda e, tg=tg, xs=xs, tile=tile, d2s=d2s:
                        e.scalar_tensor_tensor(
                            xs, PB[4 + tg][:, :], 0.5, xs, ALU.mult, ALU.add,
                            accum_out=STT[:, S_ACC + tile * 4 + d2s:
                                          S_ACC + tile * 4 + d2s + 1]),
                        deps=[bt], inc="sv")
                    pb_guard[4 + tg] = rv
                    res_toks[tile].append(rv)
        if _part >= 5:
            for t in range(NT):
                finalize_sums(t, res_toks[t], 4)
        state["lt_free"] = [last_s1]
        state["qt_free"] = [v for v in ht_free.values()]
        state["stream_guard"] = ("st", p.cnt["st"])

    def emit_l0attn():
        p.dma("gpsimd", wc_bf[:, :], I["L1_ffn_Wc"][:, :], sem="sgw2")
        p.dma("gpsimd", wd_bf[:, :], I["L1_ffn_Wd"][:, :], sem="sgw2")
        p.dma("gpsimd", wu_bf[:, :], I["L1_ffn_Wu"][:, :], sem="sgw2")
        imgs_tok["wcdu"] = ("sgw2", p.cnt["sgw2"])
        lt_toks = ln_prologue(rms=False)
        t_gm = p.dma("sync", STT[:, S_GAM:S_GAM + 16],
                     I["L0_attn_ln_gamma"][0, :].rearrange("(c p) -> p c", p=128),
                     allow_slow_non_contiguous=True)
        t_bt = p.dma("sync", STT[:, S_BET:S_BET + 16],
                     I["L0_attn_ln_beta"][0, :].rearrange("(c p) -> p c", p=128),
                     allow_slow_non_contiguous=True)
        t_be = p.dma("sync", RW[0:1, 112:112 + NHEAD], I["L0_attn_beta"][:, :])
        f_sm = p.fence("sync", "sd")
        t_gm = t_bt = t_be = f_sm
        t_p2 = p.emit("scalar", lambda e: e.activation(
            out=RW[0:1, 96:96 + NHEAD], in_=RW[0:1, 112:112 + NHEAD],
            func=AF.Exp, scale=-LN2), deps=[t_be], inc="ss")
        t_lam = p.emit("vector", lambda e: e.tensor_scalar(
            RW[0:1, 64:64 + NHEAD], RW[0:1, 96:96 + NHEAD], -1.0, 1.0,
            ALU.mult, ALU.add), deps=[t_p2], inc="sv")
        t_cl1 = p.emit("vector", lambda e: e.tensor_scalar_min(
            RW[0:1, 64:64 + NHEAD], RW[0:1, 64:64 + NHEAD], 1.0 - 1e-9),
            deps=[t_lam], inc="sv")
        t_cl2 = p.emit("vector", lambda e: e.tensor_scalar_max(
            RW[0:1, 64:64 + NHEAD], RW[0:1, 64:64 + NHEAD], F32_TINY),
            deps=[t_cl1], inc="sv")
        t_ll = p.emit("scalar", lambda e: e.activation(
            out=RW[0:1, 80:80 + NHEAD], in_=RW[0:1, 64:64 + NHEAD],
            func=AF.Ln), deps=[t_cl2], inc="ss")
        t_ls = p.dma("gpsimd", rowscr[3:4, 0:NHEAD], RW[0:1, 64:64 + NHEAD],
                     deps=[t_cl2])
        f_ls = p.fence("gpsimd", "sg")
        t_lb = p.dma("gpsimd", CB[:, 8:8 + NHEAD],
                     rowscr[3:4, 0:NHEAD].to_broadcast([128, NHEAD]),
                     deps=[f_ls], allow_slow_non_contiguous=True)
        t_lb = p.fence("gpsimd", "sg")

        vt_toks = {}
        qt_guard = state["qt_free"]
        sbufg = {0: None, 1: None, 2: None}
        sq_state = {"j": 0}

        def run_strip_phase(img, imgtok, compute_set, nsets=4, entry=()):
            loaded = {}

            def load(j, extra=()):
                buf = sq_state["j"] % 3
                sq_state["j"] += 1
                base = buf * 4096
                os_, h = divmod(j, 2)
                for k in range(8):
                    dc = h * 8 + k
                    p.dmat("sync",
                           AR[:, base + k * 512: base + (k + 1) * 512],
                           img[os_ * 512:(os_ + 1) * 512,
                               dc * 128:(dc + 1) * 128],
                           deps=[imgtok, sbufg[buf]] + list(extra))
                loaded[j] = (buf, base, p.fence("sync", "sd"))
            for j in range(min(3, nsets * 2)):
                load(j, extra=entry)
            nxt = 3
            for os_ in range(nsets):
                b0, base0, f0 = loaded[2 * os_]
                b1, base1, f1 = loaded[2 * os_ + 1]

                def chunk(dc, base0=base0, base1=base1, f0=f0, f1=f1):
                    if dc < 8:
                        return base0 + dc * 512, f0
                    return base1 + (dc - 8) * 512, f1
                lastmm = compute_set(os_, chunk)
                sbufg[b0] = lastmm
                sbufg[b1] = lastmm
                while nxt < nsets * 2 and nxt <= 2 * os_ + 4:
                    load(nxt)
                    nxt += 1

        def vt_set(os_, chunk):
            last = None
            for oc in range(os_ * 4, os_ * 4 + 4):
                for T in range(2):
                    pb = PB[(oc * 2 + T) % 4]
                    for dc in range(DCH):
                        col, ftok = chunk(dc)
                        last = p.emit(
                            "tensor",
                            lambda e, col=col, oc=oc, T=T, dc=dc, os_=os_,
                            pb=pb:
                            e.matmul(pb[:, :],
                                     AR[:, col + (oc - os_ * 4) * 128:
                                        col + (oc - os_ * 4) * 128 + 128],
                                     LT[:, dc * TPC + T * 512:
                                        dc * TPC + T * 512 + 512],
                                     start=(dc == 0), stop=(dc == DCH - 1)),
                            deps=[ftok, lt_toks[dc],
                                  vt_toks.get((oc - 2, T))], inc="st")
                    ev = p.emit("vector", lambda e, oc=oc, T=T, pb=pb:
                                e.tensor_copy(QT[:, oc * TPC + T * 512:
                                                 oc * TPC + T * 512 + 512],
                                              pb[:, :]),
                                deps=[last] + qt_guard, inc="sv")
                    vt_toks[(oc, T)] = ev
            return last

        run_strip_phase(wv_bf, t_wv, vt_set, entry=qt_guard)
        p1_toks = []
        for oc in range(NHEAD):
            prev = None
            for T in range(2):
                sl = T % 2
                dst = BR[:, 8192 + sl * 512:8192 + (sl + 1) * 512]
                ini = 0.0 if T == 0 else BR[:, 8192 + (1 - sl) * 512 + 511:
                                            8192 + (1 - sl) * 512 + 512]
                prev = p.emit(
                    "vector", lambda e, oc=oc, T=T, dst=dst, ini=ini:
                    e.tensor_tensor_scan(
                        out=dst,
                        data0=CB[:, 8 + oc:9 + oc].to_broadcast([128, 512]),
                        data1=QT[:, oc * TPC + T * 512:oc * TPC + T * 512 + 512],
                        initial=ini, op0=ALU.mult, op1=ALU.add),
                    deps=[vt_toks[(oc, T)], t_lb, prev], inc="sv")
            cp = p.emit("vector", lambda e, oc=oc: e.tensor_copy(
                CB[:, 24 + oc:25 + oc], BR[:, 8192 + 512 + 511:8192 + 1024]),
                deps=[prev], inc="sv")
            p1_toks.append(cp)

        t_ssav = p.dma("gpsimd", cc_s_in[:, :], CB[:, 24:24 + NHEAD],
                       deps=p1_toks)
        f_sv = p.fence("gpsimd", "sg")
        t_ag = p.emit("gpsimd", lambda e: e.collective_compute(
            "AllGather", ALU.bypass, replica_groups=[list(range(NCORES))],
            ins=[cc_s_in[:, :]], outs=[cc_s_out[:, :]]),
            deps=[f_sv], inc="scc")
        t_csb = p.dma("sync",
                      BR[:, 9216:9216 + 128].rearrange("p (j h) -> p j h", j=8),
                      cc_s_out.rearrange("(j p) h -> p j h", p=128),
                      deps=[t_ag], allow_slow_non_contiguous=True)
        t_csb = p.fence("sync", "sd")
        t_llb = p.dma("gpsimd", rowscr[3:4, 16:16 + NHEAD], RW[0:1, 80:80 + NHEAD],
                      deps=[t_ll])
        f_llb = p.fence("gpsimd", "sg")
        t_ll8 = p.dma("gpsimd", RW[0:8, 144:144 + NHEAD],
                      rowscr[3:4, 16:16 + NHEAD].to_broadcast([8, NHEAD]),
                      deps=[f_llb], allow_slow_non_contiguous=True)
        t_ll8 = p.fence("gpsimd", "sg")
        t_wm = p.emit("vector", lambda e: e.tensor_single_scalar(
            RW[0:8, 144:144 + NHEAD], RW[0:8, 144:144 + NHEAD],
            CB[0:8, 0:1], ALU.mult), deps=[t_ll8, t_coef], inc="sv")
        t_wme = p.emit("scalar", lambda e: e.activation(
            out=RW[0:8, 144:144 + NHEAD], in_=RW[0:8, 144:144 + NHEAD],
            func=AF.Exp), deps=[t_wm], inc="ss")
        t_wms = p.dma("gpsimd", rowscr[4:5, 0:128], RW[0:8, 144:144 + NHEAD],
                      deps=[t_wme])
        f_wms = p.fence("gpsimd", "sg")
        t_wmb = p.dma("gpsimd", BR[:, 9344:9344 + 128],
                      rowscr[4:5, 0:128].to_broadcast([128, 128]),
                      deps=[f_wms], allow_slow_non_contiguous=True)
        t_wmb = p.fence("gpsimd", "sg")
        t_pr = p.emit("vector", lambda e: e.tensor_mul(
            BR[:, 9216:9216 + 128], BR[:, 9216:9216 + 128],
            BR[:, 9344:9344 + 128]), deps=[t_csb, t_wmb], inc="sv")
        t_car = p.emit("vector", lambda e: e.tensor_reduce(
            CB[:, 44:44 + NHEAD],
            BR[:, 9216:9216 + 128].rearrange("p (j h) -> p h j", j=8),
            axis=AX, op=ALU.add), deps=[t_pr], inc="sv")

        pt_toks = {}
        ret_end_prev = {}
        for T in range(2):
            s_toks = {}
            ret_end = {}
            sum_last = None
            sq_mm_last = None
            sq_act_last = None
            qg_state = {"sum": sum_last, "sqa": sq_act_last,
                        "sqm": sq_mm_last}

            def q_set(os_, chunk, T=T, s_toks=s_toks, ret_end=ret_end,
                      ret_end_prev=ret_end_prev):
                last = None
                for oc in range(os_ * 4, os_ * 4 + 4):
                    sl = oc % 2
                    dst = BR[:, 8192 + sl * 512:8192 + (sl + 1) * 512]
                    ini = (CB[:, 44 + oc:45 + oc] if T == 0
                           else CB[:, 24 + oc:25 + oc])
                    ideps = ([t_car] if T == 0 else [ret_end_prev[oc]])
                    sc = p.emit(
                        "vector", lambda e, oc=oc, T=T, dst=dst, ini=ini:
                        e.tensor_tensor_scan(
                            out=dst,
                            data0=CB[:, 8 + oc:9 + oc].to_broadcast([128, 512]),
                            data1=QT[:, oc * TPC + T * 512:
                                     oc * TPC + T * 512 + 512],
                            initial=ini, op0=ALU.mult, op1=ALU.add),
                        deps=ideps + [vt_toks[(oc, T)],
                                      s_toks.get(oc - 2)], inc="sv")
                    sv = p.emit("vector", lambda e, oc=oc, dst=dst:
                                e.tensor_copy(CB[:, 24 + oc:25 + oc],
                                              dst[:, 511:512]),
                                deps=[sc], inc="sv")
                    ret_end[oc] = sv
                    pb = PB[4 + (oc % 2)]
                    for dc in range(DCH):
                        col, ftok = chunk(dc)
                        last = p.emit(
                            "tensor",
                            lambda e, col=col, oc=oc, T=T, dc=dc, os_=os_,
                            pb=pb:
                            e.matmul(pb[:, :],
                                     AR[:, col + (oc - os_ * 4) * 128:
                                        col + (oc - os_ * 4) * 128 + 128],
                                     LT[:, dc * TPC + T * 512:
                                        dc * TPC + T * 512 + 512],
                                     start=(dc == 0), stop=(dc == DCH - 1)),
                            deps=[ftok, lt_toks[dc], s_toks.get(oc - 2),
                                  pt_toks.get((NHEAD - 2 + (oc % 2), T - 1))],
                            inc="st")
                    stk = p.emit("vector", lambda e, oc=oc, pb=pb, dst=dst:
                                 e.tensor_mul(BR[:, oc * 512:(oc + 1) * 512],
                                              pb[:, :], dst),
                                 deps=[last, sc], inc="sv")
                    s_toks[oc] = stk
                    qg_state["sum"] = p.emit(
                        "tensor", lambda e, oc=oc: e.matmul(
                            PB[6][0:1, :], ONEC[:, :],
                            BR[:, oc * 512:(oc + 1) * 512],
                            start=(oc == 0), stop=(oc == NHEAD - 1)),
                        deps=[stk, t_ones], inc="st")
                    qg_state["sqa"] = p.emit(
                        "scalar", lambda e, oc=oc: e.activation(
                            out=BR[:, 9216:9216 + 512],
                            in_=BR[:, oc * 512:(oc + 1) * 512],
                            func=AF.Square),
                        deps=[stk, qg_state["sqm"]], inc="ss")
                    qg_state["sqm"] = p.emit(
                        "tensor", lambda e, oc=oc: e.matmul(
                            PB[7][0:1, :], ONEC[:, :], BR[:, 9216:9216 + 512],
                            start=(oc == 0), stop=(oc == NHEAD - 1)),
                        deps=[qg_state["sqa"], t_ones], inc="st")
                return last

            run_strip_phase(wq_bf, t_wq, q_set)
            sum_last = qg_state["sum"]
            sq_act_last = qg_state["sqa"]
            sq_mm_last = qg_state["sqm"]
            ret_end_prev = ret_end
            t_su = p.emit("vector", lambda e: e.tensor_scalar_mul(
                RW[0:1, 256:768], PB[6][0:1, :], 1.0 / D),
                deps=[sum_last], inc="sv")
            t_s2 = p.emit("vector", lambda e: e.tensor_scalar_mul(
                RW[0:1, 768:1280], PB[7][0:1, :], 1.0 / D),
                deps=[sq_mm_last], inc="sv")
            t_m2 = p.emit("vector", lambda e: e.tensor_mul(
                RW[0:1, 1280:1792], RW[0:1, 256:768], RW[0:1, 256:768]),
                deps=[t_su], inc="sv")
            t_va = p.emit("vector", lambda e: e.tensor_sub(
                RW[0:1, 768:1280], RW[0:1, 768:1280], RW[0:1, 1280:1792]),
                deps=[t_s2, t_m2], inc="sv")
            t_lv = p.emit("scalar", lambda e: e.activation(
                out=RW[0:1, 1280:1792], in_=RW[0:1, 768:1280], func=AF.Ln,
                bias=LN_EPS), deps=[t_va, t_c0], inc="ss")
            t_rr = p.emit("scalar", lambda e: e.activation(
                out=RW[0:1, 1280:1792], in_=RW[0:1, 1280:1792], func=AF.Exp,
                scale=-0.5), deps=[t_lv], inc="ss")
            t_nm = p.emit("vector", lambda e: e.tensor_scalar_mul(
                RW[0:1, 256:768], RW[0:1, 256:768], -1.0),
                deps=[t_su, t_m2], inc="sv")
            t_nb = p.emit("vector", lambda e: e.tensor_mul(
                RW[0:1, 256:768], RW[0:1, 256:768], RW[0:1, 1280:1792]),
                deps=[t_nm, t_rr], inc="sv")
            t_ab = p.dma("gpsimd", rowscr[6:7, 0:512], RW[0:1, 1280:1792],
                         deps=[t_nb])
            t_ab2 = p.dma("gpsimd", rowscr[6:7, 512:1024], RW[0:1, 256:768],
                          deps=[t_nb])
            f_ab = p.fence("gpsimd", "sg")
            t_abb = p.dma("gpsimd", BR[:, 9728:9728 + 512],
                          rowscr[6:7, 0:512].to_broadcast([128, 512]),
                          deps=[f_ab], allow_slow_non_contiguous=True)
            t_bbb = p.dma("gpsimd", BR[:, 10240:10240 + 512],
                          rowscr[6:7, 512:1024].to_broadcast([128, 512]),
                          deps=[f_ab], allow_slow_non_contiguous=True)
            f_ab2 = p.fence("gpsimd", "sg")
            t_abb = t_bbb = f_ab2
            ln_toks = {}
            for oc in range(NHEAD):
                a1 = p.emit("vector", lambda e, oc=oc: e.tensor_mul(
                    BR[:, oc * 512:(oc + 1) * 512],
                    BR[:, oc * 512:(oc + 1) * 512], BR[:, 9728:9728 + 512]),
                    deps=[t_abb, s_toks[oc]], inc="sv")
                a2 = p.emit("vector", lambda e, oc=oc: e.tensor_add(
                    BR[:, oc * 512:(oc + 1) * 512],
                    BR[:, oc * 512:(oc + 1) * 512], BR[:, 10240:10240 + 512]),
                    deps=[t_bbb, a1], inc="sv")
                a3 = p.emit("vector", lambda e, oc=oc: e.tensor_scalar(
                    BR[:, oc * 512:(oc + 1) * 512],
                    BR[:, oc * 512:(oc + 1) * 512],
                    STT[:, S_GAM + oc:S_GAM + oc + 1],
                    STT[:, S_BET + oc:S_BET + oc + 1], ALU.mult, ALU.add),
                    deps=[a2, t_gm, t_bt], inc="sv")
                ln_toks[oc] = a3
            def g_set(os_, chunk, T=T, ln_toks=ln_toks,
                      sq_mm_last=sq_mm_last):
                last = None
                for oc in range(os_ * 4, os_ * 4 + 4):
                    pb = PB[4 + (oc % 2)]
                    for dc in range(DCH):
                        col, ftok = chunk(dc)
                        last = p.emit(
                            "tensor",
                            lambda e, col=col, oc=oc, T=T, dc=dc, os_=os_,
                            pb=pb:
                            e.matmul(pb[:, :],
                                     AR[:, col + (oc - os_ * 4) * 128:
                                        col + (oc - os_ * 4) * 128 + 128],
                                     LT[:, dc * TPC + T * 512:
                                        dc * TPC + T * 512 + 512],
                                     start=(dc == 0), stop=(dc == DCH - 1)),
                            deps=[ftok, pt_toks.get((oc - 2, T)),
                                  sq_mm_last], inc="st")
                    gsil = p.emit("scalar", lambda e, pb=pb:
                                  e.activation(out=BR[:, 9216:9216 + 512],
                                               in_=pb[:, :], func=AF.Silu),
                                  deps=[last, sq_mm_last,
                                        pt_toks.get((oc - 1, T)) if oc >= 1
                                        else pt_toks.get((NHEAD - 1, T - 1))],
                                  inc="ss")
                    ptk = p.emit(
                        "vector", lambda e, oc=oc, T=T: e.tensor_mul(
                            QT[:, oc * TPC + T * 512:oc * TPC + T * 512 + 512],
                            BR[:, oc * 512:(oc + 1) * 512],
                            BR[:, 9216:9216 + 512]),
                        deps=[gsil, ln_toks[oc]], inc="sv")
                    pt_toks[(oc, T)] = ptk
                return last

            run_strip_phase(wg_bf, t_wg, g_set)
        all_pt = [max((pt_toks[(oc, T)] for oc in range(NHEAD)
                       for T in range(2)), key=lambda tk: tk[1])]
        res_toks = {t: [] for t in range(NT)}
        fin_state = {"last": None}

        def fin_set(es, chunk):
            last = None
            for tg in range(NT):
                pb = PB[tg % 4]
                for dc in range(DCH):
                    col, ftok = chunk(dc)
                    last = p.emit(
                        "tensor", lambda e, col=col, es=es, tg=tg, dc=dc,
                        pb=pb:
                        e.matmul(pb[:, :],
                                 QT[:, dc * TPC + tg * 128:
                                    dc * TPC + tg * 128 + 128],
                                 AR[:, col:col + 512],
                                 start=(dc == 0), stop=(dc == DCH - 1)),
                        deps=[ftok,
                              res_toks[tg - 4][es] if tg >= 4 else
                              (res_toks[tg + 4][es - 1] if es > 0 else None)]
                        + all_pt, inc="st")
                xs = X[:, tg * D + es * 512: tg * D + (es + 1) * 512]
                rv = p.emit(
                    "vector", lambda e, xs=xs, tg=tg, es=es, pb=pb:
                    e.scalar_tensor_tensor(
                        xs, pb[:, :], 1.0, xs, ALU.mult, ALU.add,
                        accum_out=STT[:, S_ACC + tg * 4 + es:
                                      S_ACC + tg * 4 + es + 1]),
                    deps=[last], inc="sv")
                res_toks[tg].append(rv)
                fin_state["last"] = last
            return last

        run_strip_phase(wo0_bf, t_wo0, fin_set)
        last_fin = fin_state["last"]
        for t in range(NT):
            finalize_sums(t, res_toks[t], 4)
        state["lt_free"] = [last_fin]
        state["qt_free"] = [last_fin]
        state["stream_guard"] = ("st", p.cnt["st"])

    def emit_l1attn():
        lt_toks = ln_prologue(rms=False)
        BIA = 8192
        wimg_tok = imgs_tok["l1a"]
        wp_prev = {b: state.get("stream_guard") for b in range(4)}

        def proj_phase(wname, bias_name, bias_dep, psum_guards, evac):
            t_bias = p.dma("gpsimd", AR[0:1, BIA:BIA + D], I[bias_name][:, :],
                           deps=bias_dep)
            f_bias = p.fence("gpsimd", "sg")
            ev_toks = {}
            bt_toks = {}
            for seg in range(4):
                for dc in range(DCH):
                    bslot = dc % 4
                    buf = bslot * 512
                    wtk = p.dmat("sync", AR[:, buf:buf + 512],
                                 l1_imgs[wname][seg * 512:(seg + 1) * 512,
                                                dc * 128:(dc + 1) * 128],
                                 deps=[wimg_tok, wp_prev.get(bslot)],
                                 sem="wp%d" % bslot)
                    tk = None
                    for t in range(NT):
                        guard = (psum_guards.get(t) if seg == 0
                                 else ev_toks.get((t, seg - 1)))
                        tk = p.emit(
                            "tensor",
                            lambda e, dc=dc, t=t, buf=buf:
                            e.matmul(PB[t][:, :],
                                     LT[:, dc * TPC + t * 128:
                                        dc * TPC + t * 128 + 128],
                                     AR[:, buf:buf + 512],
                                     start=(dc == 0), stop=False),
                            deps=[wtk, lt_toks[dc], guard], inc="st")
                    wp_prev[bslot] = tk
                for t in range(NT):
                    bt = p.emit(
                        "tensor", lambda e, t=t, seg=seg:
                        e.matmul(PB[t][:, :], ONEB[:, :],
                                 AR[0:1, BIA + seg * 512:
                                    BIA + (seg + 1) * 512],
                                 start=False, stop=True),
                        deps=[f_bias, t_onesb], inc="st")
                    bt_toks[(t, seg)] = bt
                    ev_toks[(t, seg)] = evac(t, seg, bt)
            return ev_toks, bt_toks

        k_store = {}
        kst = {"slot": None}

        def evac_k(t, seg, bt):
            mn = p.emit("vector", lambda e, t=t: e.tensor_scalar_min(
                BR[:, 0:512], PB[t][:, :], 0.0),
                deps=[bt, kst["slot"]], inc="sv")
            ex = p.emit("scalar", lambda e: e.activation(
                out=BR[:, 0:512], in_=BR[:, 0:512], func=AF.Exp),
                deps=[mn], inc="ss")
            mx = p.emit("vector", lambda e, t=t: e.tensor_scalar_max(
                BR[:, 512:1024], PB[t][:, :], 0.0),
                deps=[ex], inc="sv")
            kb = p.emit("vector", lambda e: e.scalar_tensor_tensor(
                AR[:, 2048:2560], BR[:, 0:512], -1.0, BR[:, 512:1024],
                ALU.add, ALU.add), deps=[mx], inc="sv")
            st = p.dma("scalar", scr2[t * 128:(t + 1) * 128,
                                      seg * 512:(seg + 1) * 512],
                       AR[:, 2048:2560], deps=[kb] + state["scr2_free"])
            state["scr2_free"] = []
            k_store[(t, seg)] = st
            kst["slot"] = st
            return mx

        kev, kbt = proj_phase("L1_attn_Wqk", "L1_attn_bqk",
                              state["qt_free"], {}, evac_k)
        f_k = p.fence("scalar", "so")

        v_store = {}
        vst = {"slot": None}

        def evac_v(t, seg, bt):
            sv_ = p.emit("scalar", lambda e, t=t: e.activation(
                out=AR[:, 2560:3072], in_=PB[t][:, :], func=AF.Silu),
                deps=[bt, vst["slot"]], inc="ss")
            st = p.dma("scalar", lnscr[t * 128:(t + 1) * 128,
                                       seg * 512:(seg + 1) * 512],
                       AR[:, 2560:3072], deps=[sv_] + state["lnscr_free"])
            state["lnscr_free"] = []
            v_store[(t, seg)] = st
            vst["slot"] = st
            return sv_

        vev, vbt = proj_phase("L1_attn_Wv", "L1_attn_bv",
                              [kbt[(NT - 1, 3)]],
                              {t: kev[(t, 3)] for t in range(NT)}, evac_v)
        f_v = p.fence("scalar", "so")

        for dc in range(DCH):
            p.dmat("sync", QT[:, dc * TPC:(dc + 1) * TPC],
                   scr2[:, dc * 128:(dc + 1) * 128],
                   deps=[f_k] + state["qt_free"])
        f_qt = p.fence("sync", "sd")
        kv_mm = {}
        khb_t = {}
        vhb_t = {}
        acc_tok = {}
        for t in range(NT):
            khb_t[t] = p.dma("sync", AR[:, 2048:2048 + D],
                             scr2[t * 128:(t + 1) * 128, :],
                             deps=[f_k, f_v, kv_mm.get((t - 1, NHEAD - 1))])
            vhb_t[t] = p.dma("sync", AR[:, 4096:4096 + D],
                             lnscr[t * 128:(t + 1) * 128, :],
                             deps=[f_v, kv_mm.get((t - 1, NHEAD - 1))])
            for h in range(NHEAD):
                kv_mm[(t, h)] = p.emit(
                    "tensor", lambda e, t=t, h=h:
                    e.matmul(PB[4 + h // 4][:, (h % 4) * 128:(h % 4) * 128 + 128],
                             AR[:, 2048 + h * 128:2048 + (h + 1) * 128],
                             AR[:, 4096 + h * 128:4096 + (h + 1) * 128],
                             start=True, stop=True),
                    deps=[khb_t[t], vhb_t[t],
                          acc_tok.get((t - 1, h // 4))], inc="st")
            for b in range(4):
                dst = BR[:, 2048 + b * 512:2048 + (b + 1) * 512]
                if t == 0:
                    acc_tok[(t, b)] = p.emit(
                        "vector", lambda e, dst=dst, b=b: e.tensor_copy(
                            dst, PB[4 + b][:, :]),
                        deps=[kv_mm[(t, 4 * b + 3)]], inc="sv")
                else:
                    acc_tok[(t, b)] = p.emit(
                        "vector", lambda e, dst=dst, b=b: e.tensor_add(
                            dst, dst, PB[4 + b][:, :]),
                        deps=[kv_mm[(t, 4 * b + 3)],
                              acc_tok[(t - 1, b)]], inc="sv")
        kv_last = kv_mm[(NT - 1, NHEAD - 1)]
        kv_ev = [acc_tok[(NT - 1, b)] for b in range(4)]
        t_kvs = p.dma("gpsimd", cc_kv_in[:, :], BR[:, 2048:2048 + D],
                      deps=kv_ev)
        f_kvs = p.fence("gpsimd", "sg")
        t_kvar = p.emit("gpsimd", lambda e: e.collective_compute(
            "AllReduce", ALU.add, replica_groups=[list(range(NCORES))],
            ins=[cc_kv_in[:, :]], outs=[cc_kv_out[:, :]]),
            deps=[f_kvs], inc="scc")
        t_kvb = p.dma("gpsimd", AR[:, 6144:6144 + D], cc_kv_out[:, :],
                      deps=[t_kvar])
        f_kvb = p.fence("gpsimd", "sg")
        u_store = {}
        ust = {"slot": None}

        def evac_u(t, seg, bt):
            su = p.emit("scalar", lambda e, t=t: e.activation(
                out=AR[:, 2560:3072], in_=PB[t][:, :], func=AF.Silu),
                deps=[bt, ust["slot"]], inc="ss")
            st = p.dma("scalar", lnscr[t * 128:(t + 1) * 128,
                                       seg * 512:(seg + 1) * 512],
                       AR[:, 2560:3072], deps=[su, kv_last])
            u_store[(t, seg)] = st
            ust["slot"] = st
            return su

        u_guards = {t: (kv_ev[t - 4] if t >= 4 else vev[(t, 3)])
                    for t in range(NT)}
        uev, ubt = proj_phase("L1_attn_Wu", "L1_attn_bu",
                              [vbt[(NT - 1, 3)]], u_guards, evac_u)
        f_u = p.fence("scalar", "so")
        t_gb = p.dma("gpsimd", BR[:, 4096:4096 + D],
                     I["L1_attn_ln_g"].to_broadcast([128, D]),
                     allow_slow_non_contiguous=True, deps=[kv_last])
        t_bb = p.dma("gpsimd", BR[:, 6144:6144 + D],
                     I["L1_attn_ln_b"].to_broadcast([128, D]),
                     allow_slow_non_contiguous=True)
        f_gbb = p.fence("gpsimd", "sg")
        prod_store = {}
        prev_ev = {}
        t_uf_prev = None
        t_pd_prev = None
        for t in range(NT):
            pb = PB[t % 2]
            seg_toks = []
            for seg in range(4):
                last = None
                for hh in range(4):
                    h = seg * 4 + hh
                    last = p.emit(
                        "tensor", lambda e, t=t, h=h, hh=hh, pb=pb:
                        e.matmul(pb[:, hh * 128:(hh + 1) * 128],
                                 QT[:, h * TPC + t * 128:h * TPC + t * 128 + 128],
                                 AR[:, 6144 + h * 128:6144 + (h + 1) * 128],
                                 start=True, stop=True),
                        deps=[f_qt, f_kvb,
                              uev[(t, 3)] if t < 2 else
                              prev_ev.get((t - 2, seg)),
                              seg_toks[seg - 1] if seg > 0 else
                              prev_ev.get((t - 2, 3))], inc="st")
                ev = p.emit("scalar", lambda e, t=t, seg=seg, pb=pb:
                            e.activation(out=BR[:, seg * 512:(seg + 1) * 512],
                                         in_=pb[:, :], func=AF.Copy),
                            deps=[last, t_pd_prev], inc="ss")
                prev_ev[(t, seg)] = ev
                seg_toks.append(ev)
            t_s1 = p.emit("vector", lambda e: e.reduce_sum(
                STT[:, S_M1:S_M1 + 1], BR[:, 0:D], axis=AX),
                deps=seg_toks, inc="sv")
            t_s2a = p.emit("vector", lambda e: e.tensor_mul(
                BR[:, 8192:8192 + D], BR[:, 0:D], BR[:, 0:D]),
                deps=seg_toks, inc="sv")
            t_s2 = p.emit("vector", lambda e: e.reduce_sum(
                STT[:, S_M1 + 1:S_M1 + 2], BR[:, 8192:8192 + D], axis=AX),
                deps=[t_s2a], inc="sv")
            t_mu = p.emit("vector", lambda e: e.tensor_scalar_mul(
                STT[:, S_M1 + 2:S_M1 + 3], STT[:, S_M1:S_M1 + 1], 1.0 / D),
                deps=[t_s1], inc="sv")
            t_e2 = p.emit("vector", lambda e: e.tensor_scalar_mul(
                STT[:, S_M1 + 3:S_M1 + 4], STT[:, S_M1 + 1:S_M1 + 2], 1.0 / D),
                deps=[t_s2], inc="sv")
            t_m2 = p.emit("vector", lambda e: e.tensor_tensor(
                STT[:, S_M1 + 4:S_M1 + 5], STT[:, S_M1 + 2:S_M1 + 3],
                STT[:, S_M1 + 2:S_M1 + 3], ALU.mult), deps=[t_mu], inc="sv")
            t_va = p.emit("vector", lambda e: e.tensor_sub(
                STT[:, S_M1 + 3:S_M1 + 4], STT[:, S_M1 + 3:S_M1 + 4],
                STT[:, S_M1 + 4:S_M1 + 5]), deps=[t_e2, t_m2], inc="sv")
            t_lv = p.emit("scalar", lambda e: e.activation(
                out=STT[:, S_M1 + 4:S_M1 + 5], in_=STT[:, S_M1 + 3:S_M1 + 4],
                func=AF.Ln, bias=LN_EPS), deps=[t_va, t_c0], inc="ss")
            t_rr = p.emit("scalar", lambda e: e.activation(
                out=STT[:, S_M1 + 5:S_M1 + 6], in_=STT[:, S_M1 + 4:S_M1 + 5],
                func=AF.Exp, scale=-0.5), deps=[t_lv], inc="ss")
            t_nm = p.emit("vector", lambda e: e.tensor_scalar_mul(
                STT[:, S_M1 + 6:S_M1 + 7], STT[:, S_M1 + 2:S_M1 + 3], -1.0),
                deps=[t_mu], inc="sv")
            t_nb = p.emit("vector", lambda e: e.tensor_tensor(
                STT[:, S_M1 + 6:S_M1 + 7], STT[:, S_M1 + 6:S_M1 + 7],
                STT[:, S_M1 + 5:S_M1 + 6], ALU.mult),
                deps=[t_nm, t_rr], inc="sv")
            t_af = p.emit("scalar", lambda e: e.activation(
                out=BR[:, 0:D], in_=BR[:, 0:D], func=AF.Identity,
                bias=STT[:, S_M1 + 6:S_M1 + 7],
                scale=STT[:, S_M1 + 5:S_M1 + 6]), deps=[t_nb, t_s2], inc="ss")
            t_g1 = p.emit("vector", lambda e: e.tensor_mul(
                BR[:, 0:D], BR[:, 0:D], BR[:, 4096:4096 + D]),
                deps=[t_af, f_gbb], inc="sv")
            t_g2 = p.emit("vector", lambda e: e.tensor_add(
                BR[:, 0:D], BR[:, 0:D], BR[:, 6144:6144 + D]),
                deps=[t_g1], inc="sv")
            t_ul = p.dma("sync", AR[:, 2048:2048 + D],
                         lnscr[t * 128:(t + 1) * 128, :],
                         deps=[f_u, kv_last, t_uf_prev])
            f_ul = p.fence("sync", "sd")
            t_uf = p.emit("vector", lambda e: e.tensor_copy(
                BR[:, 2048:2048 + D], AR[:, 2048:2048 + D]),
                deps=[f_ul], inc="sv")
            t_uf_prev = t_uf
            t_pd = p.emit("vector", lambda e: e.tensor_mul(
                AR[:, 4096:4096 + D], BR[:, 0:D], BR[:, 2048:2048 + D]),
                deps=[t_uf, t_g2, kv_last], inc="sv")
            t_pd_prev = t_pd
            pst = p.dma("scalar", scr2[t * 128:(t + 1) * 128, :],
                        AR[:, 4096:4096 + D],
                        deps=[t_pd] + ([f_qt] if t == 0 else []))
            prod_store[t] = pst
        t_bo = p.dma("gpsimd", AR[0:1, BIA:BIA + D], I["L1_attn_bo"][:, :],
                     deps=[ubt[(NT - 1, 3)]])
        f_bo = p.fence("gpsimd", "sg")
        f_ps = p.fence("scalar", "so")
        for dc in range(DCH):
            p.dmat("sync", LT[:, dc * TPC:(dc + 1) * TPC],
                   scr2[:, dc * 128:(dc + 1) * 128],
                   deps=[f_ps, ubt[(NT - 1, 3)]] + state["lt_free"])
        f_ltp = p.fence("sync", "sd")
        res_toks = {t: [] for t in range(NT)}
        for seg in range(4):
            for dc in range(DCH):
                bslot = dc % 4
                buf = bslot * 512
                wtk = p.dmat("sync", AR[:, buf:buf + 512],
                             wo1_bf[seg * 512:(seg + 1) * 512,
                                    dc * 128:(dc + 1) * 128],
                             deps=[wimg_tok, wp_prev.get(bslot)],
                             sem="wp%d" % bslot)
                tk = None
                for t in range(NT):
                    tk = p.emit(
                        "tensor",
                        lambda e, dc=dc, t=t, buf=buf:
                        e.matmul(PB[t][:, :],
                                 LT[:, dc * TPC + t * 128:
                                    dc * TPC + t * 128 + 128],
                                 AR[:, buf:buf + 512],
                                 start=(dc == 0), stop=False),
                        deps=[wtk, f_ltp,
                              res_toks[t][seg - 1] if seg > 0 else None],
                        inc="st")
                wp_prev[bslot] = tk
            for t in range(NT):
                bt = p.emit(
                    "tensor", lambda e, t=t, seg=seg: e.matmul(
                        PB[t][:, :], ONEB[:, :],
                        AR[0:1, BIA + seg * 512:BIA + (seg + 1) * 512],
                        start=False, stop=True),
                    deps=[f_bo, t_onesb], inc="st")
                xs = X[:, t * D + seg * 512: t * D + (seg + 1) * 512]
                rv = p.emit(
                    "vector", lambda e, xs=xs, t=t, seg=seg:
                    e.scalar_tensor_tensor(
                        xs, PB[t][:, :], 1.0, xs, ALU.mult, ALU.add,
                        accum_out=STT[:, S_ACC + t * 4 + seg:
                                      S_ACC + t * 4 + seg + 1]),
                    deps=[bt], inc="sv")
                res_toks[t].append(rv)
        for t in range(NT):
            finalize_sums(t, res_toks[t], 4)
        barrier = ("st", p.cnt["st"])
        state["lt_free"] = [barrier]
        state["qt_free"] = [barrier]
        state["scr2_free"] = [f_ltp]
        state["stream_guard"] = barrier

    def emit_l1ffn(call):
        t_wc = t_wd = t_wu = imgs_tok["wcdu"]
        if call == 0:
            p.dma("gpsimd", wqk1_bf[:, :], I["L1_attn_WqkT"][:, :], sem="sgw3")
            p.dma("gpsimd", wv1_bf[:, :], I["L1_attn_WvT"][:, :], sem="sgw3")
            p.dma("gpsimd", wu1_bf[:, :], I["L1_attn_WuT"][:, :], sem="sgw3")
            p.dma("gpsimd", wo1_bf[:, :], I["L1_attn_WoT"][:, :], sem="sgw3")
            imgs_tok["l1a"] = ("sgw3", p.cnt["sgw3"])
        t_nw = p.dma("gpsimd", BR[:, 8192:8192 + D],
                     I["L1_ffn_norm_w"].to_broadcast([128, D]),
                     allow_slow_non_contiguous=True,
                     deps=state["qt_free"])
        t_nw = p.fence("gpsimd", "sg")
        lt_toks = ln_prologue(rms=True, post_mul=BR[:, 8192:8192 + D],
                              post_mul_tok=t_nw)
        t_gen = p.dma("gpsimd",
                      QT[:, 0:8192].rearrange("p (gc c) -> p gc c", c=GD),
                      I["L1_ffn_genes"].rearrange("(gc p) c -> p gc c", p=128),
                      deps=state["qt_free"])
        t_gen = p.fence("gpsimd", "sg")
        for cc in range(2):
            p.dmat("sync", QT[:, 8192 + cc * 2048:8192 + (cc + 1) * 2048],
                   wd_bf[:, cc * 128:(cc + 1) * 128],
                   deps=[t_wd] + state["qt_free"])
        for cc in range(2):
            p.dmat("sync", QT[:, 12288 + cc * 2048:12288 + (cc + 1) * 2048],
                   wu_bf[:, cc * 128:(cc + 1) * 128],
                   deps=[t_wu] + state["qt_free"])
        f_wdu = p.fence("sync", "sd")
        wdt_toks = [f_wdu, f_wdu]
        wut_toks = [f_wdu, f_wdu]
        lg_stores = {}
        ev_prev = {}
        sbufg = {0: None, 1: None, 2: None}
        sq_state = {"j": 0}
        loaded = {}

        def load_hs(j, extra=()):
            buf = sq_state["j"] % 3
            sq_state["j"] += 1
            base = buf * 4096
            gs, h = divmod(j, 2)
            for k in range(8):
                dc = h * 8 + k
                p.dmat("sync", AR[:, base + k * 512: base + (k + 1) * 512],
                       wc_bf[gs * 512:(gs + 1) * 512,
                             dc * 128:(dc + 1) * 128],
                       deps=[t_wc, sbufg[buf]] + list(extra))
            loaded[j] = (buf, base, p.fence("sync", "sd"))
        for j in range(3):
            load_hs(j, extra=state["qt_free"])
        nxt = 3
        for gs in range(8):
            b0, base0, f0 = loaded[2 * gs]
            b1, base1, f1 = loaded[2 * gs + 1]
            last = None
            for t in range(NT):
                pb = PB[t % 2]
                for dc in range(DCH):
                    col = (base0 + dc * 512 if dc < 8
                           else base1 + (dc - 8) * 512)
                    ftok = f0 if dc < 8 else f1
                    last = p.emit(
                        "tensor", lambda e, col=col, gs=gs, t=t, dc=dc, pb=pb:
                        e.matmul(pb[:, :],
                                 LT[:, dc * TPC + t * 128:
                                    dc * TPC + t * 128 + 128],
                                 AR[:, col:col + 512],
                                 start=(dc == 0), stop=(dc == DCH - 1)),
                        deps=[ftok, lt_toks[dc],
                              ev_prev.get(t - 2) if t >= 2
                              else ev_prev.get(t + 6)], inc="st")
                sl = t % 2
                slot_guard = (lg_stores.get((gs, t - 2)) if t >= 2
                              else lg_stores.get((gs - 1, t + 6)))
                ev = p.emit("scalar", lambda e, sl=sl, pb=pb: e.activation(
                    out=BR[:, sl * 512:(sl + 1) * 512],
                    in_=pb[:, :], func=AF.Identity, scale=CB[:, 43:44]),
                    deps=[last, t_it, slot_guard], inc="ss")
                ev_prev[t] = ev
                st = p.dma("gpsimd", lgscr[t * 128:(t + 1) * 128,
                                           gs * 512:(gs + 1) * 512],
                           BR[:, sl * 512:(sl + 1) * 512],
                           deps=[ev], sem="wp%d" % (2 + t % 2))
                lg_stores[(gs, t)] = st
            sbufg[b0] = last
            sbufg[b1] = last
            while nxt < 16 and nxt <= 2 * gs + 4:
                load_hs(nxt)
                nxt += 1
        lg_ceil = [("wp2", p.cnt["wp2"]), ("wp3", p.cnt["wp3"])]
        prev_tile = {}
        for t in range(NT):
            t_ld = p.dma("sync", BR[:, 0:G],
                         lgscr[t * 128:(t + 1) * 128, :],
                         deps=lg_ceil + [prev_tile.get(t - 1)])
            t_lc = p.emit("vector", lambda e: e.tensor_copy(
                BR[:, 4096:4096 + G], BR[:, 0:G]), deps=[t_ld], inc="sv")
            rt = t_lc
            for r in range(4):
                mx = p.emit("vector", lambda e, r=r: e.max(
                    out=T8[:, r * 8:(r + 1) * 8], in_=BR[:, 4096:4096 + G]),
                    deps=[rt], inc="sv")
                dr = p.emit("vector", lambda e: e.drain(), deps=[], inc="sv")
                if r < 3:
                    cm = p.emit("vector", lambda e, r=r: e.tensor_single_scalar(
                        MSK[:, :], BR[:, 4096:4096 + G],
                        T8[:, r * 8 + 7:r * 8 + 8], ALU.is_ge),
                        deps=[dr], inc="sv")
                    rt = p.emit("vector", lambda e: e.copy_predicated(
                        BR[:, 4096:4096 + G], MSK[:, :],
                        NEGB[:, :].to_broadcast([128, G])),
                        deps=[cm, t_negb], inc="sv")
                else:
                    rt = dr
            t_nx = p.emit("vector", lambda e: e.tensor_scalar_mul(
                STT[:, S_DEN:S_DEN + 1], T8[:, 0:1], -1.0),
                deps=[rt], inc="sv")
            t_ex = p.emit("scalar", lambda e: e.activation(
                out=BR[:, 4096:4096 + G], in_=BR[:, 0:G], func=AF.Exp,
                bias=STT[:, S_DEN:S_DEN + 1]), deps=[t_nx], inc="ss")
            t_wk = p.emit("vector", lambda e: e.scalar_tensor_tensor(
                BR[:, 4096:4096 + G], BR[:, 0:G], T8[:, 31:32],
                BR[:, 4096:4096 + G], ALU.is_ge, ALU.mult,
                accum_out=STT[:, S_DEN + 1:S_DEN + 2]),
            deps=[t_ex], inc="sv")
            wt_toks = []
            for grp in range(8):
                pb = PB[6 + grp % 2]
                trs = []
                for b in range(4):
                    gc = grp * 4 + b
                    trs.append(p.emit(
                        "tensor", lambda e, gc=gc, b=b, pb=pb: e.transpose(
                            pb[:, b * 128:(b + 1) * 128],
                            BR[:, 4096 + gc * 128:4096 + (gc + 1) * 128],
                            IDF[:, :]),
                        deps=[t_wk, t_idf,
                              wt_toks[grp - 2] if grp >= 2 else None],
                        inc="st"))
                wt = p.emit("vector", lambda e, grp=grp, pb=pb: e.tensor_copy(
                    AR[:, grp * 512:(grp + 1) * 512], pb[:, :]),
                    deps=[trs[3]], inc="sv")
                wt_toks.append(wt)
            t_rc = p.emit("vector", lambda e: e.reciprocal(
                STT[:, S_DEN + 2:S_DEN + 3], STT[:, S_DEN + 1:S_DEN + 2]),
                deps=[t_wk], inc="sv")
            last = None
            for gc in range(GCH):
                last = p.emit(
                    "tensor", lambda e, gc=gc: e.matmul(
                        PB[5][:, 0:256],
                        AR[:, gc * 128:gc * 128 + 128],
                        QT[:, gc * 256:(gc + 1) * 256],
                        start=(gc == 0), stop=(gc == GCH - 1)),
                    deps=[wt_toks[gc // 4], t_gen], inc="st")
            t_eb = p.emit("scalar", lambda e: e.activation(
                out=BR[:, 2048:2048 + 256], in_=PB[5][:, 0:256],
                func=AF.Identity, scale=STT[:, S_DEN + 2:S_DEN + 3]),
                deps=[last, t_rc], inc="ss")
            tr1 = p.emit("tensor", lambda e: e.transpose(
                PB[6][:, 0:128], BR[:, 2048:2048 + 128], IDF[:, :]),
                deps=[t_eb, t_idf], inc="st")
            tr2 = p.emit("tensor", lambda e: e.transpose(
                PB[6][:, 128:256], BR[:, 2048 + 128:2048 + 256], IDF[:, :]),
                deps=[t_eb], inc="st")
            t_et = p.emit("vector", lambda e: e.tensor_copy(
                AR[:, 4352:4352 + 256], PB[6][:, 0:256]),
                deps=[tr1, tr2], inc="sv")
            dn_toks = []
            up_toks = []
            for ds in range(4):
                lastd = None
                for cc in range(2):
                    lastd = p.emit(
                        "tensor", lambda e, ds=ds, cc=cc: e.matmul(
                            PB[0][:, :],
                            AR[:, 4352 + cc * 128:4352 + (cc + 1) * 128],
                            QT[:, 8192 + cc * 2048 + ds * 512:
                               8192 + cc * 2048 + (ds + 1) * 512],
                            start=(cc == 0), stop=(cc == 1)),
                        deps=[t_et, wdt_toks[cc],
                              dn_toks[ds - 1] if ds > 0 else None], inc="st")
                dn = p.emit("scalar", lambda e, ds=ds: e.activation(
                    out=BR[:, 0 + ds * 512:(ds + 1) * 512], in_=PB[0][:, :],
                    func=AF.Tanh), deps=[lastd], inc="ss")
                dn_toks.append(dn)
                lastu = None
                for cc in range(2):
                    lastu = p.emit(
                        "tensor", lambda e, ds=ds, cc=cc: e.matmul(
                            PB[1][:, :],
                            AR[:, 4352 + cc * 128:4352 + (cc + 1) * 128],
                            QT[:, 12288 + cc * 2048 + ds * 512:
                               12288 + cc * 2048 + (ds + 1) * 512],
                            start=(cc == 0), stop=(cc == 1)),
                        deps=[t_et, wut_toks[cc],
                              up_toks[ds - 1] if ds > 0 else None], inc="st")
                up = p.emit("scalar", lambda e, ds=ds: e.activation(
                    out=BR[:, 4096 + ds * 512:4096 + (ds + 1) * 512],
                    in_=PB[1][:, :], func=AF.Copy), deps=[lastu], inc="ss")
                up_toks.append(up)
            xs = X[:, t * D:(t + 1) * D]
            t_xa = p.emit("scalar", lambda e, xs=xs, t=t: e.activation(
                out=BR[:, 2048:2048 + D], in_=xs, func=AF.Identity,
                bias=STT[:, S_NMR + t:S_NMR + t + 1],
                scale=STT[:, S_R + t:S_R + t + 1]),
                deps=dn_toks + up_toks, inc="ss")
            t_xn = p.emit("vector", lambda e: e.tensor_mul(
                BR[:, 2048:2048 + D], BR[:, 2048:2048 + D],
                BR[:, 8192:8192 + D]), deps=[t_xa, t_nw], inc="sv")
            t_p1 = p.emit("vector", lambda e: e.tensor_mul(
                BR[:, 0:D], BR[:, 0:D], BR[:, 2048:2048 + D]),
                deps=[t_xn] + dn_toks, inc="sv")
            t_p2 = p.emit("vector", lambda e: e.tensor_mul(
                BR[:, 0:D], BR[:, 0:D], BR[:, 4096:4096 + D]),
                deps=[t_p1] + up_toks, inc="sv")
            rv = p.emit("vector", lambda e, xs=xs, t=t:
                        e.scalar_tensor_tensor(
                            xs, BR[:, 0:D], CB[:, 42:43], xs,
                            ALU.mult, ALU.add,
                            accum_out=STT[:, S_ACC + t * 4:S_ACC + t * 4 + 1]),
                        deps=[t_p2, t_hs], inc="sv")
            prev_tile[t] = rv
            finalize_sums(t, [rv], 1)
        bar = ("st", p.cnt["st"])
        state["lt_free"] = [bar]
        state["qt_free"] = [bar]
        state["stream_guard"] = bar

    stages = [lambda: emit_l0ffn(0), emit_l0attn, lambda: emit_l0ffn(1),
              lambda: emit_l1ffn(0), emit_l1attn, lambda: emit_l1ffn(1)]
    global _STAGE_MARKS
    _STAGE_MARKS = [dict(p.cnt)]
    for i in range(min(upto, 6)):
        stages[i]()
        _STAGE_MARKS.append(dict(p.cnt))

    out_toks = []
    for t in range(NT):
        out_toks.append(p.dma("sync", y[t * 128:(t + 1) * 128, :],
                              X[:, t * D:(t + 1) * D],
                              deps=[state["x_tok"][t]] +
                              ([state["sum_tok"][t]] if state["sum_tok"][t]
                               else [])))
    for i, (tn, tsh, tf) in enumerate(taps):
        src = tf(ns)
        from concourse.bass_types import DRamTensorHandle
        eng = "gpsimd" if (src.dtype != F32 or
                           isinstance(src.tensor, DRamTensorHandle)) else "sync"
        out_toks.append(p.dma(eng, tap_dram[i], src, deps=p.all_toks()))
    final_deps = out_toks + [("sg", p.cnt["sg"]), ("so", p.cnt["so"])]
    p.emit("gpsimd", lambda e: e.engine_nop(), deps=final_deps, inc="sq")

    with nc.Block() as block:
        @block.sync
        def _(eng):
            for fn in p.q["sync"]:
                fn(eng)

        @block.scalar
        def _(eng):
            for fn in p.q["scalar"]:
                fn(eng)

        @block.vector
        def _(eng):
            for fn in p.q["vector"]:
                fn(eng)

        @block.tensor
        def _(eng):
            for fn in p.q["tensor"]:
                fn(eng)

        @block.gpsimd
        def _(eng):
            for fn in p.q["gpsimd"]:
                fn(eng)

    ctx.close()
    return nc


_NC_CACHE = {}


def kernel(**inputs):
    x = np.ascontiguousarray(np.asarray(inputs["L0_attn_x"],
                                        dtype=np.float32).reshape(8192, D))
    key = 6
    if key not in _NC_CACHE:
        _NC_CACHE[key] = build_program(upto=6)
    nc = _NC_CACHE[key]

    def arr(name, shape):
        return np.ascontiguousarray(
            np.asarray(inputs[name], dtype=np.float32).reshape(shape))

    common = {
        "L0_attn_Wq": arr("L0_attn_Wq", (D, D)),
        "L0_attn_Wv": arr("L0_attn_Wv", (D, D)),
        "L0_attn_Wo": arr("L0_attn_Wo", (D, D)),
        "L0_attn_Wg": arr("L0_attn_Wg", (D, D)),
        "L0_attn_ln_gamma": arr("L0_attn_ln_gamma", (1, D)),
        "L0_attn_ln_beta": arr("L0_attn_ln_beta", (1, D)),
        "L0_attn_beta": arr("L0_attn_beta", (1, NHEAD)),
        "L0_ffn_gate_W": arr("L0_ffn_gate_W", (NEXP, D)),
        "L0_ffn_W1T": np.ascontiguousarray(
            arr("L0_ffn_W1", (NEXP, D, DFF)).transpose(0, 2, 1)),
        "L0_ffn_b1": arr("L0_ffn_b1", (NEXP, DFF)),
        "L0_ffn_W2T": np.ascontiguousarray(
            arr("L0_ffn_W2", (NEXP, DFF, D)).transpose(0, 2, 1)),
        "L0_ffn_b2": arr("L0_ffn_b2", (NEXP, D)),
        "L1_attn_WqkT": np.ascontiguousarray(arr("L1_attn_Wqk", (D, D)).T),
        "L1_attn_bqk": arr("L1_attn_bqk", (1, D)),
        "L1_attn_WvT": np.ascontiguousarray(arr("L1_attn_Wv", (D, D)).T),
        "L1_attn_bv": arr("L1_attn_bv", (1, D)),
        "L1_attn_WuT": np.ascontiguousarray(arr("L1_attn_Wu", (D, D)).T),
        "L1_attn_bu": arr("L1_attn_bu", (1, D)),
        "L1_attn_WoT": np.ascontiguousarray(arr("L1_attn_Wo", (D, D)).T),
        "L1_attn_bo": arr("L1_attn_bo", (1, D)),
        "L1_attn_ln_g": arr("L1_attn_ln_g", (1, D)),
        "L1_attn_ln_b": arr("L1_attn_ln_b", (1, D)),
        "L1_ffn_Wc": arr("L1_ffn_Wc", (G, D)),
        "L1_ffn_temperature": arr("L1_ffn_temperature", (1, 1)),
        "L1_ffn_genes": arr("L1_ffn_genes", (G, GD)),
        "L1_ffn_Wd": arr("L1_ffn_Wd", (D, GD)),
        "L1_ffn_Wu": arr("L1_ffn_Wu", (D, GD)),
        "L1_ffn_norm_w": arr("L1_ffn_norm_w", (1, D)),
        "L1_ffn_scale": arr("L1_ffn_scale", (1, 1)),
    }
    in_maps = []
    for i in range(NCORES):
        m = dict(common)
        m["x"] = np.ascontiguousarray(x[i * TPC:(i + 1) * TPC])
        coef = np.array([[1024.0 * (i - 1 - j)] if j < i else [1e30]
                         for j in range(8)], dtype=np.float32)
        m["coef"] = coef
        in_maps.append(m)
    res = run_bass_kernel_spmd(nc, in_maps, core_ids=list(range(NCORES)))
    out = np.concatenate([res.results[i]["y"] for i in range(NCORES)], axis=0)
    return out.reshape(1, 8192, D)
